# version 29
# baseline (speedup 1.0000x reference)
import numpy as np
import concourse.bass as bass
import concourse.mybir as mybir
from concourse.bass_utils import run_bass_kernel_spmd
from contextlib import ExitStack

F32 = mybir.dt.float32
BF16 = mybir.dt.bfloat16
I32 = mybir.dt.int32
AF = mybir.ActivationFunctionType
ALU = mybir.AluOpType
AX = mybir.AxisListType

T = 2048
TC = 512
TC4 = 256
NCH = T // TC
NCH4 = T // TC4
NEG_BIG = -1e30

PHASE_LIMIT = 99
DEBUG = False
SCAN_N = 16
P4STAGE = 99
P5STAGE = 99
TTADD = True
NHCORR = 16
CONSTLHS = False
CONSTRHS = False
TTMODE = 0
FWD_MM = True
BACK_MM = True
PAIRS = [[0, 1], [2, 3], [4, 5], [6, 7]]


def _pack(w, cs=128):
    Din, Dout = w.shape
    return np.ascontiguousarray(
        w.reshape(Din // 128, 128, Dout // cs, cs).transpose(1, 2, 0, 3).reshape(128, -1))


def _packv(v, P=128):
    n = v.shape[0]
    return np.ascontiguousarray(v.reshape(n // P, P).T)


def prep_inputs(I):
    f32 = np.float32
    x = np.asarray(I["L0_attn_x"], f32)
    S = {}
    S["W_in_p"] = _pack(np.ascontiguousarray(I["L0_attn_W_in"].T))
    wxt = np.ascontiguousarray(I["L0_attn_W_x"].T)
    S["W_x_p"] = _pack(wxt[:, 0:2048])
    S["W_xbc_p"] = _pack(wxt[:, 2048:2080], cs=32)
    S["W_dt_p"] = _pack(np.ascontiguousarray(I["L0_attn_W_dt"].T))
    S["W_out_p"] = _pack(np.ascontiguousarray(I["L0_attn_W_out"].T))
    cw = I["L0_attn_conv_w"].reshape(2048, 4)
    S["convw_sb"] = np.ascontiguousarray(cw.reshape(16, 128, 4).transpose(1, 0, 2).reshape(128, 64))
    S["convb_sb"] = _packv(I["L0_attn_conv_b"])
    S["bdt_sb"] = _packv(I["L0_attn_b_dt"])
    S["D_sb"] = _packv(I["L0_attn_D"])
    S["l0a_g"] = _packv(I["L0_attn_ln_g"]); S["l0a_b"] = _packv(I["L0_attn_ln_b"])
    S["enc_p"] = _pack(np.asarray(I["L0_ffn_enc_w"], f32))
    S["enc_b_sb"] = _packv(I["L0_ffn_enc_b"])
    S["s2c_p"] = _pack(np.asarray(I["L0_ffn_s2c_w"], f32), cs=64)
    S["s2c_b_sb"] = I["L0_ffn_s2c_b"].reshape(64, 1).astype(f32)
    S["r_w1"] = np.asarray(I["L0_ffn_r_w1"], f32)
    S["r_b1_sb"] = I["L0_ffn_r_b1"].reshape(64, 1).astype(f32)
    S["r_w2"] = np.asarray(I["L0_ffn_r_w2"], f32)
    S["r_b2_sb"] = I["L0_ffn_r_b2"].reshape(8, 1).astype(f32)
    S["e_w1_p"] = np.ascontiguousarray(
        I["L0_ffn_e_w1"].reshape(8, 64, 8, 128).transpose(1, 0, 2, 3).reshape(64, 8192))
    S["e_b1p"] = np.ascontiguousarray(
        I["L0_ffn_e_b1"].reshape(8, 8, 128).transpose(2, 0, 1).reshape(128, 64))
    S["e_w2_p"] = np.ascontiguousarray(
        I["L0_ffn_e_w2"].reshape(8, 8, 128, 64).transpose(2, 0, 1, 3).reshape(128, 4096))
    S["e_b2p"] = np.ascontiguousarray(I["L0_ffn_e_b2"].T)
    S["c2s_p"] = np.ascontiguousarray(
        I["L0_ffn_c2s_w"].reshape(64, 16, 128).transpose(0, 1, 2).reshape(64, 2048))
    S["c2s_b_sb"] = _packv(I["L0_ffn_c2s_b"])
    S["dec_p"] = _pack(np.asarray(I["L0_ffn_dec_w"], f32))
    S["dec_b_sb"] = _packv(I["L0_ffn_dec_b"])
    S["l0f_g"] = _packv(I["L0_ffn_ln_g"]); S["l0f_b"] = _packv(I["L0_ffn_ln_b"])
    S["Wq_p"] = _pack(np.asarray(I["L1_attn_Wq"], f32))
    S["Wk_p"] = _pack(np.asarray(I["L1_attn_Wk"], f32))
    S["Wv_p"] = np.ascontiguousarray(
        I["L1_attn_Wv"].reshape(8, 128, 1024).transpose(1, 0, 2).reshape(128, 8192))
    S["Wo_p"] = _pack(np.asarray(I["L1_attn_Wo"], f32))
    S["bo_sb"] = _packv(I["L1_attn_bo"])
    mk = I["L1_ffn_mix_kernel"]
    S["mixk_sb"] = np.ascontiguousarray(mk.T.reshape(8, 128, 3).transpose(1, 0, 2).reshape(128, 24))
    S["W1_p"] = np.ascontiguousarray(
        I["L1_ffn_W1"].reshape(9, 8, 128, 2, 128).transpose(2, 0, 3, 1, 4).reshape(128, 18432))
    S["b1p"] = np.ascontiguousarray(
        I["L1_ffn_b1"].reshape(9, 2, 128).transpose(2, 0, 1).reshape(128, 18))
    S["W2_p"] = np.ascontiguousarray(
        I["L1_ffn_W2"].reshape(9, 2, 128, 8, 128).transpose(2, 0, 1, 3, 4).reshape(128, 18432))
    S["b2f"] = np.asarray(I["L1_ffn_b2"], f32)
    S["Wo1_p"] = _pack(np.asarray(I["L1_ffn_Wo1"], f32))
    S["bo1_sb"] = _packv(I["L1_ffn_bo1"])
    S["Wo2_p"] = np.ascontiguousarray(
        I["L1_ffn_Wo2"].reshape(2, 128, 9).transpose(1, 0, 2).reshape(128, 18))
    S["bo2_sb"] = I["L1_ffn_bo2"].reshape(9, 1).astype(f32)
    S["l1f_g"] = _packv(I["L1_ffn_gamma"]); S["l1f_b"] = _packv(I["L1_ffn_beta"])
    S["ident"] = np.eye(128, dtype=f32)
    tj = np.arange(128)[:, None]; ti = np.arange(128)[None, :]
    S["cmask"] = (tj <= ti).astype(f32)
    S["ones_row"] = np.ones((1, 128), f32)
    S["inv1024"] = np.full((128, 1), 1.0 / 1024.0, f32)
    S["iota8"] = np.broadcast_to(np.arange(8, dtype=f32), (128, 8)).copy()
    S["iota8m"] = S["iota8"] - 8.0
    S["iota9"] = np.broadcast_to(np.arange(9, dtype=f32), (128, 9)).copy()
    S["rb2_rep"] = np.broadcast_to(I["L0_ffn_r_b2"].astype(f32), (128, 8)).copy()
    S["bo2_rep"] = np.broadcast_to(I["L1_ffn_bo2"].astype(f32), (128, 9)).copy()
    S["iota9m"] = S["iota9"] - 9.0

    maps = []
    p_ = np.arange(128)
    for c in range(8):
        b, h = c // 2, c % 2
        m = dict(S)
        xT = np.ascontiguousarray(x[b].T)
        sl = xT[:, h * T:(h + 1) * T]
        halo = xT[:, h * T - 3:h * T] if h == 1 else np.zeros((1024, 3), f32)
        m["x_ext"] = np.ascontiguousarray(np.concatenate([halo, sl], axis=1))
        m["is_odd"] = np.full((128, 1), float(h), f32)
        m["is_even"] = np.full((128, 1), float(1 - h), f32)
        m["odd64"] = np.full((64, 1), float(h), f32)
        al = I["L0_attn_A_log"]
        m["A_pack"] = np.ascontiguousarray(
            al[1024 * h:1024 * (h + 1)].reshape(8, 128, 16).transpose(1, 0, 2).reshape(128, 128))
        gdu = np.empty((128, 32), np.int32)
        for q in range(2):
            for k in range(8):
                gdu[:, q * 16 + k] = q * 2048 + h * 1024 + k * 128 + p_
                gdu[:, q * 16 + 8 + k] = q * 2080 + h * 1024 + k * 128 + p_
        m["gdu_idx"] = gdu
        gy = np.empty((128, 64), np.int32)
        for k in range(16):
            for cc in range(4):
                gy[:, k * 4 + cc] = (2048 * h + 128 * k + p_) * 4 + cc
        m["gy_idx"] = gy
        maps.append(m)
    return maps


class B:
    ENGINES = ("sync", "gpsimd", "tensor", "scalar", "vector")

    def __init__(self, nc, stack):
        self.nc = nc
        self.q = {e: [] for e in self.ENGINES}
        self.sem = {}
        for s in ("dma", "gdma", "mm", "sc", "vx", "cc", "st", "gp"):
            self.sem[s] = [stack.enter_context(nc.semaphore(s)), 0]

    def val(self, s):
        return self.sem[s][1]

    def emit(self, eng, fn, waits=(), inc=None):
        tgt = None
        if inc is not None:
            self.sem[inc[0]][1] += inc[1]
            tgt = self.sem[inc[0]][1]
        sems = self.sem

        def op(e, fn=fn, waits=tuple(waits), inc=inc):
            for s, v in waits:
                if v > 0:
                    e.wait_ge(sems[s][0], v)
            r = fn(e)
            if inc is not None:
                r.then_inc(sems[inc[0]][0], inc[1])
        self.q[eng].append(op)
        return tgt

    def barrier(self):
        snap = {s: c for s, (hh, c) in self.sem.items()}
        sems = self.sem
        for e in self.ENGINES:
            def bop(eng, snap=dict(snap)):
                for s, v in snap.items():
                    if v > 0:
                        eng.wait_ge(sems[s][0], v)
            self.q[e].append(bop)

    def dma(self, out, in_, waits=(), q="sync"):
        s = "dma" if q == "sync" else "gdma"
        return self.emit(q, lambda e, o=out, i=in_: e.dma_start(out=o, in_=i),
                         waits=waits, inc=(s, 16))

    def gather(self, out, in_, idx, waits=()):
        return self.emit(
            "gpsimd",
            lambda e, o=out, i=in_, x=idx: e.indirect_dma_start(
                out=o, out_offset=None, in_=i,
                in_offset=bass.IndirectOffsetOnAxis(ap=x, axis=0)),
            waits=waits, inc=("gdma", 16))

    def mm(self, out, lhsT, rhs, start, stop, waits=(), inc=True, skip_gc=False):
        return self.emit(
            "tensor",
            lambda e, o=out, l=lhsT, r=rhs, s1=start, s2=stop, sg=skip_gc:
                e.matmul(o, l, r, start=s1, stop=s2, skip_group_check=sg),
            waits=waits, inc=("mm", 1) if inc else None)

    def tr(self, out, in_, ident, waits=()):
        r = self.emit("tensor",
                      lambda e, o=out, i=in_, d=ident: e.transpose(o, i, d),
                      waits=waits, inc=("mm", 1))
        self.emit("tensor",
                  lambda e: e.matmul(self.drain_ps, self.drain_lhs, self.drain_rhs,
                                     start=True, stop=True, skip_group_check=True),
                  inc=("mm", 1))
        return r

    def act(self, out, in_, func, bias=0.0, scale=1.0, waits=()):
        return self.emit(
            "scalar",
            lambda e, o=out, i=in_, f=func, bb=bias, s=scale:
                e.activation(out=o, in_=i, func=f, bias=bb, scale=s),
            waits=waits, inc=("sc", 1))

    def fence_v(self, srcap):
        return self.vop("tensor_copy", self.ftile[0:srcap.shape[0], 0:1], srcap,
                        waits=[("vx", self.val("vx"))])

    def fence_s(self, srcap):
        return self.emit(
            "scalar",
            lambda e, o=self.ftile[0:srcap.shape[0], 1:2], i=srcap:
                e.activation(out=o, in_=i, func=AF.Copy),
            inc=("sc", 1))

    def vch(self, name, *args, waits=(), **kw):
        return self.vop(name, *args, waits=tuple(waits) + (("vx", self.val("vx")),), **kw)

    def vop(self, name, *args, waits=(), **kw):
        return self.emit(
            "vector",
            lambda e, n=name, a=args, k=kw: getattr(e, n)(*a, **k),
            waits=waits, inc=("vx", 1))


def build(nc):
    stack = ExitStack()
    b = B(nc, stack)
    dt = nc.dram_tensor
    EI = lambda n, s, d=F32: dt(n, s, d, kind="ExternalInput").ap()

    x_ext = EI("x_ext", [1024, 2051])
    W_in_p = EI("W_in_p", [128, 32768]); W_x_p = EI("W_x_p", [128, 32768])
    W_xbc_p = EI("W_xbc_p", [128, 512]); W_dt_p = EI("W_dt_p", [128, 32768])
    W_out_p = EI("W_out_p", [128, 16384])
    convw_i = EI("convw_sb", [128, 64]); convb_i = EI("convb_sb", [128, 16])
    bdt_i = EI("bdt_sb", [128, 16]); A_pack = EI("A_pack", [128, 128])
    D_i = EI("D_sb", [128, 16]); l0ag_i = EI("l0a_g", [128, 8]); l0ab_i = EI("l0a_b", [128, 8])
    enc_p = EI("enc_p", [128, 16384]); encb_i = EI("enc_b_sb", [128, 16])
    s2c_p = EI("s2c_p", [128, 1024]); s2cb_i = EI("s2c_b_sb", [64, 1])
    rw1_i = EI("r_w1", [64, 64]); rb1_i = EI("r_b1_sb", [64, 1])
    rw2_i = EI("r_w2", [64, 8]); rb2_i = EI("r_b2_sb", [8, 1])
    ew1_p = EI("e_w1_p", [64, 8192]); eb1_i = EI("e_b1p", [128, 64])
    ew2_p = EI("e_w2_p", [128, 4096]); eb2_i = EI("e_b2p", [64, 8])
    c2s_p = EI("c2s_p", [64, 2048]); c2sb_i = EI("c2s_b_sb", [128, 16])
    dec_p = EI("dec_p", [128, 16384]); decb_i = EI("dec_b_sb", [128, 8])
    l0fg_i = EI("l0f_g", [128, 8]); l0fb_i = EI("l0f_b", [128, 8])
    Wq_p = EI("Wq_p", [128, 8192]); Wk_p = EI("Wk_p", [128, 8192])
    Wv_p = EI("Wv_p", [128, 8192]); Wo_p = EI("Wo_p", [128, 8192])
    bo_i = EI("bo_sb", [128, 8])
    mixk_i = EI("mixk_sb", [128, 24])
    W1_p = EI("W1_p", [128, 18432]); b1p_i = EI("b1p", [128, 18])
    W2_p = EI("W2_p", [128, 18432]); b2f = EI("b2f", [9, 1024])
    Wo1_p = EI("Wo1_p", [128, 2048]); bo1_i = EI("bo1_sb", [128, 2])
    Wo2_p = EI("Wo2_p", [128, 18]); bo2_i = EI("bo2_sb", [9, 1])
    l1fg_i = EI("l1f_g", [128, 8]); l1fb_i = EI("l1f_b", [128, 8])
    ident_i = EI("ident", [128, 128]); cmask_i = EI("cmask", [128, 128])
    onesr_i = EI("ones_row", [1, 128]); inv1024_i = EI("inv1024", [128, 1])
    iota8_i = EI("iota8", [128, 8]); iota8m_i = EI("iota8m", [128, 8])
    iota9_i = EI("iota9", [128, 9]); iota9m_i = EI("iota9m", [128, 9])
    isodd_i = EI("is_odd", [128, 1]); iseven_i = EI("is_even", [128, 1])
    odd64_i = EI("odd64", [64, 1])
    rb2r_i = EI("rb2_rep", [128, 8]); bo2r_i = EI("bo2_rep", [128, 9])
    gdu_i = dt("gdu_idx", [128, 32], I32, kind="ExternalInput").ap()
    gy_i = dt("gy_idx", [128, 64], I32, kind="ExternalInput").ap()

    out_fm = dt("out_fm", [1024, T], F32, kind="ExternalOutput").ap()

    cc_pre_in = dt("cc_pre_in", [4128, T], BF16, kind="Internal")
    cc_pre_outD = dt("cc_pre_outD", [4096, T], BF16, kind="Internal")
    cc_pre_outU = dt("cc_pre_outU", [4160, T], BF16, kind="Internal")
    cc_y_in = dt("cc_y_in", [2048, 2048], BF16, kind="Internal")
    cc_y_out = dt("cc_y_out", [4096, 2048], BF16, kind="Internal")
    cc_att_in = dt("cc_att_in", [64, 1040], F32, kind="Internal")
    cc_att_out = dt("cc_att_out", [128, 1040], F32, kind="Internal")
    cc_mix_in = dt("cc_mix_in", [128, 16], BF16, kind="Internal")
    cc_mix_out = dt("cc_mix_out", [256, 16], BF16, kind="Internal")
    z_dram = dt("z_dram", [2048, T], BF16, kind="Internal")
    ln_dram = dt("ln_dram", [1024, T], BF16, kind="Internal")
    xres_A = dt("xres_A", [1024, T], F32, kind="Internal")
    xres_B = dt("xres_B", [1024, T], F32, kind="Internal")
    comb_dram = dt("comb_dram", [64, TC4], F32, kind="Internal")
    mask9_dram = dt("mask9_dram", [36, TC], F32, kind="Internal")
    q_dram = dt("q_dram", [1024, T], BF16, kind="Internal")
    k_dram = dt("k_dram", [1024, T], BF16, kind="Internal")
    v_dram = dt("v_dram", [T, 1024], BF16, kind="Internal")
    num_dram = dt("num_dram", [1024, T], F32, kind="Internal")
    den_dram = dt("den_dram", [16, T], F32, kind="Internal")
    attn_dram = dt("attn_dram", [1024, T], BF16, kind="Internal")

    dbg = {}
    if DEBUG:
        for nm, sh, ddt in [("dbg_pre", [8256, T], BF16), ("dbg_y", [2048, 4096], BF16),
                            ("dbg_x1", [1024, T], F32), ("dbg_x2", [1024, T], F32),
                            ("dbg_x3", [1024, T], F32), ("dbg_z", [2048, T], BF16),
                            ("dbg_att", [1024, T], BF16),
                            ("dbg_xsh", [2048, 3], BF16), ("dbg_lnh", [1024, 3], BF16),
                            ("dbg_yin", [1024, 4096], BF16), ("dbg_comb", [64, TC4], F32), ("dbg_m9", [36, TC], F32), ("dbg_xm", [1024, TC], BF16),
                            ("dbg_lgt", [128, 8], F32), ("dbg_cmb", [128, 8], F32),
                            ("dbg_lgr", [8, TC4], F32), ("dbg_rh", [64, TC4], F32),
                            ("dbg_cont", [64, TC4], F32)]:
            dbg[nm] = dt(nm, sh, ddt, kind="ExternalOutput").ap()

    sb = lambda n, s, d: stack.enter_context(nc.sbuf_tensor(n, s, d))
    bigA_t = sb("s_bigA", [128, 32768], BF16)
    big1_t = sb("s_big1", [128, 32768], BF16)
    xs_t = sb("s_xs_c", [128, 8240], BF16)
    wp_t = sb("s_wping", [128, 4096], BF16)
    st_t = sb("s_stag", [128, 8192], BF16)
    A_sb = sb("s_A_sb", [128, 128], F32).ap()
    carry = sb("s_carry", [128, 128], F32).ap()
    convw = sb("s_convw", [128, 64], F32).ap(); convb = sb("s_convb", [128, 16], F32).ap()
    bdt = sb("s_bdt", [128, 16], F32).ap(); D_s = sb("s_D_s", [128, 16], F32).ap()
    l0ag = sb("s_l0ag", [128, 8], F32).ap(); l0ab = sb("s_l0ab", [128, 8], F32).ap()
    encb = sb("s_encb", [128, 16], F32).ap(); s2cb = sb("s_s2cb", [64, 1], F32).ap()
    rw1 = sb("s_rw1", [64, 64], BF16).ap(); rb1 = sb("s_rb1", [64, 1], F32).ap()
    rw2 = sb("s_rw2", [64, 8], BF16).ap(); rb2 = sb("s_rb2", [8, 1], F32).ap()
    eb1 = sb("s_eb1", [128, 64], F32).ap(); eb2 = sb("s_eb2", [64, 8], F32).ap()
    c2sb = sb("s_c2sb", [128, 16], F32).ap(); decb = sb("s_decb", [128, 8], F32).ap()
    l0fg = sb("s_l0fg", [128, 8], F32).ap(); l0fb = sb("s_l0fb", [128, 8], F32).ap()
    bo_s = sb("s_bo_s", [128, 8], F32).ap(); mixk = sb("s_mixk", [128, 24], F32).ap()
    b1p = sb("s_b1p_s", [128, 18], F32).ap(); bo1 = sb("s_bo1", [128, 2], F32).ap()
    bo2 = sb("s_bo2", [9, 1], F32).ap()
    l1fg = sb("s_l1fg", [128, 8], F32).ap(); l1fb = sb("s_l1fb", [128, 8], F32).ap()
    b2bf = sb("s_b2bf", [9, 1024], BF16).ap()
    b.ftile = sb("s_ftile", [128, 2], F32).ap()
    rb2rep = sb("s_rb2rep", [128, 8], F32).ap()
    bo2rep = sb("s_bo2rep", [128, 9], F32).ap()
    wo2b = sb("s_wo2b", [128, 18], BF16).ap()
    identf = sb("s_identf", [128, 128], F32).ap(); identb = sb("s_identb", [128, 128], BF16).ap()
    cmask = sb("s_cmask", [128, 128], F32).ap()
    onesr = sb("s_onesr", [1, 128], F32).ap(); inv1024 = sb("s_inv1024", [128, 1], F32).ap()
    iota8 = sb("s_iota8", [128, 8], F32).ap(); iota8m = sb("s_iota8m", [128, 8], F32).ap()
    iota9 = sb("s_iota9", [128, 9], F32).ap(); iota9m = sb("s_iota9m", [128, 9], F32).ap()
    isodd = sb("s_isodd", [128, 1], F32).ap(); iseven = sb("s_iseven", [128, 1], F32).ap()
    odd64 = sb("s_odd64", [64, 1], F32).ap()
    gdu_sb = sb("s_gdu_sb", [128, 32], I32).ap()
    gy_sb = sb("s_gy_sb", [128, 64], I32).ap()
    mean_r = sb("s_mean_r", [1, TC], F32).ap(); msq_r = sb("s_msq_r", [1, TC], F32).ap()
    tmp_r = sb("s_tmp_r", [1, TC], F32).ap(); rstd_r = sb("s_rstd_r", [1, TC], F32).ap()
    vaug = sb("s_vaug", [128, 130], BF16).ap()
    ktok = sb("s_ktok", [128, 64], BF16).ap()
    amb = sb("s_am_bf", [128, 128], BF16).ap()
    S_f = sb("s_S_sb", [64, 65], F32).ap()
    saug = sb("s_saug", [64, 65], BF16).ap()
    sprev_f = sb("s_sprev_f", [64, 65], F32).ap()
    sprev = sb("s_sprev", [64, 65], BF16).ap()
    sprev64 = sb("s_sprev64", [64, 64], BF16).ap()
    mix_sb = sb("s_mix_sb", [128, 16], BF16).ap()
    vaug2 = sb("s_vaug2", [128, 130], BF16).ap()
    ktok2 = sb("s_ktok2", [128, 64], BF16).ap()
    amb2 = sb("s_amb2", [128, 128], BF16).ap()
    S_f2 = sb("s_S_f2", [64, 65], F32).ap()
    saug2 = sb("s_saug2", [64, 65], BF16).ap()
    onesb = sb("s_onesb", [1, 128], BF16).ap()
    mixh = sb("s_mixh", [128, 32], BF16).ap()
    spk1 = sb("s_spk1", [64, 1], BF16).ap()
    den_s = sb("s_den_s", [1, T], F32).ap()
    lgt = sb("s_lgt_t", [128, 16], F32).ap()
    m1t = sb("s_m1_t", [128, 1], F32).ap(); m2t = sb("s_m2_t", [128, 1], F32).ap()
    i1t = sb("s_i1_t", [128, 1], F32).ap()
    candt = sb("s_cand_t", [128, 16], F32).ap()
    msk1 = sb("s_msk1", [128, 16], F32).ap(); msk2 = sb("s_msk2", [128, 16], F32).ap()
    w1t = sb("s_w1_t", [128, 1], F32).ap(); w2t = sb("s_w2_t", [128, 1], F32).ap()
    logit_sb = sb("s_logit_sb", [9, TC], F32).ap()
    comb_sb = sb("s_comb_sb", [8, TC4], F32).ap()
    mask9_sb = sb("s_mask9_sb", [9, TC], F32).ap()
    mask9b = sb("s_mask9_bf", [9, TC], BF16).ap()
    cont_bf = sb("s_cont_bf", [64, TC4], BF16).ap()
    rh_bf = sb("s_rh_bf", [64, TC4], BF16).ap()
    moe_f = sb("s_moe_f", [64, TC4], F32).ap()
    eo_f = sb("s_eo_f", [64, TC4], F32).ap()
    moe_bf = sb("s_moe_bf", [64, TC4], BF16).ap()
    comb_rep = sb("s_comb_rep", [64, TC4], F32).ap()
    shb = sb("s_sh_bf", [128, 1024], BF16).ap()

    Aap = bigA_t.ap(); A1 = big1_t.ap(); XS = xs_t.ap(); WP = wp_t.ap(); ST = st_t.ap()
    Af = Aap.bitcast(F32); A1f = A1.bitcast(F32); big1f = A1f
    xsf = XS.bitcast(F32); wpf = WP.bitcast(F32); stf = ST.bitcast(F32)
    den_s2 = wpf[0:1, 0:T]

    pt = lambda n, s, d: stack.enter_context(nc.psum_tensor(n, s, d))
    P0 = pt("P0", [128, 512], F32).ap()
    P1 = pt("P1", [128, 512], F32).ap()
    P2 = pt("P2", [128, 512], F32).ap()
    P3 = pt("P3", [128, 512], F32).ap()
    P4 = pt("P4", [128, 512], F32).ap()
    PT = pt("PTb", [128, 1024], BF16).ap()

    b.drain_ps = P4[0:1, 384:512]
    b.drain_lhs = inv1024
    b.drain_rhs = identf
    for dst, src in [(A_sb, A_pack), (convw, convw_i), (convb, convb_i),
                     (bdt, bdt_i), (D_s, D_i), (l0ag, l0ag_i), (l0ab, l0ab_i),
                     (encb, encb_i), (s2cb, s2cb_i), (rb1, rb1_i), (rb2, rb2_i),
                     (eb1, eb1_i), (eb2, eb2_i), (c2sb, c2sb_i), (decb, decb_i),
                     (l0fg, l0fg_i), (l0fb, l0fb_i), (bo_s, bo_i), (mixk, mixk_i),
                     (b1p, b1p_i), (bo1, bo1_i), (bo2, bo2_i), (l1fg, l1fg_i),
                     (l1fb, l1fb_i), (identf, ident_i), (cmask, cmask_i),
                     (onesr, onesr_i), (inv1024, inv1024_i), (iota8, iota8_i),
                     (iota8m, iota8m_i), (iota9, iota9_i), (iota9m, iota9m_i),
                     (isodd, isodd_i), (iseven, iseven_i), (odd64, odd64_i),
                     (gdu_sb, gdu_i), (gy_sb, gy_i), (rb2rep, rb2r_i), (bo2rep, bo2r_i)]:
        b.dma(dst, src)
    b.dma(identb, ident_i, q="gpsimd")
    b.dma(b2bf, b2f, q="gpsimd")
    b.dma(rw1, rw1_i, q="gpsimd")
    b.dma(rw2, rw2_i, q="gpsimd")
    b.dma(onesb, onesr_i, q="gpsimd")
    w0 = b.val("dma")
    b.act(A_sb, A_sb, AF.Exp, waits=[("dma", w0)])
    b.emit("scalar", lambda e: e.mul(A_sb, A_sb, -1.0),
           waits=[("sc", b.val("sc"))], inc=("sc", 1))
    b.barrier()

    def ln_stats(tiles, W, w0=False, extra=()):
        vx0 = b.val("vx")
        mm0 = b.val("mm")
        sc0 = b.val("sc")
        for k in range(8):
            wmm = list(extra) + ([("vx", vx0), ("sc", sc0)] if w0 else []) \
                if k == 0 else ()
            b.mm(P4[0:1, 0:W], inv1024, tiles[k], start=(k == 0), stop=(k == 7),
                 waits=wmm)
        for k in range(8):
            sq = wpf[:, (k % 2) * 512:(k % 2) * 512 + W]
            wv = [("mm", b.val("mm"))] if k >= 2 else \
                (list(extra) + [("vx", vx0), ("mm", mm0)] if (w0 and k == 0) else ())
            b.act(sq, tiles[k], AF.Square, waits=wv)
            b.mm(P3[0:1, 0:W], inv1024, sq, start=(k == 0), stop=(k == 7),
                 waits=[("sc", b.val("sc"))], skip_gc=True)
        mmv = b.val("mm")
        b.vop("tensor_copy", mean_r[:, 0:W], P4[0:1, 0:W], waits=[("mm", mmv)])
        b.vop("tensor_copy", msq_r[:, 0:W], P3[0:1, 0:W])
        b.vch("tensor_mul", tmp_r[:, 0:W], mean_r[:, 0:W], mean_r[:, 0:W])
        b.vch("tensor_sub", tmp_r[:, 0:W], msq_r[:, 0:W], tmp_r[:, 0:W])
        b.vch("tensor_scalar", tmp_r[:, 0:W], tmp_r[:, 0:W], 1e-5, None, ALU.add)
        vv = b.val("vx")
        b.act(tmp_r[:, 0:W], tmp_r[:, 0:W], AF.Sqrt, waits=[("vx", vv)])
        b.vop("reciprocal", rstd_r[:, 0:W], tmp_r[:, 0:W], waits=[("sc", b.val("sc"))])
        vv = b.val("vx")
        b.mm(P2[:, 0:W], onesr, mean_r[:, 0:W], start=True, stop=True, waits=[("vx", vv)])
        b.vop("tensor_copy", stf[:, 0:W], P2[:, 0:W], waits=[("mm", b.val("mm"))])
        vv = b.val("vx")
        b.mm(P2[:, 0:W], onesr, rstd_r[:, 0:W], start=True, stop=True, waits=[("vx", vv)])
        b.vop("tensor_copy", stf[:, 512:512 + W], P2[:, 0:W], waits=[("mm", b.val("mm"))])
        return stf[:, 0:W], stf[:, 512:512 + W]

    if PHASE_LIMIT >= 1:
        chunksA = [(0, 3)] + [(3 + c * TC, 3 + (c + 1) * TC) for c in range(NCH)]

        def sa_buf(i, k, W):
            if i % 2 == 0:
                return xsf[:, k * 512:k * 512 + W]
            return Af[:, 8204 + k * 512:8204 + k * 512 + W]

        def sa_loads(i, w):
            a0, a1 = chunksA[i]
            for k in range(8):
                b.dma(sa_buf(i, k, a1 - a0), x_ext[k * 128:(k + 1) * 128, a0:a1],
                      waits=w if k == 0 else ())
            return b.val("dma")

        sa_mk = {0: sa_loads(0, ())}
        for ci, (a0, a1) in enumerate(chunksA):
            W = a1 - a0
            tiles = [sa_buf(ci, k, W) for k in range(8)]
            if ci + 1 < len(chunksA):
                sa_mk[ci + 1] = sa_loads(ci + 1, [("vx", b.val("vx")),
                                                  ("sc", b.val("sc")),
                                                  ("mm", b.val("mm"))])
            m_rep, r_rep = ln_stats(tiles, W, w0=True, extra=[("dma", sa_mk[ci])])
            vv0 = b.val("vx")
            for k in range(8):
                wsub = [("vx", vv0), ("dma", sa_mk[ci])] if k == 0 else ()
                b.vop("tensor_sub", tiles[k], tiles[k], m_rep, waits=wsub)
                b.vch("tensor_tensor", Aap[:, k * 2051 + a0:k * 2051 + a1],
                      tiles[k], r_rep, ALU.mult)
                if a0 >= 3:
                    b.fence_v(Aap[:, k * 2051 + a1 - 1:k * 2051 + a1])
                    b.dma(ln_dram.ap()[k * 128:(k + 1) * 128, a0 - 3:a1 - 3],
                          Aap[:, k * 2051 + a0:k * 2051 + a1], waits=[("vx", b.val("vx"))])
        b.barrier()

        for c in range(NCH):
            e0 = 3 + c * TC
            lw_tab = {0: b.dma(WP[:, 0:1024], W_in_p[:, 0:1024],
                               waits=[("mm", b.val("mm"))], q="gpsimd")}
            for o in range(32):
                half = (o % 2) * 2048
                if o + 1 < 32:
                    lw_tab[o + 1] = b.dma(
                        WP[:, ((o + 1) % 2) * 2048:((o + 1) % 2) * 2048 + 1024],
                        W_in_p[:, (o + 1) * 1024:(o + 2) * 1024],
                        waits=[("mm", b.val("mm"))], q="gpsimd")
                lw = lw_tab[o]
                ps = P0 if o % 2 == 0 else P1
                if o < 16 and c == 0:
                    for k in range(8):
                        wv0 = [("gdma", lw), ("vx", b.val("vx"))] if k == 0 else ()
                        b.mm(P2[:, 0:3], WP[:, half + k * 128:half + (k + 1) * 128],
                             Aap[:, k * 2051:k * 2051 + 3], start=(k == 0), stop=(k == 7),
                             waits=wv0)
                    b.vop("tensor_copy", XS[:, o * 515:o * 515 + 3], P2[:, 0:3],
                          waits=[("mm", b.val("mm"))])
                for k in range(8):
                    wv = []
                    if k == 0:
                        wv = [("gdma", lw), ("vx", b.val("vx")), ("sc", b.val("sc"))]
                    b.mm(ps[:, 0:TC], WP[:, half + k * 128:half + (k + 1) * 128],
                         Aap[:, k * 2051 + e0:k * 2051 + e0 + TC],
                         start=(k == 0), stop=(k == 7), waits=wv)
                mmv = b.val("mm")
                if o < 16:
                    b.vop("tensor_copy", XS[:, o * 515 + 3:o * 515 + 515], ps[:, 0:TC],
                          waits=[("mm", mmv)])
                else:
                    kz = o - 16
                    zst = ST[:, (kz % 2) * 512:(kz % 2) * 512 + 512]
                    b.act(zst, ps[:, 0:TC], AF.Silu, waits=[("mm", mmv), ("dma", b.val("dma"))])
                    b.fence_s(zst[:, 511:512])
                    b.dma(z_dram.ap()[kz * 128:(kz + 1) * 128, c * TC:(c + 1) * TC], zst,
                          waits=[("sc", b.val("sc"))])
            b.barrier()
            if DEBUG and c == 0:
                for k in range(16):
                    b.dma(dbg["dbg_xsh"][k * 128:(k + 1) * 128, :], XS[:, k * 515:k * 515 + 3])
                for k in range(8):
                    b.dma(dbg["dbg_lnh"][k * 128:(k + 1) * 128, :], Aap[:, k * 2051:k * 2051 + 3])
                b.barrier()
            for k in range(16):
                acc = stf[:, 2048 + (k % 2) * 512:2048 + (k % 2) * 512 + 512]
                base = k * 515
                wv = [("sc", b.val("sc"))] if k >= 2 else ()
                b.vop("tensor_scalar", acc, XS[:, base:base + 512],
                      convw[:, k * 4:k * 4 + 1], None, ALU.mult, waits=wv)
                for j in range(1, 4):
                    b.vop("scalar_tensor_tensor", acc, XS[:, base + j:base + j + 512],
                          convw[:, k * 4 + j:k * 4 + j + 1], acc, ALU.mult, ALU.add)
                b.act(A1[:, k * 2048 + c * TC:k * 2048 + (c + 1) * TC], acc, AF.Silu,
                      bias=convb[:, k:k + 1], waits=[("vx", b.val("vx"))])
                b.fence_s(A1[:, k * 2048 + (c + 1) * TC - 1:k * 2048 + (c + 1) * TC])
                b.dma(cc_pre_in.ap()[2048 + k * 128:2048 + (k + 1) * 128, c * TC:(c + 1) * TC],
                      A1[:, k * 2048 + c * TC:k * 2048 + (c + 1) * TC],
                      waits=[("sc", b.val("sc"))])
                if c < NCH - 1:
                    b.vop("tensor_copy", XS[:, base:base + 3], XS[:, base + 512:base + 515],
                          waits=[("sc", b.val("sc"))])
            b.barrier()

        def wx_src(o):
            return W_x_p[:, o * 2048:(o + 1) * 2048] if o < 16 else W_xbc_p[:, 0:512]
        lw_tab = {0: b.dma(WP[:, 0:2048], wx_src(0),
                           waits=[("mm", b.val("mm"))], q="gpsimd")}
        for o in range(17):
            cols = 128 if o < 16 else 32
            half = (o % 2) * 2048
            if o + 1 < 17:
                ncols = 128 if o + 1 < 16 else 32
                lw_tab[o + 1] = b.dma(
                    WP[:, ((o + 1) % 2) * 2048:((o + 1) % 2) * 2048 + 16 * ncols],
                    wx_src(o + 1), waits=[("mm", b.val("mm"))], q="gpsimd")
            lw = lw_tab[o]
            for c in range(NCH):
                ps = P0 if (o * NCH + c) % 2 == 0 else P1
                for k in range(16):
                    wv = [("gdma", lw), ("vx", b.val("vx"))] if k == 0 else ()
                    b.mm(ps[0:cols, 0:TC], WP[:, half + k * cols:half + (k + 1) * cols],
                         A1[:, k * 2048 + c * TC:k * 2048 + (c + 1) * TC],
                         start=(k == 0), stop=(k == 15), waits=wv)
                mmv = b.val("mm")
                if o < 16:
                    b.vop("tensor_copy", Aap[:, o * 2048 + c * TC:o * 2048 + (c + 1) * TC],
                          ps[:, 0:TC], waits=[("mm", mmv)])
                else:
                    bc = ST[0:32, 0:512]
                    b.vop("tensor_copy", bc, ps[0:32, 0:TC],
                          waits=[("mm", mmv), ("dma", b.val("dma"))])
                    b.fence_v(bc[:, 511:512])
                    b.dma(cc_pre_in.ap()[4096:4128, c * TC:(c + 1) * TC], bc,
                          waits=[("vx", b.val("vx"))])
        b.barrier()
        b.emit("gpsimd",
               lambda e: e.collective_compute(
                   "AllGather", ALU.bypass, replica_groups=PAIRS,
                   ins=[cc_pre_in.ap()[2048:4128, :].opt()],
                   outs=[cc_pre_outU.ap().opt()]),
               waits=[("dma", b.val("dma")), ("gdma", b.val("gdma"))], inc=("cc", 1))
        ccU = b.val("cc")
        dpostD = b.val("dma")

        lw_tab = {0: b.dma(WP[:, 0:2048], W_dt_p[:, 0:2048],
                           waits=[("mm", b.val("mm"))], q="gpsimd")}
        for o in range(16):
            half = (o % 2) * 2048
            if o + 1 < 16:
                lw_tab[o + 1] = b.dma(
                    WP[:, ((o + 1) % 2) * 2048:((o + 1) % 2) * 2048 + 2048],
                    W_dt_p[:, (o + 1) * 2048:(o + 2) * 2048],
                    waits=[("mm", b.val("mm"))], q="gpsimd")
            lw = lw_tab[o]
            for c in range(NCH):
                ps = P0 if (o * NCH + c) % 2 == 0 else P1
                for k in range(16):
                    wv = []
                    if k == 0:
                        wv = [("gdma", lw), ("dma", b.val("dma")),
                              ("sc", b.val("sc")), ("vx", b.val("vx"))]
                    b.mm(ps[:, 0:TC], WP[:, half + k * 128:half + (k + 1) * 128],
                         Aap[:, k * 2048 + c * TC:k * 2048 + (c + 1) * TC],
                         start=(k == 0), stop=(k == 15), waits=wv)
                mmv = b.val("mm")
                sf1 = stf[:, 2048:2560]
                sf2 = stf[:, 2560:3072]
                bcol = bdt[:, o:o + 1]
                b.act(sf1, ps[:, 0:TC], AF.Abs, bias=bcol, waits=[("mm", mmv)])
                b.act(sf1, sf1, AF.Exp, scale=-1.0)
                b.act(sf1, sf1, AF.Ln, bias=1.0)
                b.act(sf2, ps[:, 0:TC], AF.Relu, bias=bcol)
                dl = ST[:, 6144:6656]
                b.vop("tensor_add", dl, sf1, sf2,
                      waits=[("sc", b.val("sc")), ("dma", b.val("dma"))])
                b.fence_v(dl[:, 511:512])
                b.dma(cc_pre_in.ap()[o * 128:(o + 1) * 128, c * TC:(c + 1) * TC], dl,
                      waits=[("vx", b.val("vx"))])
        pend0 = (b.dma(ST[:, 0:2048],
                       cc_pre_outU.ap()[2048:2049, :].partition_broadcast(128),
                       waits=[("cc", ccU), ("dma", dpostD)]),
                 b.dma(ST[:, 2048:4096],
                       cc_pre_outU.ap()[2064:2065, :].partition_broadcast(128)))
        b.emit("gpsimd",
               lambda e: e.collective_compute(
                   "AllGather", ALU.bypass, replica_groups=PAIRS,
                   ins=[cc_pre_in.ap()[0:2048, :].opt()],
                   outs=[cc_pre_outD.ap().opt()]),
               waits=[("dma", b.val("dma")), ("gdma", b.val("gdma"))], inc=("cc", 1))
        ccD = b.val("cc")
        if DEBUG:
            b.dma(dbg["dbg_pre"], cc_pre_out.ap())
            b.dma(dbg["dbg_z"], z_dram.ap())
            b.barrier()

    if PHASE_LIMIT >= 2:
        preD = cc_pre_outD.ap()[0:128, :]
        preU = cc_pre_outU.ap()[0:128, :]
        dA_read = {}

        def bslot(i):
            return ST[:, (i % 2) * 4096:(i % 2) * 4096 + 2048]

        def cslot(i):
            return ST[:, (i % 2) * 4096 + 2048:(i % 2) * 4096 + 4096]

        for q in range(2):
            def ld_bc(i, w, _q=q):
                return (("dma", b.dma(bslot(i),
                        cc_pre_outU.ap()[_q * 2080 + 2048 + i:_q * 2080 + 2049 + i, :]
                        .partition_broadcast(128), waits=w)),
                        ("dma", b.dma(cslot(i),
                         cc_pre_outU.ap()[_q * 2080 + 2064 + i:_q * 2080 + 2065 + i, :]
                         .partition_broadcast(128))))
            for k in range(8):
                wu = ()
                if k == 0:
                    wu = [("cc", ccD)] if q == 0 else [("vx", b.val("vx"))]
                b.gather(Aap[:, (8 + k) * 2048:(9 + k) * 2048], preU,
                         gdu_sb[:, q * 16 + 8 + k:q * 16 + 9 + k], waits=wu)
            gw = () if q == 0 else [("vx", b.val("vx")), ("sc", b.val("sc"))]
            for k in range(8):
                b.gather(Aap[:, k * 2048:(k + 1) * 2048], preD,
                         gdu_sb[:, q * 16 + k:q * 16 + k + 1], waits=gw if k == 0 else ())
            fv = b.dma(gdu_sb, gdu_i, q="gpsimd")
            for k in range(8):
                b.vop("tensor_mul", Aap[:, (8 + k) * 2048:(9 + k) * 2048],
                      Aap[:, k * 2048:(k + 1) * 2048],
                      Aap[:, (8 + k) * 2048:(9 + k) * 2048],
                      waits=[("gdma", fv)] if k == 0 else ())
            b.barrier()
            if q == 0:
                pend = {0: (("dma", pend0[0]), ("dma", pend0[1]))}
            else:
                pend = {0: ld_bc(0, ())}
            carry_gp = None
            for n in range(SCAN_N):
                (bsem, lb), (csem, lc) = pend.pop(n)
                if n + 1 < SCAN_N:
                    pend[n + 1] = ld_bc(n + 1, [("vx", b.val("vx"))])
                for k in range(8):
                    it = n * 8 + k
                    ping = it % 2
                    dAv = xsf[:, ping * 2048:(ping + 1) * 2048]
                    wv = []
                    if ping in dA_read:
                        wv = [("vx", dA_read[ping])]
                    b.act(dAv, Aap[:, k * 2048:(k + 1) * 2048], AF.Exp,
                          scale=A_sb[:, k * 16 + n:k * 16 + n + 1], waits=wv)
                    scv = b.val("sc")
                    dBu = WP[:, 2048:4096]
                    wv2 = [(bsem, lb)] if k == 0 else []
                    b.vop("tensor_mul", dBu, Aap[:, (8 + k) * 2048:(9 + k) * 2048],
                          bslot(n), waits=wv2)
                    h = WP[:, 0:2048]
                    init = 0.0 if q == 0 else carry[:, k * 16 + n:k * 16 + n + 1]
                    wsc = [("sc", scv)]
                    if q == 0 and carry_gp is not None:
                        wsc.append(("gp", carry_gp))
                    b.vop("tensor_tensor_scan", h, dAv, dBu, init, ALU.mult, ALU.add,
                          waits=wsc)
                    dA_read[ping] = b.val("vx")
                    if q == 0:
                        cdst = carry[:, k * 16 + n:k * 16 + n + 1]
                        csrc = h[:, 2047:2048]
                        vhv = b.val("vx")
                        b.emit("gpsimd",
                               lambda e, _d=cdst, _s=csrc: e.tensor_copy(_d, _s),
                               waits=[("vx", vhv)], inc=("gp", 1))
                        carry_gp = b.val("gp")
                    tmp = WP[:, 2048:4096]
                    wv3 = [(csem, lc)] if k == 0 else []
                    b.vop("tensor_mul", tmp, h, cslot(n), waits=wv3)
                    ya = big1f[:, k * 2048:(k + 1) * 2048]
                    if n == 0:
                        b.vop("tensor_copy", ya, tmp)
                    else:
                        b.vop("tensor_add", ya, ya, tmp)
            vx_scan_end = b.val("vx")
            for k in range(8):
                slot = XS[:, (k % 2) * 4096:(k % 2) * 4096 + 4096].bitcast(BF16) \
                    if False else XS[:, (k % 2) * 2048 * 2:(k % 2) * 2048 * 2 + 2048 * 2]
                stg = XS[:, (k % 2) * 4096:(k % 2) * 4096 + 2048]
                wv = [("dma", b.val("dma"))] if k >= 2 else ()
                wv = tuple(wv) + (("gp", b.val("gp")),)
                b.vop("tensor_copy", stg, big1f[:, k * 2048:(k + 1) * 2048], waits=wv)
                b.fence_v(stg[:, 2047:2048])
                b.dma(cc_y_in.ap()[q * 1024 + k * 128:q * 1024 + (k + 1) * 128, :],
                      stg, waits=[("vx", b.val("vx"))])
            b.barrier()
            b.emit("gpsimd",
                   lambda e, _q=q: e.collective_compute(
                       "AllGather", ALU.bypass, replica_groups=PAIRS,
                       ins=[cc_y_in.ap()[_q * 1024:(_q + 1) * 1024, :].opt()],
                       outs=[cc_y_out.ap()[_q * 2048:(_q + 1) * 2048, :].opt()]),
                   waits=[("dma", b.val("dma")), ("gdma", b.val("gdma"))],
                   inc=("cc", 1))
        cc_y_done = b.val("cc")
        if DEBUG:
            b.dma(dbg["dbg_y"], cc_y_out.ap())
            b.dma(dbg["dbg_yin"], cc_y_in.ap())
            b.barrier()

    if PHASE_LIMIT >= 3:
        y_flat = cc_y_out.ap().rearrange("d (c w) -> (d c) w", w=512)[0:128, :]
        p3_cc = [cc_y_done]
        for o in range(8):
            b.dma(A1[:, o * 2048:(o + 1) * 2048], W_out_p[:, o * 2048:(o + 1) * 2048],
                  q="gpsimd")

        def p3_yb(c):
            return 0 if c % 2 == 0 else 24576

        def p3_y(c, w):
            base = p3_yb(c)
            for k in range(16):
                b.gather(Aap[:, base + k * 512:base + (k + 1) * 512], y_flat,
                         gy_sb[:, k * 4 + c:k * 4 + c + 1], waits=w if k == 0 else ())
            return b.dma(gy_sb, gy_i, q="gpsimd")

        def p3_usz(c, w):
            for k in range(16):
                b.dma(Aap[:, 8192 + k * 512:8192 + (k + 1) * 512],
                      cc_pre_in.ap()[2048 + k * 128:2048 + (k + 1) * 128,
                                     c * TC:(c + 1) * TC], waits=w if k == 0 else ())
                b.dma(A1[:, 20480 + k * 512:20480 + (k + 1) * 512],
                      z_dram.ap()[k * 128:(k + 1) * 128, c * TC:(c + 1) * TC])
            return b.val("dma")

        def p3_ln(c, w):
            for k in range(8):
                b.dma(A1[:, 16384 + k * 512:16384 + (k + 1) * 512],
                      ln_dram.ap()[k * 128:(k + 1) * 128, c * TC:(c + 1) * TC],
                      waits=w if k == 0 else ())
            return b.val("dma")

        def p3_x0(c, w):
            for k in range(8):
                b.dma(Af[:, 8192 + k * 512:8192 + (k + 1) * 512],
                      x_ext[k * 128:(k + 1) * 128, 3 + c * TC:3 + (c + 1) * TC],
                      waits=w if k == 0 else ())
            return b.val("dma")

        fy = {0: p3_y(0, [("cc", cc_y_done)])}
        du = {0: p3_usz(0, ())}
        dln = {0: p3_ln(0, ())}
        dx0 = {0: p3_x0(0, ())}
        store_mark = None
        for c in range(NCH):
            base = p3_yb(c)
            for k in range(16):
                yk = Aap[:, base + k * 512:base + (k + 1) * 512]
                w1 = [("gdma", fy[c]), ("dma", du[c])] if k == 0 else ()
                b.vop("scalar_tensor_tensor", yk,
                      Aap[:, 8192 + k * 512:8192 + (k + 1) * 512],
                      D_s[:, k:k + 1], yk, ALU.mult, ALU.add, waits=w1)
                b.vop("tensor_mul", yk, yk, A1[:, 20480 + k * 512:20480 + (k + 1) * 512])
            vx_y2 = b.val("vx")
            if c + 1 < NCH:
                fy[c + 1] = p3_y(c + 1, [("mm", b.val("mm"))])
                du[c + 1] = p3_usz(c + 1, [("vx", vx_y2)])
            for o in range(8):
                ps = P0 if o % 2 == 0 else P1
                for k in range(16):
                    wv = ()
                    if k == 0:
                        wv = [("vx", vx_y2 if o < 2 else b.val("vx"))]
                    b.mm(ps[:, 0:TC], A1[:, (o * 16 + k) * 128:(o * 16 + k + 1) * 128],
                         Aap[:, base + k * 512:base + (k + 1) * 512],
                         start=(k == 0), stop=(k == 15), waits=wv)
                wadd = [("mm", b.val("mm")), ("dma", dln[c])]
                if o == 0 and store_mark is not None:
                    wadd.append(("dma", store_mark))
                b.vop("tensor_add", xsf[:, o * 512:(o + 1) * 512], ps[:, 0:TC],
                      A1[:, 16384 + o * 512:16384 + (o + 1) * 512], waits=wadd)
            if c + 1 < NCH:
                dln[c + 1] = p3_ln(c + 1, [("vx", b.val("vx"))])
            tiles = [xsf[:, o * 512:(o + 1) * 512] for o in range(8)]
            m_rep, r_rep = ln_stats(tiles, TC, w0=True)
            vv = b.val("vx")
            for o in range(8):
                wsub = [("vx", vv)] if o == 0 else ()
                if o == 0:
                    wsub = wsub + [("dma", dx0[c])]
                b.vop("tensor_sub", tiles[o], tiles[o], m_rep, waits=wsub)
                b.vch("tensor_mul", tiles[o], tiles[o], r_rep)
                b.vch("tensor_scalar", tiles[o], tiles[o], l0ag[:, o:o + 1],
                      l0ab[:, o:o + 1], ALU.mult, ALU.add)
                b.vch("tensor_add", tiles[o], tiles[o],
                      Af[:, 8192 + o * 512:8192 + (o + 1) * 512])
                b.fence_v(tiles[o][:, 511:512])
                b.dma(xres_A.ap()[o * 128:(o + 1) * 128, c * TC:(c + 1) * TC], tiles[o],
                      waits=[("vx", b.val("vx"))])
            store_mark = b.val("dma")
            if c + 1 < NCH:
                dx0[c + 1] = p3_x0(c + 1, [("vx", b.val("vx"))])
        b.barrier()
        if DEBUG:
            b.dma(dbg["dbg_x1"], xres_A.ap())
            b.barrier()

    if PHASE_LIMIT >= 4:
        for j in range(4):
            b.dma(A1[:, j * 4096:(j + 1) * 4096], enc_p[:, j * 4096:(j + 1) * 4096], q="gpsimd")
            b.dma(A1[:, 16384 + j * 4096:16384 + (j + 1) * 4096],
                  dec_p[:, j * 4096:(j + 1) * 4096], q="gpsimd")
        b.dma(Aap[0:64, 20480:28672], ew1_p, q="gpsimd")
        b.dma(ST[0:64, 3072:5120], c2s_p, q="gpsimd")
        b.dma(Aap[:, 28672:32768], ew2_p, q="gpsimd")
        b.dma(ST[:, 2048:3072], s2c_p, q="gpsimd")
        b.emit("tensor", lambda e: None,
               waits=[("gdma", b.val("gdma")), ("dma", b.val("dma"))])
        W4 = TC4

        def p4_x1buf(c):
            return xsf if c % 2 == 0 else Af[:, 4096:6144]

        def p4_x1(c, w):
            xb = p4_x1buf(c)
            for k in range(8):
                b.dma(xb[:, k * W4:(k + 1) * W4],
                      xres_A.ap()[k * 128:(k + 1) * 128, c * W4:(c + 1) * W4],
                      waits=w if k == 0 else ())
            return b.val("dma")

        dx1 = {0: p4_x1(0, [("vx", b.val("vx")), ("sc", b.val("sc"))])}
        p4_store = None
        for c in range(NCH4):
            xb = p4_x1buf(c)
            tiles = [xb[:, k * W4:(k + 1) * W4] for k in range(8)]
            m_rep, r_rep = ln_stats(tiles, W4, w0=True, extra=[("dma", dx1[c])])
            vv = b.val("vx")
            ln2 = []
            for k in range(8):
                b.vop("tensor_sub", wpf[:, 1024:1024 + W4], tiles[k], m_rep,
                      waits=[("vx", vv), ("dma", dx1[c]), ("mm", b.val("mm"))]
                      if k == 0 else ())
                dst = Aap[:, 16384 + k * W4:16384 + (k + 1) * W4]
                b.vch("tensor_tensor", dst, wpf[:, 1024:1024 + W4], r_rep, ALU.mult)
                ln2.append(dst)
            vx_ln2 = b.val("vx")
            if c + 1 < NCH4:
                dx1[c + 1] = p4_x1(c + 1, [("vx", b.val("vx")),
                                           ("sc", b.val("sc")),
                                           ("mm", b.val("mm"))])
            for o in range(16):
                ps = P0 if o % 2 == 0 else P1
                for k in range(8):
                    wv = ()
                    if k == 0:
                        wv = [("sc", b.val("sc"))]
                        if o < 2:
                            wv.append(("vx", vx_ln2))
                    b.mm(ps[:, 0:W4], A1[:, (o * 8 + k) * 128:(o * 8 + k + 1) * 128],
                         ln2[k], start=(k == 0), stop=(k == 7), waits=wv)
                b.act(Aap[:, o * W4:(o + 1) * W4], ps[:, 0:W4], AF.Sigmoid,
                      bias=encb[:, o:o + 1], waits=[("mm", b.val("mm"))])
            spk = [Aap[:, o * W4:(o + 1) * W4] for o in range(16)]
            for k in range(16):
                wv = [("sc", b.val("sc")), ("vx", b.val("vx"))] if k == 0 else ()
                b.mm(P2[0:64, 0:W4], ST[:, 2048 + k * 64:2048 + (k + 1) * 64], spk[k],
                     start=(k == 0), stop=(k == 15), waits=wv)
            b.vop("tensor_scalar", moe_f, P2[0:64, 0:W4], s2cb[:, 0:1], None, ALU.add,
                  waits=[("mm", b.val("mm"))])
            b.vch("tensor_copy", cont_bf, moe_f)
            vx_cont = b.val("vx")
            b.mm(P2[0:64, 0:W4], rw1, cont_bf, start=True, stop=True,
                 waits=[("vx", vx_cont)])
            b.act(rh_bf, P2[0:64, 0:W4], AF.Relu, bias=rb1[:, 0:1],
                  waits=[("mm", b.val("mm"))])
            if DEBUG and c == 0:
                b.dma(dbg["dbg_rh"], rh_bf, q="gpsimd")
                b.dma(dbg["dbg_cont"], cont_bf, q="gpsimd")
                b.barrier()
            if DEBUG and c == 0 and P4STAGE >= 21:
                b.fence_v(logit_sb[0:8, 0:1])
                b.dma(dbg["dbg_lgr"], logit_sb[0:8, 0:W4], waits=[("vx", b.val("vx"))])
                b.barrier()
            for tb in range(W4 // 128 if P4STAGE >= 3 else 0):
                po = (c % 2) * 32 + tb * 16
                wtb = [("vx", b.val("vx"))]
                if tb == 0:
                    wtb.append(("sc", b.val("sc")))
                b.mm(P3[:, po:po + 8], rh_bf[0:64, tb * 128:(tb + 1) * 128], rw2,
                     start=True, stop=True, waits=wtb)
                wcc = [("mm", b.val("mm"))]
                if tb == 0:
                    wcc.append(("dma", b.val("dma")))
                b.vch("tensor_tensor", lgt[:, 0:8], P3[:, po:po + 8], rb2rep, ALU.add,
                      waits=wcc)
                b.vch("reduce_max", m1t, lgt[:, 0:8], AX.X)
                vv = b.val("vx")
                b.vch("tensor_scalar", msk1[:, 0:8], lgt[:, 0:8], m1t[:, 0:1], None,
                      ALU.is_equal, waits=[("vx", vv)])
                b.vch("tensor_mul", candt[:, 0:8], msk1[:, 0:8], iota8m)
                b.vch("tensor_scalar", candt[:, 0:8], candt[:, 0:8], 8.0, None, ALU.add)
                b.vch("tensor_reduce", i1t, candt[:, 0:8], AX.X, ALU.min)
                vv = b.val("vx")
                b.vch("tensor_scalar", msk1[:, 0:8], iota8, i1t[:, 0:1], None,
                      ALU.is_equal, waits=[("vx", vv)])
                b.vch("tensor_scalar", candt[:, 0:8], msk1[:, 0:8], NEG_BIG, None, ALU.mult)
                b.vch("tensor_add", candt[:, 0:8], candt[:, 0:8], lgt[:, 0:8])
                b.vch("reduce_max", m2t, candt[:, 0:8], AX.X)
                vv = b.val("vx")
                b.vch("tensor_scalar", msk2[:, 0:8], candt[:, 0:8], m2t[:, 0:1], None,
                      ALU.is_equal, waits=[("vx", vv)])
                b.vch("tensor_mul", msk2[:, 0:8], msk2[:, 0:8], iota8m)
                b.vch("tensor_scalar", msk2[:, 0:8], msk2[:, 0:8], 8.0, None, ALU.add)
                b.vch("tensor_reduce", i1t, msk2[:, 0:8], AX.X, ALU.min)
                vv = b.val("vx")
                b.vch("tensor_scalar", msk2[:, 0:8], iota8, i1t[:, 0:1], None,
                      ALU.is_equal, waits=[("vx", vv)])
                b.vch("tensor_sub", w1t, m1t, m2t)
                vv = b.val("vx")
                b.act(w1t, w1t, AF.Sigmoid, waits=[("vx", vv)])
                scv = b.val("sc")
                b.vch("tensor_scalar", w2t, w1t, -1.0, 1.0, ALU.mult, ALU.add,
                      waits=[("sc", scv)])
                vv = b.val("vx")
                b.vch("tensor_scalar", candt[:, 0:8], msk1[:, 0:8], w1t[:, 0:1], None,
                      ALU.mult, waits=[("vx", vv)])
                b.vch("scalar_tensor_tensor", candt[:, 0:8], msk2[:, 0:8], w2t[:, 0:1],
                      candt[:, 0:8], ALU.mult, ALU.add)
                vv = b.val("vx")
                b.mm(P3[0:8, 256:384], candt[:, 0:8], identf, start=True, stop=True,
                     waits=[("vx", vv)])
                b.vch("tensor_copy", comb_sb[0:8, tb * 128:(tb + 1) * 128],
                      P3[0:8, 256:384], waits=[("mm", b.val("mm"))])
            b.fence_v(comb_sb[0:8, 0:1])
            b.fence_v(comb_sb[0:8, W4 - 1:W4])
            b.dma(comb_dram.ap()[c * 8:(c + 1) * 8, :], comb_sb,
                  waits=[("vx", b.val("vx"))])
            for e in range(8 if P4STAGE >= 4 else 0):
                lb = b.dma(comb_rep, comb_dram.ap()[c * 8 + e:c * 8 + e + 1, :]
                           .partition_broadcast(64), waits=[("vx", b.val("vx"))])
                for oc in range(8):
                    wenc = [("sc", b.val("sc"))]
                    if e == 0 and oc == 0:
                        wenc.append(("vx", vx_cont))
                    b.mm(P0[:, 0:W4], Aap[0:64, 20480 + (e * 8 + oc) * 128:
                                          20480 + (e * 8 + oc + 1) * 128],
                         cont_bf, start=True, stop=True, waits=wenc)
                    b.act(shb[:, 0:W4], P0[:, 0:W4], AF.Relu,
                          bias=eb1[:, e * 8 + oc:e * 8 + oc + 1],
                          waits=[("mm", b.val("mm"))])
                    b.mm(P1[0:64, 0:W4], Aap[:, 28672 + (e * 8 + oc) * 64:
                                             28672 + (e * 8 + oc + 1) * 64],
                         shb[:, 0:W4], start=(oc == 0), stop=(oc == 7),
                         waits=[("sc", b.val("sc"))], skip_gc=True)
                mmv = b.val("mm")
                b.vop("tensor_scalar", eo_f, P1[0:64, 0:W4], eb2[:, e:e + 1], None,
                      ALU.add, waits=[("mm", mmv)])
                b.vch("tensor_mul", eo_f, eo_f, comb_rep, waits=[("dma", lb)])
                if e == 0:
                    b.vch("tensor_copy", moe_f, eo_f)
                else:
                    b.vch("tensor_add", moe_f, moe_f, eo_f)
            b.vch("tensor_copy", moe_bf, moe_f)
            vx_moe = b.val("vx")
            for o in range(16):
                ps = P0 if o % 2 == 0 else P1
                wc2 = [("sc", b.val("sc"))]
                if o < 2:
                    wc2.append(("vx", vx_moe))
                b.mm(ps[:, 0:W4], ST[0:64, 3072 + o * 128:3072 + (o + 1) * 128],
                     moe_bf, start=True, stop=True, waits=wc2)
                b.act(Aap[:, o * W4:(o + 1) * W4], ps[:, 0:W4], AF.Sigmoid,
                      bias=c2sb[:, o:o + 1], waits=[("mm", b.val("mm"))])
            rates = [Aap[:, k * W4:(k + 1) * W4] for k in range(16)]
            for o in range(8):
                ps = P0 if o % 2 == 0 else P1
                for k in range(16):
                    wv = [("sc", b.val("sc"))] if k == 0 else ()
                    b.mm(ps[:, 0:W4], A1[:, 16384 + (o * 16 + k) * 128:
                                         16384 + (o * 16 + k + 1) * 128],
                         rates[k], start=(k == 0), stop=(k == 15), waits=wv)
                wact = [("mm", b.val("mm"))]
                if o == 0:
                    wact.append(("vx", b.val("vx")))
                    if p4_store is not None:
                        wact.append(("dma", p4_store))
                b.act(xsf[:, 2048 + o * W4:2048 + (o + 1) * W4], ps[:, 0:W4], AF.Sigmoid,
                      bias=decb[:, o:o + 1], waits=wact)
            tiles = [xsf[:, 2048 + o * W4:2048 + (o + 1) * W4] for o in range(8)]
            m_rep, r_rep = ln_stats(tiles, W4, w0=True)
            vv = b.val("vx")
            scv9 = b.val("sc")
            for o in range(8):
                b.vop("tensor_sub", tiles[o], tiles[o], m_rep,
                      waits=[("vx", vv), ("sc", scv9)] if o == 0 else ())
                b.vch("tensor_mul", tiles[o], tiles[o], r_rep)
                b.vch("tensor_scalar", tiles[o], tiles[o], l0fg[:, o:o + 1],
                      l0fb[:, o:o + 1], ALU.mult, ALU.add)
                b.vch("tensor_add", tiles[o], tiles[o], xb[:, o * W4:(o + 1) * W4])
                b.fence_v(tiles[o][:, W4 - 1:W4])
                b.dma(xres_B.ap()[o * 128:(o + 1) * 128, c * W4:(c + 1) * W4], tiles[o],
                      waits=[("vx", b.val("vx"))])
            p4_store = b.val("dma")
        b.barrier()
        if DEBUG:
            b.dma(dbg["dbg_x2"], xres_B.ap())
            b.dma(dbg["dbg_comb"], comb_dram.ap())
            b.barrier()

    if PHASE_LIMIT >= 5:
        for j in range(2):
            b.dma(A1[:, j * 4096:(j + 1) * 4096], Wq_p[:, j * 4096:(j + 1) * 4096], q="gpsimd")
            b.dma(A1[:, 8192 + j * 4096:8192 + (j + 1) * 4096],
                  Wk_p[:, j * 4096:(j + 1) * 4096], q="gpsimd")
            b.dma(A1[:, 16384 + j * 4096:16384 + (j + 1) * 4096],
                  Wv_p[:, j * 4096:(j + 1) * 4096], q="gpsimd")
            b.dma(A1[:, 24576 + j * 4096:24576 + (j + 1) * 4096],
                  Wo_p[:, j * 4096:(j + 1) * 4096], q="gpsimd")
        for c in range(NCH):
            for k in range(8):
                b.dma(Af[:, 8192 + k * 512:8192 + (k + 1) * 512],
                      xres_B.ap()[k * 128:(k + 1) * 128, c * TC:(c + 1) * TC],
                      waits=[("vx", b.val("vx"))] if k == 0 else ())
            b.barrier()
            tiles = [Af[:, 8192 + k * 512:8192 + (k + 1) * 512] for k in range(8)]
            m_rep, r_rep = ln_stats(tiles, TC)
            vv = b.val("vx")
            for k in range(8):
                b.vop("tensor_sub", tiles[k], tiles[k], m_rep,
                      waits=[("vx", vv)] if k == 0 else ())
                b.vch("tensor_tensor", Aap[:, k * 2048 + c * TC:k * 2048 + (c + 1) * TC],
                      tiles[k], r_rep, ALU.mult)
            b.barrier()
        b.emit("tensor", lambda e: None, waits=[("gdma", b.val("gdma"))])
        a5_store = {}
        a5_i = 0
        for (wofs, dst) in [(0, q_dram), (8192, k_dram)]:
            for o in range(8):
                for c in range(NCH):
                    ps = P0 if (o * NCH + c) % 2 == 0 else P1
                    for k in range(8):
                        wv = [("vx", b.val("vx"))] if (o + c > 0 and k == 0) else ()
                        b.mm(ps[:, 0:TC],
                             A1[:, wofs + (o * 8 + k) * 128:wofs + (o * 8 + k + 1) * 128],
                             Aap[:, k * 2048 + c * TC:k * 2048 + (c + 1) * TC],
                             start=(k == 0), stop=(k == 7), waits=wv)
                    sofs = (a5_i % 2) * 512
                    wvp = [("mm", b.val("mm"))]
                    if a5_i % 2 in a5_store:
                        wvp.append(("dma", a5_store[a5_i % 2]))
                    elif a5_i < 2:
                        wvp.append(("dma", b.val("dma")))
                    b.vop("tensor_scalar", ST[:, sofs:sofs + 512], ps[:, 0:TC],
                          0.0, 0.001, ALU.max, ALU.add, waits=wvp)
                    b.fence_v(ST[:, sofs + 511:sofs + 512])
                    a5_store[a5_i % 2] = b.dma(
                        dst.ap()[o * 128:(o + 1) * 128, c * TC:(c + 1) * TC],
                        ST[:, sofs:sofs + 512], waits=[("vx", b.val("vx"))])
                    a5_i += 1
        b.barrier()
        a5_store = {}
        a5_i = 0
        for tb in range(16):
            for vc in range(2):
                ps = P0 if (tb * 2 + vc) % 2 == 0 else P1
                for k in range(8):
                    wv = [("vx", b.val("vx"))] if (tb + vc > 0 and k == 0) else ()
                    b.mm(ps[:, 0:512],
                         Aap[:, k * 2048 + tb * 128:k * 2048 + (tb + 1) * 128],
                         A1[:, 16384 + k * 1024 + vc * 512:16384 + k * 1024 + (vc + 1) * 512],
                         start=(k == 0), stop=(k == 7), waits=wv)
                sofs = (a5_i % 2) * 512
                wvp = [("mm", b.val("mm"))]
                if a5_i % 2 in a5_store:
                    wvp.append(("dma", a5_store[a5_i % 2]))
                elif a5_i < 2:
                    wvp.append(("dma", b.val("dma")))
                b.vop("tensor_copy", ST[:, sofs:sofs + 512], ps[:, 0:512], waits=wvp)
                b.fence_v(ST[:, sofs + 511:sofs + 512])
                a5_store[a5_i % 2] = b.dma(
                    v_dram.ap()[tb * 128:(tb + 1) * 128, vc * 512:(vc + 1) * 512],
                    ST[:, sofs:sofs + 512], waits=[("vx", b.val("vx"))])
                a5_i += 1
        b.barrier()
        b.vop("memset", vaug[:, 64:65], 1.0)
        b.vop("memset", vaug[:, 129:130], 1.0)
        b.vop("memset", vaug2[:, 64:65], 1.0)
        b.vop("memset", vaug2[:, 129:130], 1.0)
        SFs = (S_f, S_f2)
        SAs = (saug, saug2)
        VAs = (vaug, vaug2)
        KTs = (ktok, ktok2)
        AMs = (amb, amb2)
        DENs = (den_s, den_s2)
        SUPD = (P4[0:64, 0:65], P1[0:64, 0:65])
        for j in range(8 if P5STAGE >= 2 else 0):
            lq = {}; lk = {}
            for s in (0, 1):
                hh = 2 * j + s
                QH = ST[0:64, s * 4096:s * 4096 + 2048]
                KH = ST[0:64, s * 4096 + 2048:s * 4096 + 4096]
                lq[s] = b.dma(QH, q_dram.ap()[hh * 64:(hh + 1) * 64, :],
                              waits=[("mm", b.val("mm")), ("vx", b.val("vx"))] if s == 0 else ())
                lk[s] = b.dma(KH, k_dram.ap()[hh * 64:(hh + 1) * 64, :])
                b.vop("memset", SFs[s], 0.0, waits=[("dma", b.val("dma"))] if (j > 0 and s == 0) else ())
            for cch in range(16):
                for s in (0, 1):
                    hh = 2 * j + s
                    QH = ST[0:64, s * 4096:s * 4096 + 2048]
                    KH = ST[0:64, s * 4096 + 2048:s * 4096 + 4096]
                    numv = xsf[0:64, s * 2048:(s + 1) * 2048]
                    vofs = (cch % 2) * 65
                    lv = b.dma(VAs[s][:, vofs:vofs + 64],
                               v_dram.ap()[cch * 128:(cch + 1) * 128, hh * 64:(hh + 1) * 64],
                               waits=[("mm", b.val("mm"))])
                    wk0 = [("vx", b.val("vx"))]
                    if cch == 0:
                        wk0.append(("dma", lk[s]))
                    b.mm(P4[:, 128 + 64 * s:192 + 64 * s], KH[:, cch * 128:(cch + 1) * 128],
                         identb[0:64, 0:64], start=True, stop=True, skip_gc=True, waits=wk0)
                    b.vop("tensor_copy", KTs[s], P4[:, 128 + 64 * s:192 + 64 * s],
                          waits=[("mm", b.val("mm"))])
                    wq0 = [("vx", b.val("vx"))]
                    if cch == 0:
                        wq0.append(("dma", lq[s]))
                    b.mm(P2[:, s * 128:(s + 1) * 128], KH[:, cch * 128:(cch + 1) * 128],
                         QH[:, cch * 128:(cch + 1) * 128], start=True, stop=True,
                         skip_gc=True, waits=wq0)
                    b.vop("tensor_mul", AMs[s], P2[:, s * 128:(s + 1) * 128], cmask,
                          waits=[("mm", b.val("mm"))])
                    b.vch("tensor_copy", SAs[s], SFs[s])
                    vv = b.val("vx")
                    b.mm(P3[0:64, s * 128:(s + 1) * 128], VAs[s][:, vofs:vofs + 64], AMs[s],
                         start=True, stop=True, skip_gc=True,
                         waits=[("vx", vv), ("dma", lv)], inc=False)
                    b.mm(P3[0:64, s * 128:(s + 1) * 128], SAs[s][:, 0:64],
                         QH[:, cch * 128:(cch + 1) * 128], start=False, stop=True, skip_gc=True)
                    b.mm(P4[0:1, 256 + 128 * s:384 + 128 * s], VAs[s][:, vofs + 64:vofs + 65],
                         AMs[s], start=True, stop=True, skip_gc=True, inc=False)
                    b.mm(P4[0:1, 256 + 128 * s:384 + 128 * s], SAs[s][:, 64:65],
                         QH[:, cch * 128:(cch + 1) * 128], start=False, stop=True, skip_gc=True)
                    mmv = b.val("mm")
                    wnum = [("mm", mmv)]
                    if cch == 0:
                        wnum.append(("dma", b.val("dma")))
                    b.vop("tensor_copy", numv[:, cch * 128:(cch + 1) * 128],
                          P3[0:64, s * 128:(s + 1) * 128], waits=wnum)
                    b.vop("tensor_copy", DENs[s][:, cch * 128:(cch + 1) * 128],
                          P4[0:1, 256 + 128 * s:384 + 128 * s])
                    b.mm(SUPD[s], KTs[s], VAs[s][:, vofs:vofs + 65], start=True, stop=True,
                         skip_gc=True, waits=[("vx", b.val("vx"))])
                    b.vch("tensor_add", SFs[s], SFs[s], SUPD[s], waits=[("mm", b.val("mm"))])
            for s in (0, 1):
                hh = 2 * j + s
                numv = xsf[0:64, s * 2048:(s + 1) * 2048]
                b.fence_v(SFs[s][:, 64:65])
                b.fence_v(numv[:, 2047:2048])
                b.fence_v(DENs[s][:, 2047:2048])
                vv = b.val("vx")
                b.dma(cc_att_in.ap()[:, hh * 65:(hh + 1) * 65], SFs[s], waits=[("vx", vv)])
                b.dma(num_dram.ap()[hh * 64:(hh + 1) * 64, :], numv, waits=[("vx", vv)])
                b.dma(den_dram.ap()[hh:hh + 1, :], DENs[s], waits=[("vx", vv)])
        b.barrier()
        b.emit("gpsimd",
               lambda e: e.collective_compute(
                   "AllGather", ALU.bypass, replica_groups=PAIRS,
                   ins=[cc_att_in.ap().opt()], outs=[cc_att_out.ap().opt()]),
               waits=[("dma", b.val("dma")), ("gdma", b.val("gdma"))], inc=("cc", 1))
        b.barrier()
        def c5_QH(p):
            return ST[0:64, p * 2048:(p + 1) * 2048]

        def c5_NUM(p):
            return xsf[0:64, p * 2048:(p + 1) * 2048]

        def c5_DEN(p):
            return big1f[0:1, p * 2048:(p + 1) * 2048]

        c5_SPV = (sprev_f, stf[0:64, 2048:2113])
        c5_SP64 = (sprev64, ST[0:64, 4240:4304])
        c5_SPK = (spk1, ST[0:64, 4380:4381])
        c5_DBF = (A1[0:1, 8192:10240], A1[0:1, 10240:12288])
        c5_CORR = (P3, P2)
        c5_BC = (P0, P1)
        c5_L = {}
        c5_vxend = {}
        c5_mmq = {}
        c5_mmend = {}

        def c5_loads(hh):
            p = hh % 2
            if hh >= 2:
                w = [("vx", c5_vxend[hh - 2]), ("mm", c5_mmq[hh - 2])]
            else:
                w = [("vx", b.val("vx")), ("mm", b.val("mm"))]
            lq = b.dma(c5_QH(p), q_dram.ap()[hh * 64:(hh + 1) * 64, :], waits=w)
            b.dma(c5_SPV[p], cc_att_out.ap()[0:64, hh * 65:(hh + 1) * 65])
            b.dma(c5_NUM(p), num_dram.ap()[hh * 64:(hh + 1) * 64, :])
            ld = b.dma(c5_DEN(p), den_dram.ap()[hh:hh + 1, :])
            c5_L[hh] = (lq, ld)

        NH5C = 16 if P5STAGE >= 3 else 0
        for hh in range(min(2, NH5C)):
            c5_loads(hh)
        for hh in range(NH5C):
            p = hh % 2
            lq, ld = c5_L[hh]
            Qh = c5_QH(p)
            numv = c5_NUM(p)
            denv = c5_DEN(p)
            CR = c5_CORR[p]
            wsp = [("dma", ld)]
            if hh >= 2:
                wsp.append(("mm", c5_mmq[hh - 2]))
            b.vop("tensor_scalar", c5_SP64[p], c5_SPV[p][:, 0:64], odd64[:, 0:1], None,
                  ALU.mult, waits=wsp)
            b.vop("tensor_scalar", c5_SPK[p], c5_SPV[p][:, 64:65], odd64[:, 0:1], None,
                  ALU.mult)
            for qc in range(4):
                for j4 in range(4):
                    b.mm(CR[0:64, j4 * 128:(j4 + 1) * 128], c5_SP64[p],
                         Qh[:, qc * 512 + j4 * 128:qc * 512 + (j4 + 1) * 128],
                         start=True, stop=True, skip_gc=True,
                         waits=[("vx", b.val("vx")), ("dma", lq)] if j4 == 0 else ())
                b.mm(P4[0:1, 0:512], c5_SPK[p], Qh[:, qc * 512:(qc + 1) * 512],
                     start=True, stop=True, skip_gc=True)
                mmv = b.val("mm")
                for h2 in range(2):
                    b.vop("tensor_add",
                          numv[:, qc * 512 + h2 * 256:qc * 512 + (h2 + 1) * 256],
                          numv[:, qc * 512 + h2 * 256:qc * 512 + (h2 + 1) * 256],
                          CR[0:64, h2 * 256:(h2 + 1) * 256],
                          waits=[("mm", mmv)] if h2 == 0 else ())
                b.vch("tensor_add", denv[:, qc * 512:(qc + 1) * 512],
                      denv[:, qc * 512:(qc + 1) * 512], P4[0:1, 0:512])
            c5_mmq[hh] = b.val("mm")
            b.vch("reciprocal", denv, denv)
            wdb = [("mm", c5_mmend[hh - 2])] if hh >= 2 else ()
            b.vch("tensor_copy", c5_DBF[p], denv, waits=wdb)
            vv = b.val("vx")
            for qc in range(4):
                b.mm(c5_BC[p][0:64, 0:512], onesb[0:1, 0:64],
                     c5_DBF[p][:, qc * 512:(qc + 1) * 512],
                     start=True, stop=True, skip_gc=True,
                     waits=[("vx", vv)] if qc == 0 else [("vx", b.val("vx"))])
                wmul = [("mm", b.val("mm"))]
                if qc == 0:
                    wmul.append(("gdma", b.val("gdma")))
                b.vop("tensor_mul", wpf[0:64, qc * 512:(qc + 1) * 512],
                      numv[:, qc * 512:(qc + 1) * 512], c5_BC[p][0:64, 0:512],
                      waits=wmul)
            c5_mmend[hh] = b.val("mm")
            c5_vxend[hh] = b.val("vx")
            if hh + 2 < NH5C:
                c5_loads(hh + 2)
            b.fence_v(wpf[0:64, 2047:2048])
            b.dma(attn_dram.ap()[hh * 64:(hh + 1) * 64, :],
                  wpf[0:64, 0:2048], waits=[("vx", b.val("vx"))], q="gpsimd")
        b.barrier()
        if DEBUG:
            b.dma(dbg["dbg_att"], attn_dram.ap())
        for c in range(NCH if P5STAGE >= 4 else 0):
            for k in range(8):
                b.dma(Aap[:, k * 512:(k + 1) * 512],
                      attn_dram.ap()[k * 128:(k + 1) * 128, c * TC:(c + 1) * TC],
                      waits=[("vx", b.val("vx"))] if k == 0 else ())
                b.dma(Af[:, 8192 + k * 512:8192 + (k + 1) * 512],
                      xres_B.ap()[k * 128:(k + 1) * 128, c * TC:(c + 1) * TC])
            b.barrier()
            for o in range(8):
                ps = P0 if o % 2 == 0 else P1
                for k in range(8):
                    wv = [("vx", b.val("vx"))] if o >= 2 and k == 0 else ()
                    b.mm(ps[:, 0:TC],
                         A1[:, 24576 + (o * 8 + k) * 128:24576 + (o * 8 + k + 1) * 128],
                         Aap[:, k * 512:(k + 1) * 512], start=(k == 0), stop=(k == 7),
                         waits=wv)
                b.vop("tensor_scalar", wpf[:, 0:512], ps[:, 0:TC], bo_s[:, o:o + 1],
                      None, ALU.add, waits=[("mm", b.val("mm")), ("dma", b.val("dma"))])
                b.vch("tensor_add", wpf[:, 0:512], wpf[:, 0:512],
                      Af[:, 8192 + o * 512:8192 + (o + 1) * 512])
                b.fence_v(wpf[:, 511:512])
                b.dma(xres_A.ap()[o * 128:(o + 1) * 128, c * TC:(c + 1) * TC],
                      wpf[:, 0:512], waits=[("vx", b.val("vx"))])
            b.barrier()
        if DEBUG:
            b.dma(dbg["dbg_x3"], xres_A.ap())
            b.barrier()

    if PHASE_LIMIT >= 6:
        for j in range(3):
            b.dma(A1[:, j * 6144:(j + 1) * 6144], W2_p[:, j * 6144:(j + 1) * 6144], q="gpsimd")
        b.dma(A1[:, 18432:20480], Wo1_p, q="gpsimd")
        b.dma(wo2b, Wo2_p, q="gpsimd")
        b.emit("tensor", lambda e: None, waits=[("gdma", b.val("gdma"))])
        for c in range(NCH):
            for k in range(8):
                b.dma(Af[:, 8704 + k * 512:8704 + (k + 1) * 512],
                      xres_A.ap()[k * 128:(k + 1) * 128, c * TC:(c + 1) * TC],
                      waits=[("vx", b.val("vx"))] if k == 0 else ())
            b.barrier()
            tiles = [Af[:, 8704 + k * 512:8704 + (k + 1) * 512] for k in range(8)]
            m_rep, r_rep = ln_stats(tiles, TC)
            vv = b.val("vx")
            for k in range(8):
                b.vop("tensor_sub", tiles[k], tiles[k], m_rep,
                      waits=[("vx", vv)] if k == 0 else ())
                b.vch("tensor_tensor",
                      Aap[:, k * 2050 + 1 + c * TC:k * 2050 + 1 + (c + 1) * TC],
                      tiles[k], r_rep, ALU.mult)
            b.barrier()
        for k in range(8):
            b.vch("tensor_copy", mix_sb[:, k:k + 1], Aap[:, k * 2050 + 1:k * 2050 + 2])
            b.vch("tensor_copy", mix_sb[:, 8 + k:9 + k],
                  Aap[:, k * 2050 + 2048:k * 2050 + 2049])
        b.fence_v(mix_sb[:, 15:16])
        b.dma(cc_mix_in.ap(), mix_sb, waits=[("vx", b.val("vx"))])
        b.emit("gpsimd",
               lambda e: e.collective_compute(
                   "AllGather", ALU.bypass, replica_groups=PAIRS,
                   ins=[cc_mix_in.ap().opt()], outs=[cc_mix_out.ap().opt()]),
               waits=[("dma", b.val("dma")), ("gdma", b.val("gdma"))], inc=("cc", 1))
        b.barrier()
        b.dma(mixh[:, 0:16], cc_mix_out.ap()[0:128, :])
        b.dma(mixh[:, 16:32], cc_mix_out.ap()[128:256, :])
        b.vop("tensor_scalar", mixh[:, 8:16], mixh[:, 8:16], isodd[:, 0:1], None,
              ALU.mult, waits=[("dma", b.val("dma"))])
        b.vch("tensor_scalar", mixh[:, 16:24], mixh[:, 16:24], iseven[:, 0:1], None,
              ALU.mult)
        for k in range(8):
            b.vch("tensor_copy", Aap[:, k * 2050:k * 2050 + 1], mixh[:, 8 + k:9 + k])
            b.vch("tensor_copy", Aap[:, k * 2050 + 2049:k * 2050 + 2050],
                  mixh[:, 16 + k:17 + k])
        b.barrier()
        for c in range(NCH):
            x3c = A1f[:, 10240 + 0:10240 + 4096]
            for k in range(8):
                b.dma(A1f[:, 10240 + k * 512:10240 + (k + 1) * 512],
                      xres_A.ap()[k * 128:(k + 1) * 128, c * TC:(c + 1) * TC],
                      waits=[("vx", b.val("vx"))] if k == 0 else ())
            x3mk = b.val("dma")
            for k in range(8):
                base = k * 2050 + c * TC
                acc = stf[:, 2048:2560]
                wv = [("vx", b.val("vx")), ("mm", b.val("mm"))] if k == 0 else ()
                b.vop("tensor_scalar", acc, Aap[:, base:base + 512],
                      mixk[:, k * 3:k * 3 + 1], None, ALU.mult, waits=wv)
                b.vop("scalar_tensor_tensor", acc, Aap[:, base + 1:base + 513],
                      mixk[:, k * 3 + 1:k * 3 + 2], acc, ALU.mult, ALU.add)
                b.vop("scalar_tensor_tensor", acc, Aap[:, base + 2:base + 514],
                      mixk[:, k * 3 + 2:k * 3 + 3], acc, ALU.mult, ALU.add)
                b.vop("tensor_copy", Aap[:, 16400 + k * 512:16400 + (k + 1) * 512], acc)
            vx_xm = b.val("vx")
            xm = [Aap[:, 16400 + k * 512:16400 + (k + 1) * 512] for k in range(8)]
            for o in range(2):
                ps = P0 if o == 0 else P1
                for k in range(8):
                    wv = [("vx", vx_xm), ("sc", b.val("sc"))] if k == 0 else ()
                    b.mm(ps[:, 0:TC], A1[:, 18432 + (o * 8 + k) * 128:
                                         18432 + (o * 8 + k + 1) * 128],
                         xm[k], start=(k == 0), stop=(k == 7), waits=wv)
                b.act(shb[:, o * 512:(o + 1) * 512], ps[:, 0:TC], AF.Tanh,
                      bias=bo1[:, o:o + 1], waits=[("mm", b.val("mm"))])
            scv = b.val("sc")
            for tb in range(4):
                wtb = [("vx", b.val("vx"))]
                if tb == 0:
                    wtb.append(("sc", scv))
                for mc in range(2):
                    b.mm(P3[:, 0:9], shb[:, mc * 512 + tb * 128:mc * 512 + (tb + 1) * 128],
                         wo2b[:, mc * 9:(mc + 1) * 9], start=(mc == 0), stop=(mc == 1),
                         waits=wtb if mc == 0 else ())
                wlg = [("mm", b.val("mm"))]
                if tb == 0:
                    wlg.append(("dma", b.val("dma")))
                b.vch("tensor_tensor", lgt[:, 0:9], P3[:, 0:9], bo2rep, ALU.add,
                      waits=wlg)
                b.vch("reduce_max", m1t, lgt[:, 0:9], AX.X)
                vv = b.val("vx")
                b.vch("tensor_scalar", msk1[:, 0:9], lgt[:, 0:9], m1t[:, 0:1], None,
                      ALU.is_equal, waits=[("vx", vv)])
                b.vch("tensor_mul", candt[:, 0:9], msk1[:, 0:9], iota9m)
                b.vch("tensor_scalar", candt[:, 0:9], candt[:, 0:9], 9.0, None, ALU.add)
                b.vch("tensor_reduce", i1t, candt[:, 0:9], AX.X, ALU.min)
                vv = b.val("vx")
                b.vch("tensor_scalar", msk1[:, 0:9], iota9, i1t[:, 0:1], None,
                      ALU.is_equal, waits=[("vx", vv)])
                vv = b.val("vx")
                b.mm(P3[0:9, 128:256], msk1[:, 0:9], identf, start=True, stop=True,
                     waits=[("vx", vv)])
                b.vch("tensor_copy", mask9_sb[0:9, tb * 128:(tb + 1) * 128],
                      P3[0:9, 128:256], waits=[("mm", b.val("mm"))])
            b.vch("tensor_copy", mask9b[0:9, 0:TC], mask9_sb[0:9, 0:TC],
                  waits=[("mm", b.val("mm"))])
            b.fence_v(mask9_sb[0:9, 0:1])
            b.fence_v(mask9_sb[0:9, TC - 1:TC])
            b.dma(mask9_dram.ap()[c * 9:(c + 1) * 9, :], mask9_sb[0:9, 0:TC],
                  waits=[("vx", b.val("vx"))])
            b.barrier()
            lw_tab = {0: b.dma(WP[:, 0:2048], W1_p[:, 0:2048],
                               waits=[("mm", b.val("mm"))], q="gpsimd")}
            for p in range(9):
                if p + 1 < 9:
                    lw_tab[p + 1] = b.dma(
                        WP[:, ((p + 1) % 2) * 2048:((p + 1) % 2) * 2048 + 2048],
                        W1_p[:, (p + 1) * 2048:(p + 2) * 2048],
                        waits=[("mm", b.val("mm"))], q="gpsimd")
                lw = lw_tab[p]
                lmk = b.dma(stf[:, 1536:2048],
                            mask9_dram.ap()[c * 9 + p:c * 9 + p + 1, :]
                            .partition_broadcast(128), waits=[("vx", b.val("vx"))])
                halfp = (p % 2) * 2048
                for oc in range(2):
                    ps = P0 if oc == 0 else P1
                    for k in range(8):
                        wv = ()
                        if k == 0:
                            if p == 0:
                                wv = [("gdma", lw), ("vx", vx_xm),
                                      ("sc", b.val("sc"))]
                            else:
                                wv = [("gdma", lw), ("vx", b.val("vx"))]
                        b.mm(ps[:, 0:TC],
                             WP[:, halfp + (oc * 8 + k) * 128:halfp + (oc * 8 + k + 1) * 128],
                             xm[k], start=(k == 0), stop=(k == 7), waits=wv)
                    b.vop("tensor_scalar", stf[:, 1024:1536],
                          ps[:, 0:TC], b1p[:, p * 2 + oc:p * 2 + oc + 1], None, ALU.add,
                          waits=[("mm", b.val("mm")), ("dma", lmk)])
                    b.vop("tensor_tensor",
                          Aap[:, 20992 + p * 1024 + oc * 512:20992 + p * 1024 + (oc + 1) * 512],
                          stf[:, 1024:1536],
                          stf[:, 1536:2048], ALU.mult)
            vx_paths = b.val("vx")
            for dc in range(8):
                ps = P0 if dc % 2 == 0 else P1
                first = True
                for p in range(9):
                    for mc in range(2):
                        wv = [("vx", b.val("vx") if dc >= 2 else vx_paths)] \
                            if first else ()
                        b.mm(ps[:, 0:TC],
                             A1[:, ((p * 2 + mc) * 8 + dc) * 128:
                                ((p * 2 + mc) * 8 + dc + 1) * 128],
                             Aap[:, 20992 + p * 1024 + mc * 512:
                                 20992 + p * 1024 + (mc + 1) * 512],
                             start=first, stop=False, waits=wv)
                        first = False
                b.mm(ps[:, 0:TC], b2bf[:, dc * 128:(dc + 1) * 128], mask9b[0:9, 0:TC],
                     start=False, stop=True)
                wcol = [("mm", b.val("mm"))]
                if dc == 0:
                    wcol.append(("gdma", b.val("gdma")))
                b.vop("tensor_add", xsf[:, dc * 512:(dc + 1) * 512], ps[:, 0:TC],
                      Aap[:, dc * 2050 + 1 + c * TC:dc * 2050 + 1 + (c + 1) * TC],
                      waits=wcol)
            if DEBUG and c == 0:
                for k in range(8):
                    b.dma(dbg["dbg_xm"][k * 128:(k + 1) * 128, 0:512],
                          Aap[:, 16400 + k * 512:16400 + (k + 1) * 512])
                b.barrier()
            tiles = [xsf[:, o * 512:(o + 1) * 512] for o in range(8)]
            m_rep, r_rep = ln_stats(tiles, TC, w0=True)
            vv = b.val("vx")
            for o in range(8):
                wsub = [("vx", vv), ("dma", x3mk)] if o == 0 else ()
                b.vop("tensor_sub", tiles[o], tiles[o], m_rep, waits=wsub)
                b.vch("tensor_mul", tiles[o], tiles[o], r_rep)
                b.vch("tensor_scalar", tiles[o], tiles[o], l1fg[:, o:o + 1],
                      l1fb[:, o:o + 1], ALU.mult, ALU.add)
                b.vch("tensor_add", tiles[o], tiles[o],
                      A1f[:, 10240 + o * 512:10240 + (o + 1) * 512])
                b.fence_v(tiles[o][:, 511:512])
                b.dma(out_fm[o * 128:(o + 1) * 128, c * TC:(c + 1) * TC], tiles[o],
                      waits=[("vx", b.val("vx"))], q="gpsimd")
        b.barrier()
        if DEBUG:
            b.dma(dbg["dbg_m9"], mask9_dram.ap())
            b.barrier()
    else:
        b.dma(out_fm, xres_A.ap() if PHASE_LIMIT >= 3 else x_ext[:, 3:3 + T], q="gpsimd")

    b.barrier()

    with nc.Block() as block:
        @block.sync
        def _(e):
            for f in b.q["sync"]:
                f(e)

        @block.gpsimd
        def _(e):
            for f in b.q["gpsimd"]:
                f(e)

        @block.tensor
        def _(e):
            for f in b.q["tensor"]:
                f(e)

        @block.scalar
        def _(e):
            for f in b.q["scalar"]:
                f(e)

        @block.vector
        def _(e):
            for f in b.q["vector"]:
                f(e)

    stack.close()
    return nc



def kernel(**inputs):
    maps = prep_inputs(inputs)
    nc = bass.Bass("TRN2", target_bir_lowering=False, debug=False)
    build(nc)
    res = run_bass_kernel_spmd(nc, maps, core_ids=list(range(8)))
    out = np.empty((4, 4096, 1024), np.float32)
    for c in range(8):
        b_, h = c // 2, c % 2
        out[b_, h * T:(h + 1) * T, :] = res.results[c]["out_fm"].T
    return out


# version 30
# speedup vs baseline: 1.0039x; 1.0039x over previous
import numpy as np
import concourse.bass as bass
import concourse.mybir as mybir
from concourse.bass_utils import run_bass_kernel_spmd
from contextlib import ExitStack

F32 = mybir.dt.float32
BF16 = mybir.dt.bfloat16
I32 = mybir.dt.int32
AF = mybir.ActivationFunctionType
ALU = mybir.AluOpType
AX = mybir.AxisListType

T = 2048
TC = 512
TC4 = 256
NCH = T // TC
NCH4 = T // TC4
NEG_BIG = -1e30

PHASE_LIMIT = 99
DEBUG = False
SCAN_N = 16
P4STAGE = 99
P5STAGE = 99
TTADD = True
NHCORR = 16
CONSTLHS = False
CONSTRHS = False
TTMODE = 0
FWD_MM = True
BACK_MM = True
PAIRS = [[0, 1], [2, 3], [4, 5], [6, 7]]


def _pack(w, cs=128):
    Din, Dout = w.shape
    return np.ascontiguousarray(
        w.reshape(Din // 128, 128, Dout // cs, cs).transpose(1, 2, 0, 3).reshape(128, -1))


def _packv(v, P=128):
    n = v.shape[0]
    return np.ascontiguousarray(v.reshape(n // P, P).T)


def prep_inputs(I):
    f32 = np.float32
    x = np.asarray(I["L0_attn_x"], f32)
    S = {}
    S["W_in_p"] = _pack(np.ascontiguousarray(I["L0_attn_W_in"].T))
    wxt = np.ascontiguousarray(I["L0_attn_W_x"].T)
    S["W_x_p"] = _pack(wxt[:, 0:2048])
    S["W_xbc_p"] = _pack(wxt[:, 2048:2080], cs=32)
    S["W_dt_p"] = _pack(np.ascontiguousarray(I["L0_attn_W_dt"].T))
    S["W_out_p"] = _pack(np.ascontiguousarray(I["L0_attn_W_out"].T))
    cw = I["L0_attn_conv_w"].reshape(2048, 4)
    S["convw_sb"] = np.ascontiguousarray(cw.reshape(16, 128, 4).transpose(1, 0, 2).reshape(128, 64))
    S["convb_sb"] = _packv(I["L0_attn_conv_b"])
    S["bdt_sb"] = _packv(I["L0_attn_b_dt"])
    S["D_sb"] = _packv(I["L0_attn_D"])
    S["l0a_g"] = _packv(I["L0_attn_ln_g"]); S["l0a_b"] = _packv(I["L0_attn_ln_b"])
    S["enc_p"] = _pack(np.asarray(I["L0_ffn_enc_w"], f32))
    S["enc_b_sb"] = _packv(I["L0_ffn_enc_b"])
    S["s2c_p"] = _pack(np.asarray(I["L0_ffn_s2c_w"], f32), cs=64)
    S["s2c_b_sb"] = I["L0_ffn_s2c_b"].reshape(64, 1).astype(f32)
    S["r_w1"] = np.asarray(I["L0_ffn_r_w1"], f32)
    S["r_b1_sb"] = I["L0_ffn_r_b1"].reshape(64, 1).astype(f32)
    S["r_w2"] = np.asarray(I["L0_ffn_r_w2"], f32)
    S["r_b2_sb"] = I["L0_ffn_r_b2"].reshape(8, 1).astype(f32)
    S["e_w1_p"] = np.ascontiguousarray(
        I["L0_ffn_e_w1"].reshape(8, 64, 8, 128).transpose(1, 0, 2, 3).reshape(64, 8192))
    S["e_b1p"] = np.ascontiguousarray(
        I["L0_ffn_e_b1"].reshape(8, 8, 128).transpose(2, 0, 1).reshape(128, 64))
    S["e_w2_p"] = np.ascontiguousarray(
        I["L0_ffn_e_w2"].reshape(8, 8, 128, 64).transpose(2, 0, 1, 3).reshape(128, 4096))
    S["e_b2p"] = np.ascontiguousarray(I["L0_ffn_e_b2"].T)
    S["c2s_p"] = np.ascontiguousarray(
        I["L0_ffn_c2s_w"].reshape(64, 16, 128).transpose(0, 1, 2).reshape(64, 2048))
    S["c2s_b_sb"] = _packv(I["L0_ffn_c2s_b"])
    S["dec_p"] = _pack(np.asarray(I["L0_ffn_dec_w"], f32))
    S["dec_b_sb"] = _packv(I["L0_ffn_dec_b"])
    S["l0f_g"] = _packv(I["L0_ffn_ln_g"]); S["l0f_b"] = _packv(I["L0_ffn_ln_b"])
    S["Wq_p"] = _pack(np.asarray(I["L1_attn_Wq"], f32))
    S["Wk_p"] = _pack(np.asarray(I["L1_attn_Wk"], f32))
    S["Wv_p"] = np.ascontiguousarray(
        I["L1_attn_Wv"].reshape(8, 128, 1024).transpose(1, 0, 2).reshape(128, 8192))
    S["Wo_p"] = _pack(np.asarray(I["L1_attn_Wo"], f32))
    S["bo_sb"] = _packv(I["L1_attn_bo"])
    mk = I["L1_ffn_mix_kernel"]
    S["mixk_sb"] = np.ascontiguousarray(mk.T.reshape(8, 128, 3).transpose(1, 0, 2).reshape(128, 24))
    S["W1_p"] = np.ascontiguousarray(
        I["L1_ffn_W1"].reshape(9, 8, 128, 2, 128).transpose(2, 0, 3, 1, 4).reshape(128, 18432))
    S["b1p"] = np.ascontiguousarray(
        I["L1_ffn_b1"].reshape(9, 2, 128).transpose(2, 0, 1).reshape(128, 18))
    S["W2_p"] = np.ascontiguousarray(
        I["L1_ffn_W2"].reshape(9, 2, 128, 8, 128).transpose(2, 0, 1, 3, 4).reshape(128, 18432))
    S["b2f"] = np.asarray(I["L1_ffn_b2"], f32)
    S["Wo1_p"] = _pack(np.asarray(I["L1_ffn_Wo1"], f32))
    S["bo1_sb"] = _packv(I["L1_ffn_bo1"])
    S["Wo2_p"] = np.ascontiguousarray(
        I["L1_ffn_Wo2"].reshape(2, 128, 9).transpose(1, 0, 2).reshape(128, 18))
    S["bo2_sb"] = I["L1_ffn_bo2"].reshape(9, 1).astype(f32)
    S["l1f_g"] = _packv(I["L1_ffn_gamma"]); S["l1f_b"] = _packv(I["L1_ffn_beta"])
    S["ident"] = np.eye(128, dtype=f32)
    tj = np.arange(128)[:, None]; ti = np.arange(128)[None, :]
    S["cmask"] = (tj <= ti).astype(f32)
    S["ones_row"] = np.ones((1, 128), f32)
    S["inv1024"] = np.full((128, 1), 1.0 / 1024.0, f32)
    S["iota8"] = np.broadcast_to(np.arange(8, dtype=f32), (128, 8)).copy()
    S["iota8m"] = S["iota8"] - 8.0
    S["iota9"] = np.broadcast_to(np.arange(9, dtype=f32), (128, 9)).copy()
    S["rb2_rep"] = np.broadcast_to(I["L0_ffn_r_b2"].astype(f32), (128, 8)).copy()
    S["bo2_rep"] = np.broadcast_to(I["L1_ffn_bo2"].astype(f32), (128, 9)).copy()
    S["iota9m"] = S["iota9"] - 9.0

    maps = []
    p_ = np.arange(128)
    for c in range(8):
        b, h = c // 2, c % 2
        m = dict(S)
        xT = np.ascontiguousarray(x[b].T)
        sl = xT[:, h * T:(h + 1) * T]
        halo = xT[:, h * T - 3:h * T] if h == 1 else np.zeros((1024, 3), f32)
        m["x_ext"] = np.ascontiguousarray(np.concatenate([halo, sl], axis=1))
        m["is_odd"] = np.full((128, 1), float(h), f32)
        m["is_even"] = np.full((128, 1), float(1 - h), f32)
        m["odd64"] = np.full((64, 1), float(h), f32)
        al = I["L0_attn_A_log"]
        m["A_pack"] = np.ascontiguousarray(
            al[1024 * h:1024 * (h + 1)].reshape(8, 128, 16).transpose(1, 0, 2).reshape(128, 128))
        gdu = np.empty((128, 32), np.int32)
        for q in range(2):
            for k in range(8):
                gdu[:, q * 16 + k] = q * 2048 + h * 1024 + k * 128 + p_
                gdu[:, q * 16 + 8 + k] = q * 2080 + h * 1024 + k * 128 + p_
        m["gdu_idx"] = gdu
        gy = np.empty((128, 64), np.int32)
        for k in range(16):
            for cc in range(4):
                gy[:, k * 4 + cc] = (2048 * h + 128 * k + p_) * 4 + cc
        m["gy_idx"] = gy
        maps.append(m)
    return maps


class B:
    ENGINES = ("sync", "gpsimd", "tensor", "scalar", "vector")

    def __init__(self, nc, stack):
        self.nc = nc
        self.q = {e: [] for e in self.ENGINES}
        self.sem = {}
        for s in ("dma", "gdma", "mm", "sc", "vx", "cc", "st", "gp"):
            self.sem[s] = [stack.enter_context(nc.semaphore(s)), 0]

    def val(self, s):
        return self.sem[s][1]

    def emit(self, eng, fn, waits=(), inc=None):
        tgt = None
        if inc is not None:
            self.sem[inc[0]][1] += inc[1]
            tgt = self.sem[inc[0]][1]
        sems = self.sem

        def op(e, fn=fn, waits=tuple(waits), inc=inc):
            for s, v in waits:
                if v > 0:
                    e.wait_ge(sems[s][0], v)
            r = fn(e)
            if inc is not None:
                r.then_inc(sems[inc[0]][0], inc[1])
        self.q[eng].append(op)
        return tgt

    def barrier(self):
        snap = {s: c for s, (hh, c) in self.sem.items()}
        sems = self.sem
        for e in self.ENGINES:
            def bop(eng, snap=dict(snap)):
                for s, v in snap.items():
                    if v > 0:
                        eng.wait_ge(sems[s][0], v)
            self.q[e].append(bop)

    def dma(self, out, in_, waits=(), q="sync"):
        s = "dma" if q == "sync" else "gdma"
        return self.emit(q, lambda e, o=out, i=in_: e.dma_start(out=o, in_=i),
                         waits=waits, inc=(s, 16))

    def gather(self, out, in_, idx, waits=()):
        return self.emit(
            "gpsimd",
            lambda e, o=out, i=in_, x=idx: e.indirect_dma_start(
                out=o, out_offset=None, in_=i,
                in_offset=bass.IndirectOffsetOnAxis(ap=x, axis=0)),
            waits=waits, inc=("gdma", 16))

    def mm(self, out, lhsT, rhs, start, stop, waits=(), inc=True, skip_gc=False):
        return self.emit(
            "tensor",
            lambda e, o=out, l=lhsT, r=rhs, s1=start, s2=stop, sg=skip_gc:
                e.matmul(o, l, r, start=s1, stop=s2, skip_group_check=sg),
            waits=waits, inc=("mm", 1) if inc else None)

    def tr(self, out, in_, ident, waits=()):
        r = self.emit("tensor",
                      lambda e, o=out, i=in_, d=ident: e.transpose(o, i, d),
                      waits=waits, inc=("mm", 1))
        self.emit("tensor",
                  lambda e: e.matmul(self.drain_ps, self.drain_lhs, self.drain_rhs,
                                     start=True, stop=True, skip_group_check=True),
                  inc=("mm", 1))
        return r

    def act(self, out, in_, func, bias=0.0, scale=1.0, waits=()):
        return self.emit(
            "scalar",
            lambda e, o=out, i=in_, f=func, bb=bias, s=scale:
                e.activation(out=o, in_=i, func=f, bias=bb, scale=s),
            waits=waits, inc=("sc", 1))

    def fence_v(self, srcap):
        return self.vop("tensor_copy", self.ftile[0:srcap.shape[0], 0:1], srcap,
                        waits=[("vx", self.val("vx"))])

    def fence_s(self, srcap):
        return self.emit(
            "scalar",
            lambda e, o=self.ftile[0:srcap.shape[0], 1:2], i=srcap:
                e.activation(out=o, in_=i, func=AF.Copy),
            inc=("sc", 1))

    def vch(self, name, *args, waits=(), **kw):
        return self.vop(name, *args, waits=tuple(waits) + (("vx", self.val("vx")),), **kw)

    def vop(self, name, *args, waits=(), **kw):
        return self.emit(
            "vector",
            lambda e, n=name, a=args, k=kw: getattr(e, n)(*a, **k),
            waits=waits, inc=("vx", 1))


def build(nc):
    stack = ExitStack()
    b = B(nc, stack)
    dt = nc.dram_tensor
    EI = lambda n, s, d=F32: dt(n, s, d, kind="ExternalInput").ap()

    x_ext = EI("x_ext", [1024, 2051])
    W_in_p = EI("W_in_p", [128, 32768]); W_x_p = EI("W_x_p", [128, 32768])
    W_xbc_p = EI("W_xbc_p", [128, 512]); W_dt_p = EI("W_dt_p", [128, 32768])
    W_out_p = EI("W_out_p", [128, 16384])
    convw_i = EI("convw_sb", [128, 64]); convb_i = EI("convb_sb", [128, 16])
    bdt_i = EI("bdt_sb", [128, 16]); A_pack = EI("A_pack", [128, 128])
    D_i = EI("D_sb", [128, 16]); l0ag_i = EI("l0a_g", [128, 8]); l0ab_i = EI("l0a_b", [128, 8])
    enc_p = EI("enc_p", [128, 16384]); encb_i = EI("enc_b_sb", [128, 16])
    s2c_p = EI("s2c_p", [128, 1024]); s2cb_i = EI("s2c_b_sb", [64, 1])
    rw1_i = EI("r_w1", [64, 64]); rb1_i = EI("r_b1_sb", [64, 1])
    rw2_i = EI("r_w2", [64, 8]); rb2_i = EI("r_b2_sb", [8, 1])
    ew1_p = EI("e_w1_p", [64, 8192]); eb1_i = EI("e_b1p", [128, 64])
    ew2_p = EI("e_w2_p", [128, 4096]); eb2_i = EI("e_b2p", [64, 8])
    c2s_p = EI("c2s_p", [64, 2048]); c2sb_i = EI("c2s_b_sb", [128, 16])
    dec_p = EI("dec_p", [128, 16384]); decb_i = EI("dec_b_sb", [128, 8])
    l0fg_i = EI("l0f_g", [128, 8]); l0fb_i = EI("l0f_b", [128, 8])
    Wq_p = EI("Wq_p", [128, 8192]); Wk_p = EI("Wk_p", [128, 8192])
    Wv_p = EI("Wv_p", [128, 8192]); Wo_p = EI("Wo_p", [128, 8192])
    bo_i = EI("bo_sb", [128, 8])
    mixk_i = EI("mixk_sb", [128, 24])
    W1_p = EI("W1_p", [128, 18432]); b1p_i = EI("b1p", [128, 18])
    W2_p = EI("W2_p", [128, 18432]); b2f = EI("b2f", [9, 1024])
    Wo1_p = EI("Wo1_p", [128, 2048]); bo1_i = EI("bo1_sb", [128, 2])
    Wo2_p = EI("Wo2_p", [128, 18]); bo2_i = EI("bo2_sb", [9, 1])
    l1fg_i = EI("l1f_g", [128, 8]); l1fb_i = EI("l1f_b", [128, 8])
    ident_i = EI("ident", [128, 128]); cmask_i = EI("cmask", [128, 128])
    onesr_i = EI("ones_row", [1, 128]); inv1024_i = EI("inv1024", [128, 1])
    iota8_i = EI("iota8", [128, 8]); iota8m_i = EI("iota8m", [128, 8])
    iota9_i = EI("iota9", [128, 9]); iota9m_i = EI("iota9m", [128, 9])
    isodd_i = EI("is_odd", [128, 1]); iseven_i = EI("is_even", [128, 1])
    odd64_i = EI("odd64", [64, 1])
    rb2r_i = EI("rb2_rep", [128, 8]); bo2r_i = EI("bo2_rep", [128, 9])
    gdu_i = dt("gdu_idx", [128, 32], I32, kind="ExternalInput").ap()
    gy_i = dt("gy_idx", [128, 64], I32, kind="ExternalInput").ap()

    out_fm = dt("out_fm", [1024, T], F32, kind="ExternalOutput").ap()

    cc_pre_in = dt("cc_pre_in", [4128, T], BF16, kind="Internal")
    cc_pre_outD = dt("cc_pre_outD", [4096, T], BF16, kind="Internal")
    cc_pre_outU = dt("cc_pre_outU", [4160, T], BF16, kind="Internal")
    cc_y_in = dt("cc_y_in", [2048, 2048], BF16, kind="Internal")
    cc_y_out = dt("cc_y_out", [4096, 2048], BF16, kind="Internal")
    cc_att_in = dt("cc_att_in", [64, 1040], F32, kind="Internal")
    cc_att_out = dt("cc_att_out", [128, 1040], F32, kind="Internal")
    cc_mix_in = dt("cc_mix_in", [128, 16], BF16, kind="Internal")
    cc_mix_out = dt("cc_mix_out", [256, 16], BF16, kind="Internal")
    z_dram = dt("z_dram", [2048, T], BF16, kind="Internal")
    ln_dram = dt("ln_dram", [1024, T], BF16, kind="Internal")
    xres_A = dt("xres_A", [1024, T], F32, kind="Internal")
    xres_B = dt("xres_B", [1024, T], F32, kind="Internal")
    comb_dram = dt("comb_dram", [64, TC4], F32, kind="Internal")
    mask9_dram = dt("mask9_dram", [36, TC], F32, kind="Internal")
    q_dram = dt("q_dram", [1024, T], BF16, kind="Internal")
    k_dram = dt("k_dram", [1024, T], BF16, kind="Internal")
    v_dram = dt("v_dram", [T, 1024], BF16, kind="Internal")
    num_dram = dt("num_dram", [1024, T], F32, kind="Internal")
    den_dram = dt("den_dram", [16, T], F32, kind="Internal")
    attn_dram = dt("attn_dram", [1024, T], BF16, kind="Internal")

    dbg = {}
    if DEBUG:
        for nm, sh, ddt in [("dbg_pre", [8256, T], BF16), ("dbg_y", [2048, 4096], BF16),
                            ("dbg_x1", [1024, T], F32), ("dbg_x2", [1024, T], F32),
                            ("dbg_x3", [1024, T], F32), ("dbg_z", [2048, T], BF16),
                            ("dbg_att", [1024, T], BF16),
                            ("dbg_xsh", [2048, 3], BF16), ("dbg_lnh", [1024, 3], BF16),
                            ("dbg_yin", [1024, 4096], BF16), ("dbg_comb", [64, TC4], F32), ("dbg_m9", [36, TC], F32), ("dbg_xm", [1024, TC], BF16),
                            ("dbg_lgt", [128, 8], F32), ("dbg_cmb", [128, 8], F32),
                            ("dbg_lgr", [8, TC4], F32), ("dbg_rh", [64, TC4], F32),
                            ("dbg_cont", [64, TC4], F32)]:
            dbg[nm] = dt(nm, sh, ddt, kind="ExternalOutput").ap()

    sb = lambda n, s, d: stack.enter_context(nc.sbuf_tensor(n, s, d))
    bigA_t = sb("s_bigA", [128, 32768], BF16)
    big1_t = sb("s_big1", [128, 32768], BF16)
    xs_t = sb("s_xs_c", [128, 8240], BF16)
    wp_t = sb("s_wping", [128, 4096], BF16)
    st_t = sb("s_stag", [128, 8192], BF16)
    A_sb = sb("s_A_sb", [128, 128], F32).ap()
    carry = sb("s_carry", [128, 128], F32).ap()
    convw = sb("s_convw", [128, 64], F32).ap(); convb = sb("s_convb", [128, 16], F32).ap()
    bdt = sb("s_bdt", [128, 16], F32).ap(); D_s = sb("s_D_s", [128, 16], F32).ap()
    l0ag = sb("s_l0ag", [128, 8], F32).ap(); l0ab = sb("s_l0ab", [128, 8], F32).ap()
    encb = sb("s_encb", [128, 16], F32).ap(); s2cb = sb("s_s2cb", [64, 1], F32).ap()
    rw1 = sb("s_rw1", [64, 64], BF16).ap(); rb1 = sb("s_rb1", [64, 1], F32).ap()
    rw2 = sb("s_rw2", [64, 8], BF16).ap(); rb2 = sb("s_rb2", [8, 1], F32).ap()
    eb1 = sb("s_eb1", [128, 64], F32).ap(); eb2 = sb("s_eb2", [64, 8], F32).ap()
    c2sb = sb("s_c2sb", [128, 16], F32).ap(); decb = sb("s_decb", [128, 8], F32).ap()
    l0fg = sb("s_l0fg", [128, 8], F32).ap(); l0fb = sb("s_l0fb", [128, 8], F32).ap()
    bo_s = sb("s_bo_s", [128, 8], F32).ap(); mixk = sb("s_mixk", [128, 24], F32).ap()
    b1p = sb("s_b1p_s", [128, 18], F32).ap(); bo1 = sb("s_bo1", [128, 2], F32).ap()
    bo2 = sb("s_bo2", [9, 1], F32).ap()
    l1fg = sb("s_l1fg", [128, 8], F32).ap(); l1fb = sb("s_l1fb", [128, 8], F32).ap()
    b2bf = sb("s_b2bf", [9, 1024], BF16).ap()
    b.ftile = sb("s_ftile", [128, 2], F32).ap()
    rb2rep = sb("s_rb2rep", [128, 8], F32).ap()
    bo2rep = sb("s_bo2rep", [128, 9], F32).ap()
    wo2b = sb("s_wo2b", [128, 18], BF16).ap()
    identf = sb("s_identf", [128, 128], F32).ap(); identb = sb("s_identb", [128, 128], BF16).ap()
    cmask = sb("s_cmask", [128, 128], F32).ap()
    onesr = sb("s_onesr", [1, 128], F32).ap(); inv1024 = sb("s_inv1024", [128, 1], F32).ap()
    iota8 = sb("s_iota8", [128, 8], F32).ap(); iota8m = sb("s_iota8m", [128, 8], F32).ap()
    iota9 = sb("s_iota9", [128, 9], F32).ap(); iota9m = sb("s_iota9m", [128, 9], F32).ap()
    isodd = sb("s_isodd", [128, 1], F32).ap(); iseven = sb("s_iseven", [128, 1], F32).ap()
    odd64 = sb("s_odd64", [64, 1], F32).ap()
    gdu_sb = sb("s_gdu_sb", [128, 32], I32).ap()
    gy_sb = sb("s_gy_sb", [128, 64], I32).ap()
    mean_r = sb("s_mean_r", [1, TC], F32).ap(); msq_r = sb("s_msq_r", [1, TC], F32).ap()
    tmp_r = sb("s_tmp_r", [1, TC], F32).ap(); rstd_r = sb("s_rstd_r", [1, TC], F32).ap()
    vaug = sb("s_vaug", [128, 130], BF16).ap()
    ktok = sb("s_ktok", [128, 64], BF16).ap()
    amb = sb("s_am_bf", [128, 128], BF16).ap()
    S_f = sb("s_S_sb", [64, 65], F32).ap()
    saug = sb("s_saug", [64, 65], BF16).ap()
    sprev_f = sb("s_sprev_f", [64, 65], F32).ap()
    sprev = sb("s_sprev", [64, 65], BF16).ap()
    sprev64 = sb("s_sprev64", [64, 64], BF16).ap()
    mix_sb = sb("s_mix_sb", [128, 16], BF16).ap()
    vaug2 = sb("s_vaug2", [128, 130], BF16).ap()
    ktok2 = sb("s_ktok2", [128, 64], BF16).ap()
    amb2 = sb("s_amb2", [128, 128], BF16).ap()
    S_f2 = sb("s_S_f2", [64, 65], F32).ap()
    saug2 = sb("s_saug2", [64, 65], BF16).ap()
    onesb = sb("s_onesb", [1, 128], BF16).ap()
    mixh = sb("s_mixh", [128, 32], BF16).ap()
    spk1 = sb("s_spk1", [64, 1], BF16).ap()
    den_s = sb("s_den_s", [1, T], F32).ap()
    lgt = sb("s_lgt_t", [128, 16], F32).ap()
    m1t = sb("s_m1_t", [128, 1], F32).ap(); m2t = sb("s_m2_t", [128, 1], F32).ap()
    i1t = sb("s_i1_t", [128, 1], F32).ap()
    candt = sb("s_cand_t", [128, 16], F32).ap()
    msk1 = sb("s_msk1", [128, 16], F32).ap(); msk2 = sb("s_msk2", [128, 16], F32).ap()
    w1t = sb("s_w1_t", [128, 1], F32).ap(); w2t = sb("s_w2_t", [128, 1], F32).ap()
    logit_sb = sb("s_logit_sb", [9, TC], F32).ap()
    comb_sb = sb("s_comb_sb", [8, TC4], F32).ap()
    mask9_sb = sb("s_mask9_sb", [9, TC], F32).ap()
    mask9b = sb("s_mask9_bf", [9, TC], BF16).ap()
    cont_bf = sb("s_cont_bf", [64, TC4], BF16).ap()
    rh_bf = sb("s_rh_bf", [64, TC4], BF16).ap()
    moe_f = sb("s_moe_f", [64, TC4], F32).ap()
    eo_f = sb("s_eo_f", [64, TC4], F32).ap()
    moe_bf = sb("s_moe_bf", [64, TC4], BF16).ap()
    comb_rep = sb("s_comb_rep", [64, TC4], F32).ap()
    shb = sb("s_sh_bf", [128, 1024], BF16).ap()

    Aap = bigA_t.ap(); A1 = big1_t.ap(); XS = xs_t.ap(); WP = wp_t.ap(); ST = st_t.ap()
    Af = Aap.bitcast(F32); A1f = A1.bitcast(F32); big1f = A1f
    xsf = XS.bitcast(F32); wpf = WP.bitcast(F32); stf = ST.bitcast(F32)
    den_s2 = wpf[0:1, 0:T]

    pt = lambda n, s, d: stack.enter_context(nc.psum_tensor(n, s, d))
    P0 = pt("P0", [128, 512], F32).ap()
    P1 = pt("P1", [128, 512], F32).ap()
    P2 = pt("P2", [128, 512], F32).ap()
    P3 = pt("P3", [128, 512], F32).ap()
    P4 = pt("P4", [128, 512], F32).ap()
    PT = pt("PTb", [128, 1024], BF16).ap()

    b.drain_ps = P4[0:1, 384:512]
    b.drain_lhs = inv1024
    b.drain_rhs = identf
    for dst, src in [(A_sb, A_pack), (convw, convw_i), (convb, convb_i),
                     (bdt, bdt_i), (D_s, D_i), (l0ag, l0ag_i), (l0ab, l0ab_i),
                     (encb, encb_i), (s2cb, s2cb_i), (rb1, rb1_i), (rb2, rb2_i),
                     (eb1, eb1_i), (eb2, eb2_i), (c2sb, c2sb_i), (decb, decb_i),
                     (l0fg, l0fg_i), (l0fb, l0fb_i), (bo_s, bo_i), (mixk, mixk_i),
                     (b1p, b1p_i), (bo1, bo1_i), (bo2, bo2_i), (l1fg, l1fg_i),
                     (l1fb, l1fb_i), (identf, ident_i), (cmask, cmask_i),
                     (onesr, onesr_i), (inv1024, inv1024_i), (iota8, iota8_i),
                     (iota8m, iota8m_i), (iota9, iota9_i), (iota9m, iota9m_i),
                     (isodd, isodd_i), (iseven, iseven_i), (odd64, odd64_i),
                     (gdu_sb, gdu_i), (gy_sb, gy_i), (rb2rep, rb2r_i), (bo2rep, bo2r_i)]:
        b.dma(dst, src)
    b.dma(identb, ident_i, q="gpsimd")
    b.dma(b2bf, b2f, q="gpsimd")
    b.dma(rw1, rw1_i, q="gpsimd")
    b.dma(rw2, rw2_i, q="gpsimd")
    b.dma(onesb, onesr_i, q="gpsimd")
    w0 = b.val("dma")
    b.act(A_sb, A_sb, AF.Exp, waits=[("dma", w0)])
    b.emit("scalar", lambda e: e.mul(A_sb, A_sb, -1.0),
           waits=[("sc", b.val("sc"))], inc=("sc", 1))
    b.barrier()

    def ln_stats(tiles, W, w0=False, extra=()):
        vx0 = b.val("vx")
        mm0 = b.val("mm")
        sc0 = b.val("sc")
        for k in range(8):
            wmm = list(extra) + ([("vx", vx0), ("sc", sc0)] if w0 else []) \
                if k == 0 else ()
            b.mm(P4[0:1, 0:W], inv1024, tiles[k], start=(k == 0), stop=(k == 7),
                 waits=wmm)
        for k in range(8):
            sq = wpf[:, (k % 2) * 512:(k % 2) * 512 + W]
            wv = [("mm", b.val("mm"))] if k >= 2 else \
                (list(extra) + [("vx", vx0), ("mm", mm0)] if (w0 and k == 0) else ())
            b.act(sq, tiles[k], AF.Square, waits=wv)
            b.mm(P3[0:1, 0:W], inv1024, sq, start=(k == 0), stop=(k == 7),
                 waits=[("sc", b.val("sc"))], skip_gc=True)
        mmv = b.val("mm")
        b.vop("tensor_copy", mean_r[:, 0:W], P4[0:1, 0:W], waits=[("mm", mmv)])
        b.vop("tensor_copy", msq_r[:, 0:W], P3[0:1, 0:W])
        b.vch("tensor_mul", tmp_r[:, 0:W], mean_r[:, 0:W], mean_r[:, 0:W])
        b.vch("tensor_sub", tmp_r[:, 0:W], msq_r[:, 0:W], tmp_r[:, 0:W])
        b.vch("tensor_scalar", tmp_r[:, 0:W], tmp_r[:, 0:W], 1e-5, None, ALU.add)
        vv = b.val("vx")
        b.act(tmp_r[:, 0:W], tmp_r[:, 0:W], AF.Sqrt, waits=[("vx", vv)])
        b.vop("reciprocal", rstd_r[:, 0:W], tmp_r[:, 0:W], waits=[("sc", b.val("sc"))])
        vv = b.val("vx")
        b.mm(P2[:, 0:W], onesr, mean_r[:, 0:W], start=True, stop=True, waits=[("vx", vv)])
        b.vop("tensor_copy", stf[:, 0:W], P2[:, 0:W], waits=[("mm", b.val("mm"))])
        vv = b.val("vx")
        b.mm(P2[:, 0:W], onesr, rstd_r[:, 0:W], start=True, stop=True, waits=[("vx", vv)])
        b.vop("tensor_copy", stf[:, 512:512 + W], P2[:, 0:W], waits=[("mm", b.val("mm"))])
        return stf[:, 0:W], stf[:, 512:512 + W]

    if PHASE_LIMIT >= 1:
        chunksA = [(0, 3)] + [(3 + c * TC, 3 + (c + 1) * TC) for c in range(NCH)]

        def sa_buf(i, k, W):
            if i % 2 == 0:
                return xsf[:, k * 512:k * 512 + W]
            return Af[:, 8204 + k * 512:8204 + k * 512 + W]

        def sa_loads(i, w):
            a0, a1 = chunksA[i]
            for k in range(8):
                b.dma(sa_buf(i, k, a1 - a0), x_ext[k * 128:(k + 1) * 128, a0:a1],
                      waits=w if k == 0 else ())
            return b.val("dma")

        sa_mk = {0: sa_loads(0, ())}
        for ci, (a0, a1) in enumerate(chunksA):
            W = a1 - a0
            tiles = [sa_buf(ci, k, W) for k in range(8)]
            if ci + 1 < len(chunksA):
                sa_mk[ci + 1] = sa_loads(ci + 1, [("vx", b.val("vx")),
                                                  ("sc", b.val("sc")),
                                                  ("mm", b.val("mm"))])
            m_rep, r_rep = ln_stats(tiles, W, w0=True, extra=[("dma", sa_mk[ci])])
            vv0 = b.val("vx")
            for k in range(8):
                wsub = [("vx", vv0), ("dma", sa_mk[ci])] if k == 0 else ()
                b.vop("tensor_sub", tiles[k], tiles[k], m_rep, waits=wsub)
                b.vch("tensor_tensor", Aap[:, k * 2051 + a0:k * 2051 + a1],
                      tiles[k], r_rep, ALU.mult)
                if a0 >= 3:
                    b.fence_v(Aap[:, k * 2051 + a1 - 1:k * 2051 + a1])
                    b.dma(ln_dram.ap()[k * 128:(k + 1) * 128, a0 - 3:a1 - 3],
                          Aap[:, k * 2051 + a0:k * 2051 + a1], waits=[("vx", b.val("vx"))])
        b.barrier()

        for c in range(NCH):
            e0 = 3 + c * TC
            lw_tab = {0: b.dma(WP[:, 0:1024], W_in_p[:, 0:1024],
                               waits=[("mm", b.val("mm"))], q="gpsimd")}
            for o in range(32):
                half = (o % 2) * 2048
                if o + 1 < 32:
                    lw_tab[o + 1] = b.dma(
                        WP[:, ((o + 1) % 2) * 2048:((o + 1) % 2) * 2048 + 1024],
                        W_in_p[:, (o + 1) * 1024:(o + 2) * 1024],
                        waits=[("mm", b.val("mm"))], q="gpsimd")
                lw = lw_tab[o]
                ps = P0 if o % 2 == 0 else P1
                if o < 16 and c == 0:
                    for k in range(8):
                        wv0 = [("gdma", lw), ("vx", b.val("vx"))] if k == 0 else ()
                        b.mm(P2[:, 0:3], WP[:, half + k * 128:half + (k + 1) * 128],
                             Aap[:, k * 2051:k * 2051 + 3], start=(k == 0), stop=(k == 7),
                             waits=wv0)
                    b.vop("tensor_copy", XS[:, o * 515:o * 515 + 3], P2[:, 0:3],
                          waits=[("mm", b.val("mm"))])
                for k in range(8):
                    wv = []
                    if k == 0:
                        wv = [("gdma", lw), ("vx", b.val("vx")), ("sc", b.val("sc"))]
                    b.mm(ps[:, 0:TC], WP[:, half + k * 128:half + (k + 1) * 128],
                         Aap[:, k * 2051 + e0:k * 2051 + e0 + TC],
                         start=(k == 0), stop=(k == 7), waits=wv)
                mmv = b.val("mm")
                if o < 16:
                    b.vop("tensor_copy", XS[:, o * 515 + 3:o * 515 + 515], ps[:, 0:TC],
                          waits=[("mm", mmv)])
                else:
                    kz = o - 16
                    zst = ST[:, (kz % 2) * 512:(kz % 2) * 512 + 512]
                    b.act(zst, ps[:, 0:TC], AF.Silu, waits=[("mm", mmv), ("dma", b.val("dma"))])
                    b.fence_s(zst[:, 511:512])
                    b.dma(z_dram.ap()[kz * 128:(kz + 1) * 128, c * TC:(c + 1) * TC], zst,
                          waits=[("sc", b.val("sc"))])
            b.barrier()
            if DEBUG and c == 0:
                for k in range(16):
                    b.dma(dbg["dbg_xsh"][k * 128:(k + 1) * 128, :], XS[:, k * 515:k * 515 + 3])
                for k in range(8):
                    b.dma(dbg["dbg_lnh"][k * 128:(k + 1) * 128, :], Aap[:, k * 2051:k * 2051 + 3])
                b.barrier()
            for k in range(16):
                acc = stf[:, 2048 + (k % 2) * 512:2048 + (k % 2) * 512 + 512]
                base = k * 515
                wv = [("sc", b.val("sc"))] if k >= 2 else ()
                b.vop("tensor_scalar", acc, XS[:, base:base + 512],
                      convw[:, k * 4:k * 4 + 1], None, ALU.mult, waits=wv)
                for j in range(1, 4):
                    b.vop("scalar_tensor_tensor", acc, XS[:, base + j:base + j + 512],
                          convw[:, k * 4 + j:k * 4 + j + 1], acc, ALU.mult, ALU.add)
                b.act(A1[:, k * 2048 + c * TC:k * 2048 + (c + 1) * TC], acc, AF.Silu,
                      bias=convb[:, k:k + 1], waits=[("vx", b.val("vx"))])
                b.fence_s(A1[:, k * 2048 + (c + 1) * TC - 1:k * 2048 + (c + 1) * TC])
                b.dma(cc_pre_in.ap()[2048 + k * 128:2048 + (k + 1) * 128, c * TC:(c + 1) * TC],
                      A1[:, k * 2048 + c * TC:k * 2048 + (c + 1) * TC],
                      waits=[("sc", b.val("sc"))])
                if c < NCH - 1:
                    b.vop("tensor_copy", XS[:, base:base + 3], XS[:, base + 512:base + 515],
                          waits=[("sc", b.val("sc"))])
            b.barrier()

        def wx_src(o):
            return W_x_p[:, o * 2048:(o + 1) * 2048] if o < 16 else W_xbc_p[:, 0:512]
        lw_tab = {0: b.dma(WP[:, 0:2048], wx_src(0),
                           waits=[("mm", b.val("mm"))], q="gpsimd")}
        for o in range(17):
            cols = 128 if o < 16 else 32
            half = (o % 2) * 2048
            if o + 1 < 17:
                ncols = 128 if o + 1 < 16 else 32
                lw_tab[o + 1] = b.dma(
                    WP[:, ((o + 1) % 2) * 2048:((o + 1) % 2) * 2048 + 16 * ncols],
                    wx_src(o + 1), waits=[("mm", b.val("mm"))], q="gpsimd")
            lw = lw_tab[o]
            for c in range(NCH):
                ps = P0 if (o * NCH + c) % 2 == 0 else P1
                for k in range(16):
                    wv = [("gdma", lw), ("vx", b.val("vx"))] if k == 0 else ()
                    b.mm(ps[0:cols, 0:TC], WP[:, half + k * cols:half + (k + 1) * cols],
                         A1[:, k * 2048 + c * TC:k * 2048 + (c + 1) * TC],
                         start=(k == 0), stop=(k == 15), waits=wv)
                mmv = b.val("mm")
                if o < 16:
                    b.vop("tensor_copy", Aap[:, o * 2048 + c * TC:o * 2048 + (c + 1) * TC],
                          ps[:, 0:TC], waits=[("mm", mmv)])
                else:
                    bc = ST[0:32, 0:512]
                    b.vop("tensor_copy", bc, ps[0:32, 0:TC],
                          waits=[("mm", mmv), ("dma", b.val("dma"))])
                    b.fence_v(bc[:, 511:512])
                    b.dma(cc_pre_in.ap()[4096:4128, c * TC:(c + 1) * TC], bc,
                          waits=[("vx", b.val("vx"))])
        b.barrier()
        b.emit("gpsimd",
               lambda e: e.collective_compute(
                   "AllGather", ALU.bypass, replica_groups=PAIRS,
                   ins=[cc_pre_in.ap()[2048:4128, :].opt()],
                   outs=[cc_pre_outU.ap().opt()]),
               waits=[("dma", b.val("dma")), ("gdma", b.val("gdma"))], inc=("cc", 1))
        ccU = b.val("cc")
        dpostD = b.val("dma")

        lw_tab = {0: b.dma(WP[:, 0:2048], W_dt_p[:, 0:2048],
                           waits=[("mm", b.val("mm"))], q="gpsimd")}
        for o in range(16):
            half = (o % 2) * 2048
            if o + 1 < 16:
                lw_tab[o + 1] = b.dma(
                    WP[:, ((o + 1) % 2) * 2048:((o + 1) % 2) * 2048 + 2048],
                    W_dt_p[:, (o + 1) * 2048:(o + 2) * 2048],
                    waits=[("mm", b.val("mm"))], q="gpsimd")
            lw = lw_tab[o]
            for c in range(NCH):
                ps = P0 if (o * NCH + c) % 2 == 0 else P1
                for k in range(16):
                    wv = []
                    if k == 0:
                        wv = [("gdma", lw), ("dma", b.val("dma")),
                              ("sc", b.val("sc")), ("vx", b.val("vx"))]
                    b.mm(ps[:, 0:TC], WP[:, half + k * 128:half + (k + 1) * 128],
                         Aap[:, k * 2048 + c * TC:k * 2048 + (c + 1) * TC],
                         start=(k == 0), stop=(k == 15), waits=wv)
                mmv = b.val("mm")
                sf1 = stf[:, 2048:2560]
                sf2 = stf[:, 2560:3072]
                bcol = bdt[:, o:o + 1]
                b.act(sf1, ps[:, 0:TC], AF.Abs, bias=bcol, waits=[("mm", mmv)])
                b.act(sf1, sf1, AF.Exp, scale=-1.0)
                b.act(sf1, sf1, AF.Ln, bias=1.0)
                b.act(sf2, ps[:, 0:TC], AF.Relu, bias=bcol)
                dl = ST[:, 6144:6656]
                b.vop("tensor_add", dl, sf1, sf2,
                      waits=[("sc", b.val("sc")), ("dma", b.val("dma"))])
                b.fence_v(dl[:, 511:512])
                b.dma(cc_pre_in.ap()[o * 128:(o + 1) * 128, c * TC:(c + 1) * TC], dl,
                      waits=[("vx", b.val("vx"))])
        pend0 = (b.dma(ST[:, 0:2048],
                       cc_pre_outU.ap()[2048:2049, :].partition_broadcast(128),
                       waits=[("cc", ccU), ("dma", dpostD)]),
                 b.dma(ST[:, 2048:4096],
                       cc_pre_outU.ap()[2064:2065, :].partition_broadcast(128)))
        b.emit("gpsimd",
               lambda e: e.collective_compute(
                   "AllGather", ALU.bypass, replica_groups=PAIRS,
                   ins=[cc_pre_in.ap()[0:2048, :].opt()],
                   outs=[cc_pre_outD.ap().opt()]),
               waits=[("dma", b.val("dma")), ("gdma", b.val("gdma"))], inc=("cc", 1))
        ccD = b.val("cc")
        if DEBUG:
            b.dma(dbg["dbg_pre"], cc_pre_out.ap())
            b.dma(dbg["dbg_z"], z_dram.ap())
            b.barrier()

    if PHASE_LIMIT >= 2:
        preD = cc_pre_outD.ap()[0:128, :]
        preU = cc_pre_outU.ap()[0:128, :]
        dA_read = {}

        def bslot(i):
            return ST[:, (i % 2) * 4096:(i % 2) * 4096 + 2048]

        def cslot(i):
            return ST[:, (i % 2) * 4096 + 2048:(i % 2) * 4096 + 4096]

        for q in range(2):
            def ld_bc(i, w, _q=q):
                return (("dma", b.dma(bslot(i),
                        cc_pre_outU.ap()[_q * 2080 + 2048 + i:_q * 2080 + 2049 + i, :]
                        .partition_broadcast(128), waits=w)),
                        ("dma", b.dma(cslot(i),
                         cc_pre_outU.ap()[_q * 2080 + 2064 + i:_q * 2080 + 2065 + i, :]
                         .partition_broadcast(128))))
            for k in range(8):
                wu = ()
                if k == 0:
                    wu = [("cc", ccD)] if q == 0 else [("vx", b.val("vx"))]
                b.gather(Aap[:, (8 + k) * 2048:(9 + k) * 2048], preU,
                         gdu_sb[:, q * 16 + 8 + k:q * 16 + 9 + k], waits=wu)
            gw = () if q == 0 else [("vx", b.val("vx")), ("sc", b.val("sc"))]
            for k in range(8):
                b.gather(Aap[:, k * 2048:(k + 1) * 2048], preD,
                         gdu_sb[:, q * 16 + k:q * 16 + k + 1], waits=gw if k == 0 else ())
            fv = b.dma(gdu_sb, gdu_i, q="gpsimd")
            for k in range(8):
                b.vop("tensor_mul", Aap[:, (8 + k) * 2048:(9 + k) * 2048],
                      Aap[:, k * 2048:(k + 1) * 2048],
                      Aap[:, (8 + k) * 2048:(9 + k) * 2048],
                      waits=[("gdma", fv)] if k == 0 else ())
            b.barrier()
            if q == 0:
                pend = {0: (("dma", pend0[0]), ("dma", pend0[1]))}
            else:
                pend = {0: ld_bc(0, ())}
            carry_gp = None
            for n in range(SCAN_N):
                (bsem, lb), (csem, lc) = pend.pop(n)
                if n + 1 < SCAN_N:
                    pend[n + 1] = ld_bc(n + 1, [("vx", b.val("vx"))])
                for k in range(8):
                    it = n * 8 + k
                    ping = it % 2
                    dAv = xsf[:, ping * 2048:(ping + 1) * 2048]
                    wv = []
                    if ping in dA_read:
                        wv = [("vx", dA_read[ping])]
                    b.act(dAv, Aap[:, k * 2048:(k + 1) * 2048], AF.Exp,
                          scale=A_sb[:, k * 16 + n:k * 16 + n + 1], waits=wv)
                    scv = b.val("sc")
                    dBu = WP[:, 2048:4096]
                    wv2 = [(bsem, lb)] if k == 0 else []
                    b.vop("tensor_mul", dBu, Aap[:, (8 + k) * 2048:(9 + k) * 2048],
                          bslot(n), waits=wv2)
                    h = WP[:, 0:2048]
                    init = 0.0 if q == 0 else carry[:, k * 16 + n:k * 16 + n + 1]
                    wsc = [("sc", scv)]
                    if q == 0 and carry_gp is not None:
                        wsc.append(("gp", carry_gp))
                    b.vop("tensor_tensor_scan", h, dAv, dBu, init, ALU.mult, ALU.add,
                          waits=wsc)
                    dA_read[ping] = b.val("vx")
                    if q == 0:
                        cdst = carry[:, k * 16 + n:k * 16 + n + 1]
                        csrc = h[:, 2047:2048]
                        vhv = b.val("vx")
                        b.emit("gpsimd",
                               lambda e, _d=cdst, _s=csrc: e.tensor_copy(_d, _s),
                               waits=[("vx", vhv)], inc=("gp", 1))
                        carry_gp = b.val("gp")
                    tmp = WP[:, 2048:4096]
                    wv3 = [(csem, lc)] if k == 0 else []
                    b.vop("tensor_mul", tmp, h, cslot(n), waits=wv3)
                    ya = big1f[:, k * 2048:(k + 1) * 2048]
                    if n == 0:
                        b.vop("tensor_copy", ya, tmp)
                    else:
                        b.vop("tensor_add", ya, ya, tmp)
            vx_scan_end = b.val("vx")
            for k in range(8):
                slot = XS[:, (k % 2) * 4096:(k % 2) * 4096 + 4096].bitcast(BF16) \
                    if False else XS[:, (k % 2) * 2048 * 2:(k % 2) * 2048 * 2 + 2048 * 2]
                stg = XS[:, (k % 2) * 4096:(k % 2) * 4096 + 2048]
                wv = [("dma", b.val("dma"))] if k >= 2 else ()
                wv = tuple(wv) + (("gp", b.val("gp")),)
                b.vop("tensor_copy", stg, big1f[:, k * 2048:(k + 1) * 2048], waits=wv)
                b.fence_v(stg[:, 2047:2048])
                b.dma(cc_y_in.ap()[q * 1024 + k * 128:q * 1024 + (k + 1) * 128, :],
                      stg, waits=[("vx", b.val("vx"))])
            b.barrier()
            b.emit("gpsimd",
                   lambda e, _q=q: e.collective_compute(
                       "AllGather", ALU.bypass, replica_groups=PAIRS,
                       ins=[cc_y_in.ap()[_q * 1024:(_q + 1) * 1024, :].opt()],
                       outs=[cc_y_out.ap()[_q * 2048:(_q + 1) * 2048, :].opt()]),
                   waits=[("dma", b.val("dma")), ("gdma", b.val("gdma"))],
                   inc=("cc", 1))
        cc_y_done = b.val("cc")
        if DEBUG:
            b.dma(dbg["dbg_y"], cc_y_out.ap())
            b.dma(dbg["dbg_yin"], cc_y_in.ap())
            b.barrier()

    if PHASE_LIMIT >= 3:
        y_flat = cc_y_out.ap().rearrange("d (c w) -> (d c) w", w=512)[0:128, :]
        p3_cc = [cc_y_done]
        for o in range(8):
            b.dma(A1[:, o * 2048:(o + 1) * 2048], W_out_p[:, o * 2048:(o + 1) * 2048],
                  q="gpsimd")

        def p3_yb(c):
            return 0 if c % 2 == 0 else 24576

        def p3_y(c, w):
            base = p3_yb(c)
            for k in range(16):
                b.gather(Aap[:, base + k * 512:base + (k + 1) * 512], y_flat,
                         gy_sb[:, k * 4 + c:k * 4 + c + 1], waits=w if k == 0 else ())
            return b.dma(gy_sb, gy_i, q="gpsimd")

        def p3_usz(c, w):
            for k in range(16):
                b.dma(Aap[:, 8192 + k * 512:8192 + (k + 1) * 512],
                      cc_pre_in.ap()[2048 + k * 128:2048 + (k + 1) * 128,
                                     c * TC:(c + 1) * TC], waits=w if k == 0 else ())
                b.dma(A1[:, 20480 + k * 512:20480 + (k + 1) * 512],
                      z_dram.ap()[k * 128:(k + 1) * 128, c * TC:(c + 1) * TC])
            return b.val("dma")

        def p3_ln(c, w):
            for k in range(8):
                b.dma(A1[:, 16384 + k * 512:16384 + (k + 1) * 512],
                      ln_dram.ap()[k * 128:(k + 1) * 128, c * TC:(c + 1) * TC],
                      waits=w if k == 0 else ())
            return b.val("dma")

        def p3_x0(c, w):
            for k in range(8):
                b.dma(Af[:, 8192 + k * 512:8192 + (k + 1) * 512],
                      x_ext[k * 128:(k + 1) * 128, 3 + c * TC:3 + (c + 1) * TC],
                      waits=w if k == 0 else ())
            return b.val("dma")

        fy = {0: p3_y(0, [("cc", cc_y_done)])}
        du = {0: p3_usz(0, ())}
        dln = {0: p3_ln(0, ())}
        dx0 = {0: p3_x0(0, ())}
        store_mark = None
        for c in range(NCH):
            base = p3_yb(c)
            for k in range(16):
                yk = Aap[:, base + k * 512:base + (k + 1) * 512]
                w1 = [("gdma", fy[c]), ("dma", du[c])] if k == 0 else ()
                b.vop("scalar_tensor_tensor", yk,
                      Aap[:, 8192 + k * 512:8192 + (k + 1) * 512],
                      D_s[:, k:k + 1], yk, ALU.mult, ALU.add, waits=w1)
                b.vop("tensor_mul", yk, yk, A1[:, 20480 + k * 512:20480 + (k + 1) * 512])
            vx_y2 = b.val("vx")
            if c + 1 < NCH:
                fy[c + 1] = p3_y(c + 1, [("mm", b.val("mm"))])
                du[c + 1] = p3_usz(c + 1, [("vx", vx_y2)])
            for o in range(8):
                ps = P0 if o % 2 == 0 else P1
                for k in range(16):
                    wv = ()
                    if k == 0:
                        wv = [("vx", vx_y2 if o < 2 else b.val("vx"))]
                    b.mm(ps[:, 0:TC], A1[:, (o * 16 + k) * 128:(o * 16 + k + 1) * 128],
                         Aap[:, base + k * 512:base + (k + 1) * 512],
                         start=(k == 0), stop=(k == 15), waits=wv)
                wadd = [("mm", b.val("mm")), ("dma", dln[c])]
                if o == 0 and store_mark is not None:
                    wadd.append(("dma", store_mark))
                b.vop("tensor_add", xsf[:, o * 512:(o + 1) * 512], ps[:, 0:TC],
                      A1[:, 16384 + o * 512:16384 + (o + 1) * 512], waits=wadd)
            if c + 1 < NCH:
                dln[c + 1] = p3_ln(c + 1, [("vx", b.val("vx"))])
            tiles = [xsf[:, o * 512:(o + 1) * 512] for o in range(8)]
            m_rep, r_rep = ln_stats(tiles, TC, w0=True)
            vv = b.val("vx")
            for o in range(8):
                wsub = [("vx", vv)] if o == 0 else ()
                if o == 0:
                    wsub = wsub + [("dma", dx0[c])]
                b.vop("tensor_sub", tiles[o], tiles[o], m_rep, waits=wsub)
                b.vch("tensor_mul", tiles[o], tiles[o], r_rep)
                b.vch("tensor_scalar", tiles[o], tiles[o], l0ag[:, o:o + 1],
                      l0ab[:, o:o + 1], ALU.mult, ALU.add)
                b.vch("tensor_add", tiles[o], tiles[o],
                      Af[:, 8192 + o * 512:8192 + (o + 1) * 512])
                b.fence_v(tiles[o][:, 511:512])
                b.dma(xres_A.ap()[o * 128:(o + 1) * 128, c * TC:(c + 1) * TC], tiles[o],
                      waits=[("vx", b.val("vx"))])
            store_mark = b.val("dma")
            if c + 1 < NCH:
                dx0[c + 1] = p3_x0(c + 1, [("vx", b.val("vx"))])
        b.barrier()
        if DEBUG:
            b.dma(dbg["dbg_x1"], xres_A.ap())
            b.barrier()

    if PHASE_LIMIT >= 4:
        for j in range(4):
            b.dma(A1[:, j * 4096:(j + 1) * 4096], enc_p[:, j * 4096:(j + 1) * 4096], q="gpsimd")
            b.dma(A1[:, 16384 + j * 4096:16384 + (j + 1) * 4096],
                  dec_p[:, j * 4096:(j + 1) * 4096], q="gpsimd")
        b.dma(Aap[0:64, 20480:28672], ew1_p, q="gpsimd")
        b.dma(ST[0:64, 3072:5120], c2s_p, q="gpsimd")
        b.dma(Aap[:, 28672:32768], ew2_p, q="gpsimd")
        b.dma(ST[:, 2048:3072], s2c_p, q="gpsimd")
        b.emit("tensor", lambda e: None,
               waits=[("gdma", b.val("gdma")), ("dma", b.val("dma"))])
        W4 = TC4

        def p4_x1buf(c):
            return xsf if c % 2 == 0 else Af[:, 4096:6144]

        def p4_x1(c, w):
            xb = p4_x1buf(c)
            for k in range(8):
                b.dma(xb[:, k * W4:(k + 1) * W4],
                      xres_A.ap()[k * 128:(k + 1) * 128, c * W4:(c + 1) * W4],
                      waits=w if k == 0 else ())
            return b.val("dma")

        dx1 = {0: p4_x1(0, [("vx", b.val("vx")), ("sc", b.val("sc"))])}
        p4_store = None
        for c in range(NCH4):
            xb = p4_x1buf(c)
            tiles = [xb[:, k * W4:(k + 1) * W4] for k in range(8)]
            m_rep, r_rep = ln_stats(tiles, W4, w0=True, extra=[("dma", dx1[c])])
            vv = b.val("vx")
            ln2 = []
            for k in range(8):
                b.vop("tensor_sub", wpf[:, 1024:1024 + W4], tiles[k], m_rep,
                      waits=[("vx", vv), ("dma", dx1[c]), ("mm", b.val("mm"))]
                      if k == 0 else ())
                dst = Aap[:, 16384 + k * W4:16384 + (k + 1) * W4]
                b.vch("tensor_tensor", dst, wpf[:, 1024:1024 + W4], r_rep, ALU.mult)
                ln2.append(dst)
            vx_ln2 = b.val("vx")
            if c + 1 < NCH4:
                dx1[c + 1] = p4_x1(c + 1, [("vx", b.val("vx")),
                                           ("sc", b.val("sc")),
                                           ("mm", b.val("mm"))])
            for o in range(16):
                ps = P0 if o % 2 == 0 else P1
                for k in range(8):
                    wv = ()
                    if k == 0:
                        wv = [("sc", b.val("sc"))]
                        if o < 2:
                            wv.append(("vx", vx_ln2))
                    b.mm(ps[:, 0:W4], A1[:, (o * 8 + k) * 128:(o * 8 + k + 1) * 128],
                         ln2[k], start=(k == 0), stop=(k == 7), waits=wv)
                b.act(Aap[:, o * W4:(o + 1) * W4], ps[:, 0:W4], AF.Sigmoid,
                      bias=encb[:, o:o + 1], waits=[("mm", b.val("mm"))])
            spk = [Aap[:, o * W4:(o + 1) * W4] for o in range(16)]
            for k in range(16):
                wv = [("sc", b.val("sc")), ("vx", b.val("vx"))] if k == 0 else ()
                b.mm(P2[0:64, 0:W4], ST[:, 2048 + k * 64:2048 + (k + 1) * 64], spk[k],
                     start=(k == 0), stop=(k == 15), waits=wv)
            b.vop("tensor_scalar", moe_f, P2[0:64, 0:W4], s2cb[:, 0:1], None, ALU.add,
                  waits=[("mm", b.val("mm"))])
            b.vch("tensor_copy", cont_bf, moe_f)
            vx_cont = b.val("vx")
            b.mm(P2[0:64, 0:W4], rw1, cont_bf, start=True, stop=True,
                 waits=[("vx", vx_cont)])
            b.act(rh_bf, P2[0:64, 0:W4], AF.Relu, bias=rb1[:, 0:1],
                  waits=[("mm", b.val("mm"))])
            if DEBUG and c == 0:
                b.dma(dbg["dbg_rh"], rh_bf, q="gpsimd")
                b.dma(dbg["dbg_cont"], cont_bf, q="gpsimd")
                b.barrier()
            if DEBUG and c == 0 and P4STAGE >= 21:
                b.fence_v(logit_sb[0:8, 0:1])
                b.dma(dbg["dbg_lgr"], logit_sb[0:8, 0:W4], waits=[("vx", b.val("vx"))])
                b.barrier()
            for tb in range(W4 // 128 if P4STAGE >= 3 else 0):
                po = (c % 2) * 32 + tb * 16
                wtb = [("vx", b.val("vx"))]
                if tb == 0:
                    wtb.append(("sc", b.val("sc")))
                b.mm(P3[:, po:po + 8], rh_bf[0:64, tb * 128:(tb + 1) * 128], rw2,
                     start=True, stop=True, waits=wtb)
                wcc = [("mm", b.val("mm"))]
                if tb == 0:
                    wcc.append(("dma", b.val("dma")))
                b.vch("tensor_tensor", lgt[:, 0:8], P3[:, po:po + 8], rb2rep, ALU.add,
                      waits=wcc)
                b.vch("reduce_max", m1t, lgt[:, 0:8], AX.X)
                vv = b.val("vx")
                b.vch("tensor_scalar", msk1[:, 0:8], lgt[:, 0:8], m1t[:, 0:1], None,
                      ALU.is_equal, waits=[("vx", vv)])
                b.vch("tensor_mul", candt[:, 0:8], msk1[:, 0:8], iota8m)
                b.vch("tensor_scalar", candt[:, 0:8], candt[:, 0:8], 8.0, None, ALU.add)
                b.vch("tensor_reduce", i1t, candt[:, 0:8], AX.X, ALU.min)
                vv = b.val("vx")
                b.vch("tensor_scalar", msk1[:, 0:8], iota8, i1t[:, 0:1], None,
                      ALU.is_equal, waits=[("vx", vv)])
                b.vch("tensor_scalar", candt[:, 0:8], msk1[:, 0:8], NEG_BIG, None, ALU.mult)
                b.vch("tensor_add", candt[:, 0:8], candt[:, 0:8], lgt[:, 0:8])
                b.vch("reduce_max", m2t, candt[:, 0:8], AX.X)
                vv = b.val("vx")
                b.vch("tensor_scalar", msk2[:, 0:8], candt[:, 0:8], m2t[:, 0:1], None,
                      ALU.is_equal, waits=[("vx", vv)])
                b.vch("tensor_mul", msk2[:, 0:8], msk2[:, 0:8], iota8m)
                b.vch("tensor_scalar", msk2[:, 0:8], msk2[:, 0:8], 8.0, None, ALU.add)
                b.vch("tensor_reduce", i1t, msk2[:, 0:8], AX.X, ALU.min)
                vv = b.val("vx")
                b.vch("tensor_scalar", msk2[:, 0:8], iota8, i1t[:, 0:1], None,
                      ALU.is_equal, waits=[("vx", vv)])
                b.vch("tensor_sub", w1t, m1t, m2t)
                vv = b.val("vx")
                b.act(w1t, w1t, AF.Sigmoid, waits=[("vx", vv)])
                scv = b.val("sc")
                b.vch("tensor_scalar", w2t, w1t, -1.0, 1.0, ALU.mult, ALU.add,
                      waits=[("sc", scv)])
                vv = b.val("vx")
                b.vch("tensor_scalar", candt[:, 0:8], msk1[:, 0:8], w1t[:, 0:1], None,
                      ALU.mult, waits=[("vx", vv)])
                b.vch("scalar_tensor_tensor", candt[:, 0:8], msk2[:, 0:8], w2t[:, 0:1],
                      candt[:, 0:8], ALU.mult, ALU.add)
                vv = b.val("vx")
                b.mm(P3[0:8, 256:384], candt[:, 0:8], identf, start=True, stop=True,
                     waits=[("vx", vv)])
                b.vch("tensor_copy", comb_sb[0:8, tb * 128:(tb + 1) * 128],
                      P3[0:8, 256:384], waits=[("mm", b.val("mm"))])
            b.fence_v(comb_sb[0:8, 0:1])
            b.fence_v(comb_sb[0:8, W4 - 1:W4])
            b.dma(comb_dram.ap()[c * 8:(c + 1) * 8, :], comb_sb,
                  waits=[("vx", b.val("vx"))])
            for e in range(8 if P4STAGE >= 4 else 0):
                lb = b.dma(comb_rep, comb_dram.ap()[c * 8 + e:c * 8 + e + 1, :]
                           .partition_broadcast(64), waits=[("vx", b.val("vx"))])
                for oc in range(8):
                    wenc = [("sc", b.val("sc"))]
                    if e == 0 and oc == 0:
                        wenc.append(("vx", vx_cont))
                    b.mm(P0[:, 0:W4], Aap[0:64, 20480 + (e * 8 + oc) * 128:
                                          20480 + (e * 8 + oc + 1) * 128],
                         cont_bf, start=True, stop=True, waits=wenc)
                    b.act(shb[:, 0:W4], P0[:, 0:W4], AF.Relu,
                          bias=eb1[:, e * 8 + oc:e * 8 + oc + 1],
                          waits=[("mm", b.val("mm"))])
                    b.mm(P1[0:64, 0:W4], Aap[:, 28672 + (e * 8 + oc) * 64:
                                             28672 + (e * 8 + oc + 1) * 64],
                         shb[:, 0:W4], start=(oc == 0), stop=(oc == 7),
                         waits=[("sc", b.val("sc"))], skip_gc=True)
                mmv = b.val("mm")
                b.vop("tensor_scalar", eo_f, P1[0:64, 0:W4], eb2[:, e:e + 1], None,
                      ALU.add, waits=[("mm", mmv)])
                b.vch("tensor_mul", eo_f, eo_f, comb_rep, waits=[("dma", lb)])
                if e == 0:
                    b.vch("tensor_copy", moe_f, eo_f)
                else:
                    b.vch("tensor_add", moe_f, moe_f, eo_f)
            b.vch("tensor_copy", moe_bf, moe_f)
            vx_moe = b.val("vx")
            for o in range(16):
                ps = P0 if o % 2 == 0 else P1
                wc2 = [("sc", b.val("sc"))]
                if o < 2:
                    wc2.append(("vx", vx_moe))
                b.mm(ps[:, 0:W4], ST[0:64, 3072 + o * 128:3072 + (o + 1) * 128],
                     moe_bf, start=True, stop=True, waits=wc2)
                b.act(Aap[:, o * W4:(o + 1) * W4], ps[:, 0:W4], AF.Sigmoid,
                      bias=c2sb[:, o:o + 1], waits=[("mm", b.val("mm"))])
            rates = [Aap[:, k * W4:(k + 1) * W4] for k in range(16)]
            for o in range(8):
                ps = P0 if o % 2 == 0 else P1
                for k in range(16):
                    wv = [("sc", b.val("sc"))] if k == 0 else ()
                    b.mm(ps[:, 0:W4], A1[:, 16384 + (o * 16 + k) * 128:
                                         16384 + (o * 16 + k + 1) * 128],
                         rates[k], start=(k == 0), stop=(k == 15), waits=wv)
                wact = [("mm", b.val("mm"))]
                if o == 0:
                    wact.append(("vx", b.val("vx")))
                    if p4_store is not None:
                        wact.append(("dma", p4_store))
                b.act(xsf[:, 2048 + o * W4:2048 + (o + 1) * W4], ps[:, 0:W4], AF.Sigmoid,
                      bias=decb[:, o:o + 1], waits=wact)
            tiles = [xsf[:, 2048 + o * W4:2048 + (o + 1) * W4] for o in range(8)]
            m_rep, r_rep = ln_stats(tiles, W4, w0=True)
            vv = b.val("vx")
            scv9 = b.val("sc")
            for o in range(8):
                b.vop("tensor_sub", tiles[o], tiles[o], m_rep,
                      waits=[("vx", vv), ("sc", scv9)] if o == 0 else ())
                b.vch("tensor_mul", tiles[o], tiles[o], r_rep)
                b.vch("tensor_scalar", tiles[o], tiles[o], l0fg[:, o:o + 1],
                      l0fb[:, o:o + 1], ALU.mult, ALU.add)
                b.vch("tensor_add", tiles[o], tiles[o], xb[:, o * W4:(o + 1) * W4])
                b.fence_v(tiles[o][:, W4 - 1:W4])
                b.dma(xres_B.ap()[o * 128:(o + 1) * 128, c * W4:(c + 1) * W4], tiles[o],
                      waits=[("vx", b.val("vx"))])
            p4_store = b.val("dma")
        b.barrier()
        if DEBUG:
            b.dma(dbg["dbg_x2"], xres_B.ap())
            b.dma(dbg["dbg_comb"], comb_dram.ap())
            b.barrier()

    if PHASE_LIMIT >= 5:
        for j in range(2):
            b.dma(A1[:, j * 4096:(j + 1) * 4096], Wq_p[:, j * 4096:(j + 1) * 4096], q="gpsimd")
            b.dma(A1[:, 8192 + j * 4096:8192 + (j + 1) * 4096],
                  Wk_p[:, j * 4096:(j + 1) * 4096], q="gpsimd")
            b.dma(A1[:, 16384 + j * 4096:16384 + (j + 1) * 4096],
                  Wv_p[:, j * 4096:(j + 1) * 4096], q="gpsimd")
            b.dma(A1[:, 24576 + j * 4096:24576 + (j + 1) * 4096],
                  Wo_p[:, j * 4096:(j + 1) * 4096], q="gpsimd")
        def a5_buf(c, k):
            base = 8192 if c % 2 == 0 else 12288
            return Af[:, base + k * 512:base + (k + 1) * 512]

        def a5_ld(c, w):
            for k in range(8):
                b.dma(a5_buf(c, k),
                      xres_B.ap()[k * 128:(k + 1) * 128, c * TC:(c + 1) * TC],
                      waits=w if k == 0 else ())
            return b.val("dma")

        a5_mk = {0: a5_ld(0, [("vx", b.val("vx"))])}
        for c in range(NCH):
            tiles = [a5_buf(c, k) for k in range(8)]
            if c + 1 < NCH:
                a5_mk[c + 1] = a5_ld(c + 1, [("vx", b.val("vx")),
                                             ("sc", b.val("sc")),
                                             ("mm", b.val("mm"))])
            m_rep, r_rep = ln_stats(tiles, TC, w0=True, extra=[("dma", a5_mk[c])])
            vv = b.val("vx")
            for k in range(8):
                wsub = [("vx", vv), ("dma", a5_mk[c])] if k == 0 else ()
                b.vop("tensor_sub", tiles[k], tiles[k], m_rep, waits=wsub)
                b.vch("tensor_tensor", Aap[:, k * 2048 + c * TC:k * 2048 + (c + 1) * TC],
                      tiles[k], r_rep, ALU.mult)
        b.barrier()
        b.emit("tensor", lambda e: None, waits=[("gdma", b.val("gdma"))])
        a5_store = {}
        a5_i = 0
        for (wofs, dst) in [(0, q_dram), (8192, k_dram)]:
            for o in range(8):
                for c in range(NCH):
                    ps = P0 if (o * NCH + c) % 2 == 0 else P1
                    for k in range(8):
                        wv = [("vx", b.val("vx"))] if (o + c > 0 and k == 0) else ()
                        b.mm(ps[:, 0:TC],
                             A1[:, wofs + (o * 8 + k) * 128:wofs + (o * 8 + k + 1) * 128],
                             Aap[:, k * 2048 + c * TC:k * 2048 + (c + 1) * TC],
                             start=(k == 0), stop=(k == 7), waits=wv)
                    sofs = (a5_i % 2) * 512
                    wvp = [("mm", b.val("mm"))]
                    if a5_i % 2 in a5_store:
                        wvp.append(("dma", a5_store[a5_i % 2]))
                    elif a5_i < 2:
                        wvp.append(("dma", b.val("dma")))
                    b.vop("tensor_scalar", ST[:, sofs:sofs + 512], ps[:, 0:TC],
                          0.0, 0.001, ALU.max, ALU.add, waits=wvp)
                    b.fence_v(ST[:, sofs + 511:sofs + 512])
                    a5_store[a5_i % 2] = b.dma(
                        dst.ap()[o * 128:(o + 1) * 128, c * TC:(c + 1) * TC],
                        ST[:, sofs:sofs + 512], waits=[("vx", b.val("vx"))])
                    a5_i += 1
        b.barrier()
        a5_store = {}
        a5_i = 0
        for tb in range(16):
            for vc in range(2):
                ps = P0 if (tb * 2 + vc) % 2 == 0 else P1
                for k in range(8):
                    wv = [("vx", b.val("vx"))] if (tb + vc > 0 and k == 0) else ()
                    b.mm(ps[:, 0:512],
                         Aap[:, k * 2048 + tb * 128:k * 2048 + (tb + 1) * 128],
                         A1[:, 16384 + k * 1024 + vc * 512:16384 + k * 1024 + (vc + 1) * 512],
                         start=(k == 0), stop=(k == 7), waits=wv)
                sofs = (a5_i % 2) * 512
                wvp = [("mm", b.val("mm"))]
                if a5_i % 2 in a5_store:
                    wvp.append(("dma", a5_store[a5_i % 2]))
                elif a5_i < 2:
                    wvp.append(("dma", b.val("dma")))
                b.vop("tensor_copy", ST[:, sofs:sofs + 512], ps[:, 0:512], waits=wvp)
                b.fence_v(ST[:, sofs + 511:sofs + 512])
                a5_store[a5_i % 2] = b.dma(
                    v_dram.ap()[tb * 128:(tb + 1) * 128, vc * 512:(vc + 1) * 512],
                    ST[:, sofs:sofs + 512], waits=[("vx", b.val("vx"))])
                a5_i += 1
        b.barrier()
        b.vop("memset", vaug[:, 64:65], 1.0)
        b.vop("memset", vaug[:, 129:130], 1.0)
        b.vop("memset", vaug2[:, 64:65], 1.0)
        b.vop("memset", vaug2[:, 129:130], 1.0)
        SFs = (S_f, S_f2)
        SAs = (saug, saug2)
        VAs = (vaug, vaug2)
        KTs = (ktok, ktok2)
        AMs = (amb, amb2)
        DENs = (den_s, den_s2)
        SUPD = (P4[0:64, 0:65], P1[0:64, 0:65])
        for j in range(8 if P5STAGE >= 2 else 0):
            lq = {}; lk = {}
            for s in (0, 1):
                hh = 2 * j + s
                QH = ST[0:64, s * 4096:s * 4096 + 2048]
                KH = ST[0:64, s * 4096 + 2048:s * 4096 + 4096]
                lq[s] = b.dma(QH, q_dram.ap()[hh * 64:(hh + 1) * 64, :],
                              waits=[("mm", b.val("mm")), ("vx", b.val("vx"))] if s == 0 else ())
                lk[s] = b.dma(KH, k_dram.ap()[hh * 64:(hh + 1) * 64, :])
                b.vop("memset", SFs[s], 0.0, waits=[("dma", b.val("dma"))] if (j > 0 and s == 0) else ())
            for cch in range(16):
                for s in (0, 1):
                    hh = 2 * j + s
                    QH = ST[0:64, s * 4096:s * 4096 + 2048]
                    KH = ST[0:64, s * 4096 + 2048:s * 4096 + 4096]
                    numv = xsf[0:64, s * 2048:(s + 1) * 2048]
                    vofs = (cch % 2) * 65
                    lv = b.dma(VAs[s][:, vofs:vofs + 64],
                               v_dram.ap()[cch * 128:(cch + 1) * 128, hh * 64:(hh + 1) * 64],
                               waits=[("mm", b.val("mm"))])
                    wk0 = [("vx", b.val("vx"))]
                    if cch == 0:
                        wk0.append(("dma", lk[s]))
                    b.mm(P4[:, 128 + 64 * s:192 + 64 * s], KH[:, cch * 128:(cch + 1) * 128],
                         identb[0:64, 0:64], start=True, stop=True, skip_gc=True, waits=wk0)
                    b.vop("tensor_copy", KTs[s], P4[:, 128 + 64 * s:192 + 64 * s],
                          waits=[("mm", b.val("mm"))])
                    wq0 = [("vx", b.val("vx"))]
                    if cch == 0:
                        wq0.append(("dma", lq[s]))
                    b.mm(P2[:, s * 128:(s + 1) * 128], KH[:, cch * 128:(cch + 1) * 128],
                         QH[:, cch * 128:(cch + 1) * 128], start=True, stop=True,
                         skip_gc=True, waits=wq0)
                    b.vop("tensor_mul", AMs[s], P2[:, s * 128:(s + 1) * 128], cmask,
                          waits=[("mm", b.val("mm"))])
                    b.vch("tensor_copy", SAs[s], SFs[s])
                    vv = b.val("vx")
                    b.mm(P3[0:64, s * 128:(s + 1) * 128], VAs[s][:, vofs:vofs + 64], AMs[s],
                         start=True, stop=True, skip_gc=True,
                         waits=[("vx", vv), ("dma", lv)], inc=False)
                    b.mm(P3[0:64, s * 128:(s + 1) * 128], SAs[s][:, 0:64],
                         QH[:, cch * 128:(cch + 1) * 128], start=False, stop=True, skip_gc=True)
                    b.mm(P4[0:1, 256 + 128 * s:384 + 128 * s], VAs[s][:, vofs + 64:vofs + 65],
                         AMs[s], start=True, stop=True, skip_gc=True, inc=False)
                    b.mm(P4[0:1, 256 + 128 * s:384 + 128 * s], SAs[s][:, 64:65],
                         QH[:, cch * 128:(cch + 1) * 128], start=False, stop=True, skip_gc=True)
                    mmv = b.val("mm")
                    wnum = [("mm", mmv)]
                    if cch == 0:
                        wnum.append(("dma", b.val("dma")))
                    b.vop("tensor_copy", numv[:, cch * 128:(cch + 1) * 128],
                          P3[0:64, s * 128:(s + 1) * 128], waits=wnum)
                    b.vop("tensor_copy", DENs[s][:, cch * 128:(cch + 1) * 128],
                          P4[0:1, 256 + 128 * s:384 + 128 * s])
                    b.mm(SUPD[s], KTs[s], VAs[s][:, vofs:vofs + 65], start=True, stop=True,
                         skip_gc=True, waits=[("vx", b.val("vx"))])
                    b.vch("tensor_add", SFs[s], SFs[s], SUPD[s], waits=[("mm", b.val("mm"))])
            for s in (0, 1):
                hh = 2 * j + s
                numv = xsf[0:64, s * 2048:(s + 1) * 2048]
                b.fence_v(SFs[s][:, 64:65])
                b.fence_v(numv[:, 2047:2048])
                b.fence_v(DENs[s][:, 2047:2048])
                vv = b.val("vx")
                b.dma(cc_att_in.ap()[:, hh * 65:(hh + 1) * 65], SFs[s], waits=[("vx", vv)])
                b.dma(num_dram.ap()[hh * 64:(hh + 1) * 64, :], numv, waits=[("vx", vv)])
                b.dma(den_dram.ap()[hh:hh + 1, :], DENs[s], waits=[("vx", vv)])
        b.barrier()
        b.emit("gpsimd",
               lambda e: e.collective_compute(
                   "AllGather", ALU.bypass, replica_groups=PAIRS,
                   ins=[cc_att_in.ap().opt()], outs=[cc_att_out.ap().opt()]),
               waits=[("dma", b.val("dma")), ("gdma", b.val("gdma"))], inc=("cc", 1))
        b.barrier()
        def c5_QH(p):
            return ST[0:64, p * 2048:(p + 1) * 2048]

        def c5_NUM(p):
            return xsf[0:64, p * 2048:(p + 1) * 2048]

        def c5_DEN(p):
            return big1f[0:1, p * 2048:(p + 1) * 2048]

        c5_SPV = (sprev_f, stf[0:64, 2048:2113])
        c5_SP64 = (sprev64, ST[0:64, 4240:4304])
        c5_SPK = (spk1, ST[0:64, 4380:4381])
        c5_DBF = (A1[0:1, 8192:10240], A1[0:1, 10240:12288])
        c5_CORR = (P3, P2)
        c5_BC = (P0, P1)
        c5_L = {}
        c5_vxend = {}
        c5_mmq = {}
        c5_mmend = {}

        def c5_loads(hh):
            p = hh % 2
            if hh >= 2:
                w = [("vx", c5_vxend[hh - 2]), ("mm", c5_mmq[hh - 2])]
            else:
                w = [("vx", b.val("vx")), ("mm", b.val("mm"))]
            lq = b.dma(c5_QH(p), q_dram.ap()[hh * 64:(hh + 1) * 64, :], waits=w)
            b.dma(c5_SPV[p], cc_att_out.ap()[0:64, hh * 65:(hh + 1) * 65])
            b.dma(c5_NUM(p), num_dram.ap()[hh * 64:(hh + 1) * 64, :])
            ld = b.dma(c5_DEN(p), den_dram.ap()[hh:hh + 1, :])
            c5_L[hh] = (lq, ld)

        NH5C = 16 if P5STAGE >= 3 else 0
        for hh in range(min(2, NH5C)):
            c5_loads(hh)
        for hh in range(NH5C):
            p = hh % 2
            lq, ld = c5_L[hh]
            Qh = c5_QH(p)
            numv = c5_NUM(p)
            denv = c5_DEN(p)
            CR = c5_CORR[p]
            wsp = [("dma", ld)]
            if hh >= 2:
                wsp.append(("mm", c5_mmq[hh - 2]))
            b.vop("tensor_scalar", c5_SP64[p], c5_SPV[p][:, 0:64], odd64[:, 0:1], None,
                  ALU.mult, waits=wsp)
            b.vop("tensor_scalar", c5_SPK[p], c5_SPV[p][:, 64:65], odd64[:, 0:1], None,
                  ALU.mult)
            for qc in range(4):
                for j4 in range(4):
                    b.mm(CR[0:64, j4 * 128:(j4 + 1) * 128], c5_SP64[p],
                         Qh[:, qc * 512 + j4 * 128:qc * 512 + (j4 + 1) * 128],
                         start=True, stop=True, skip_gc=True,
                         waits=[("vx", b.val("vx")), ("dma", lq)] if j4 == 0 else ())
                b.mm(P4[0:1, 0:512], c5_SPK[p], Qh[:, qc * 512:(qc + 1) * 512],
                     start=True, stop=True, skip_gc=True)
                mmv = b.val("mm")
                for h2 in range(2):
                    b.vop("tensor_add",
                          numv[:, qc * 512 + h2 * 256:qc * 512 + (h2 + 1) * 256],
                          numv[:, qc * 512 + h2 * 256:qc * 512 + (h2 + 1) * 256],
                          CR[0:64, h2 * 256:(h2 + 1) * 256],
                          waits=[("mm", mmv)] if h2 == 0 else ())
                b.vch("tensor_add", denv[:, qc * 512:(qc + 1) * 512],
                      denv[:, qc * 512:(qc + 1) * 512], P4[0:1, 0:512])
            c5_mmq[hh] = b.val("mm")
            b.vch("reciprocal", denv, denv)
            wdb = [("mm", c5_mmend[hh - 2])] if hh >= 2 else ()
            b.vch("tensor_copy", c5_DBF[p], denv, waits=wdb)
            vv = b.val("vx")
            for qc in range(4):
                b.mm(c5_BC[p][0:64, 0:512], onesb[0:1, 0:64],
                     c5_DBF[p][:, qc * 512:(qc + 1) * 512],
                     start=True, stop=True, skip_gc=True,
                     waits=[("vx", vv)] if qc == 0 else [("vx", b.val("vx"))])
                wmul = [("mm", b.val("mm"))]
                if qc == 0:
                    wmul.append(("gdma", b.val("gdma")))
                b.vop("tensor_mul", wpf[0:64, qc * 512:(qc + 1) * 512],
                      numv[:, qc * 512:(qc + 1) * 512], c5_BC[p][0:64, 0:512],
                      waits=wmul)
            c5_mmend[hh] = b.val("mm")
            c5_vxend[hh] = b.val("vx")
            if hh + 2 < NH5C:
                c5_loads(hh + 2)
            b.fence_v(wpf[0:64, 2047:2048])
            b.dma(attn_dram.ap()[hh * 64:(hh + 1) * 64, :],
                  wpf[0:64, 0:2048], waits=[("vx", b.val("vx"))], q="gpsimd")
        b.barrier()
        if DEBUG:
            b.dma(dbg["dbg_att"], attn_dram.ap())
        for c in range(NCH if P5STAGE >= 4 else 0):
            for k in range(8):
                b.dma(Aap[:, k * 512:(k + 1) * 512],
                      attn_dram.ap()[k * 128:(k + 1) * 128, c * TC:(c + 1) * TC],
                      waits=[("vx", b.val("vx"))] if k == 0 else ())
                b.dma(Af[:, 8192 + k * 512:8192 + (k + 1) * 512],
                      xres_B.ap()[k * 128:(k + 1) * 128, c * TC:(c + 1) * TC])
            b.barrier()
            for o in range(8):
                ps = P0 if o % 2 == 0 else P1
                for k in range(8):
                    wv = [("vx", b.val("vx"))] if o >= 2 and k == 0 else ()
                    b.mm(ps[:, 0:TC],
                         A1[:, 24576 + (o * 8 + k) * 128:24576 + (o * 8 + k + 1) * 128],
                         Aap[:, k * 512:(k + 1) * 512], start=(k == 0), stop=(k == 7),
                         waits=wv)
                b.vop("tensor_scalar", wpf[:, 0:512], ps[:, 0:TC], bo_s[:, o:o + 1],
                      None, ALU.add, waits=[("mm", b.val("mm")), ("dma", b.val("dma"))])
                b.vch("tensor_add", wpf[:, 0:512], wpf[:, 0:512],
                      Af[:, 8192 + o * 512:8192 + (o + 1) * 512])
                b.fence_v(wpf[:, 511:512])
                b.dma(xres_A.ap()[o * 128:(o + 1) * 128, c * TC:(c + 1) * TC],
                      wpf[:, 0:512], waits=[("vx", b.val("vx"))])
            b.barrier()
        if DEBUG:
            b.dma(dbg["dbg_x3"], xres_A.ap())
            b.barrier()

    if PHASE_LIMIT >= 6:
        for j in range(3):
            b.dma(A1[:, j * 6144:(j + 1) * 6144], W2_p[:, j * 6144:(j + 1) * 6144], q="gpsimd")
        b.dma(A1[:, 18432:20480], Wo1_p, q="gpsimd")
        b.dma(wo2b, Wo2_p, q="gpsimd")
        b.emit("tensor", lambda e: None, waits=[("gdma", b.val("gdma"))])
        for c in range(NCH):
            for k in range(8):
                b.dma(Af[:, 8704 + k * 512:8704 + (k + 1) * 512],
                      xres_A.ap()[k * 128:(k + 1) * 128, c * TC:(c + 1) * TC],
                      waits=[("vx", b.val("vx"))] if k == 0 else ())
            b.barrier()
            tiles = [Af[:, 8704 + k * 512:8704 + (k + 1) * 512] for k in range(8)]
            m_rep, r_rep = ln_stats(tiles, TC)
            vv = b.val("vx")
            for k in range(8):
                b.vop("tensor_sub", tiles[k], tiles[k], m_rep,
                      waits=[("vx", vv)] if k == 0 else ())
                b.vch("tensor_tensor",
                      Aap[:, k * 2050 + 1 + c * TC:k * 2050 + 1 + (c + 1) * TC],
                      tiles[k], r_rep, ALU.mult)
            b.barrier()
        for k in range(8):
            b.vch("tensor_copy", mix_sb[:, k:k + 1], Aap[:, k * 2050 + 1:k * 2050 + 2])
            b.vch("tensor_copy", mix_sb[:, 8 + k:9 + k],
                  Aap[:, k * 2050 + 2048:k * 2050 + 2049])
        b.fence_v(mix_sb[:, 15:16])
        b.dma(cc_mix_in.ap(), mix_sb, waits=[("vx", b.val("vx"))])
        b.emit("gpsimd",
               lambda e: e.collective_compute(
                   "AllGather", ALU.bypass, replica_groups=PAIRS,
                   ins=[cc_mix_in.ap().opt()], outs=[cc_mix_out.ap().opt()]),
               waits=[("dma", b.val("dma")), ("gdma", b.val("gdma"))], inc=("cc", 1))
        b.barrier()
        b.dma(mixh[:, 0:16], cc_mix_out.ap()[0:128, :])
        b.dma(mixh[:, 16:32], cc_mix_out.ap()[128:256, :])
        b.vop("tensor_scalar", mixh[:, 8:16], mixh[:, 8:16], isodd[:, 0:1], None,
              ALU.mult, waits=[("dma", b.val("dma"))])
        b.vch("tensor_scalar", mixh[:, 16:24], mixh[:, 16:24], iseven[:, 0:1], None,
              ALU.mult)
        for k in range(8):
            b.vch("tensor_copy", Aap[:, k * 2050:k * 2050 + 1], mixh[:, 8 + k:9 + k])
            b.vch("tensor_copy", Aap[:, k * 2050 + 2049:k * 2050 + 2050],
                  mixh[:, 16 + k:17 + k])
        b.barrier()
        for c in range(NCH):
            x3c = A1f[:, 10240 + 0:10240 + 4096]
            for k in range(8):
                b.dma(A1f[:, 10240 + k * 512:10240 + (k + 1) * 512],
                      xres_A.ap()[k * 128:(k + 1) * 128, c * TC:(c + 1) * TC],
                      waits=[("vx", b.val("vx"))] if k == 0 else ())
            x3mk = b.val("dma")
            for k in range(8):
                base = k * 2050 + c * TC
                acc = stf[:, 2048:2560]
                wv = [("vx", b.val("vx")), ("mm", b.val("mm"))] if k == 0 else ()
                b.vop("tensor_scalar", acc, Aap[:, base:base + 512],
                      mixk[:, k * 3:k * 3 + 1], None, ALU.mult, waits=wv)
                b.vop("scalar_tensor_tensor", acc, Aap[:, base + 1:base + 513],
                      mixk[:, k * 3 + 1:k * 3 + 2], acc, ALU.mult, ALU.add)
                b.vop("scalar_tensor_tensor", acc, Aap[:, base + 2:base + 514],
                      mixk[:, k * 3 + 2:k * 3 + 3], acc, ALU.mult, ALU.add)
                b.vop("tensor_copy", Aap[:, 16400 + k * 512:16400 + (k + 1) * 512], acc)
            vx_xm = b.val("vx")
            xm = [Aap[:, 16400 + k * 512:16400 + (k + 1) * 512] for k in range(8)]
            for o in range(2):
                ps = P0 if o == 0 else P1
                for k in range(8):
                    wv = [("vx", vx_xm), ("sc", b.val("sc"))] if k == 0 else ()
                    b.mm(ps[:, 0:TC], A1[:, 18432 + (o * 8 + k) * 128:
                                         18432 + (o * 8 + k + 1) * 128],
                         xm[k], start=(k == 0), stop=(k == 7), waits=wv)
                b.act(shb[:, o * 512:(o + 1) * 512], ps[:, 0:TC], AF.Tanh,
                      bias=bo1[:, o:o + 1], waits=[("mm", b.val("mm"))])
            scv = b.val("sc")
            for tb in range(4):
                wtb = [("vx", b.val("vx"))]
                if tb == 0:
                    wtb.append(("sc", scv))
                for mc in range(2):
                    b.mm(P3[:, 0:9], shb[:, mc * 512 + tb * 128:mc * 512 + (tb + 1) * 128],
                         wo2b[:, mc * 9:(mc + 1) * 9], start=(mc == 0), stop=(mc == 1),
                         waits=wtb if mc == 0 else ())
                wlg = [("mm", b.val("mm"))]
                if tb == 0:
                    wlg.append(("dma", b.val("dma")))
                b.vch("tensor_tensor", lgt[:, 0:9], P3[:, 0:9], bo2rep, ALU.add,
                      waits=wlg)
                b.vch("reduce_max", m1t, lgt[:, 0:9], AX.X)
                vv = b.val("vx")
                b.vch("tensor_scalar", msk1[:, 0:9], lgt[:, 0:9], m1t[:, 0:1], None,
                      ALU.is_equal, waits=[("vx", vv)])
                b.vch("tensor_mul", candt[:, 0:9], msk1[:, 0:9], iota9m)
                b.vch("tensor_scalar", candt[:, 0:9], candt[:, 0:9], 9.0, None, ALU.add)
                b.vch("tensor_reduce", i1t, candt[:, 0:9], AX.X, ALU.min)
                vv = b.val("vx")
                b.vch("tensor_scalar", msk1[:, 0:9], iota9, i1t[:, 0:1], None,
                      ALU.is_equal, waits=[("vx", vv)])
                vv = b.val("vx")
                b.mm(P3[0:9, 128:256], msk1[:, 0:9], identf, start=True, stop=True,
                     waits=[("vx", vv)])
                b.vch("tensor_copy", mask9_sb[0:9, tb * 128:(tb + 1) * 128],
                      P3[0:9, 128:256], waits=[("mm", b.val("mm"))])
            b.vch("tensor_copy", mask9b[0:9, 0:TC], mask9_sb[0:9, 0:TC],
                  waits=[("mm", b.val("mm"))])
            b.fence_v(mask9_sb[0:9, 0:1])
            b.fence_v(mask9_sb[0:9, TC - 1:TC])
            b.dma(mask9_dram.ap()[c * 9:(c + 1) * 9, :], mask9_sb[0:9, 0:TC],
                  waits=[("vx", b.val("vx"))])
            b.barrier()
            lw_tab = {0: b.dma(WP[:, 0:2048], W1_p[:, 0:2048],
                               waits=[("mm", b.val("mm"))], q="gpsimd")}
            for p in range(9):
                if p + 1 < 9:
                    lw_tab[p + 1] = b.dma(
                        WP[:, ((p + 1) % 2) * 2048:((p + 1) % 2) * 2048 + 2048],
                        W1_p[:, (p + 1) * 2048:(p + 2) * 2048],
                        waits=[("mm", b.val("mm"))], q="gpsimd")
                lw = lw_tab[p]
                lmk = b.dma(stf[:, 1536:2048],
                            mask9_dram.ap()[c * 9 + p:c * 9 + p + 1, :]
                            .partition_broadcast(128), waits=[("vx", b.val("vx"))])
                halfp = (p % 2) * 2048
                for oc in range(2):
                    ps = P0 if oc == 0 else P1
                    for k in range(8):
                        wv = ()
                        if k == 0:
                            if p == 0:
                                wv = [("gdma", lw), ("vx", vx_xm),
                                      ("sc", b.val("sc"))]
                            else:
                                wv = [("gdma", lw), ("vx", b.val("vx"))]
                        b.mm(ps[:, 0:TC],
                             WP[:, halfp + (oc * 8 + k) * 128:halfp + (oc * 8 + k + 1) * 128],
                             xm[k], start=(k == 0), stop=(k == 7), waits=wv)
                    b.vop("tensor_scalar", stf[:, 1024:1536],
                          ps[:, 0:TC], b1p[:, p * 2 + oc:p * 2 + oc + 1], None, ALU.add,
                          waits=[("mm", b.val("mm")), ("dma", lmk)])
                    b.vop("tensor_tensor",
                          Aap[:, 20992 + p * 1024 + oc * 512:20992 + p * 1024 + (oc + 1) * 512],
                          stf[:, 1024:1536],
                          stf[:, 1536:2048], ALU.mult)
            vx_paths = b.val("vx")
            for dc in range(8):
                ps = P0 if dc % 2 == 0 else P1
                first = True
                for p in range(9):
                    for mc in range(2):
                        wv = [("vx", b.val("vx") if dc >= 2 else vx_paths)] \
                            if first else ()
                        b.mm(ps[:, 0:TC],
                             A1[:, ((p * 2 + mc) * 8 + dc) * 128:
                                ((p * 2 + mc) * 8 + dc + 1) * 128],
                             Aap[:, 20992 + p * 1024 + mc * 512:
                                 20992 + p * 1024 + (mc + 1) * 512],
                             start=first, stop=False, waits=wv)
                        first = False
                b.mm(ps[:, 0:TC], b2bf[:, dc * 128:(dc + 1) * 128], mask9b[0:9, 0:TC],
                     start=False, stop=True)
                wcol = [("mm", b.val("mm"))]
                if dc == 0:
                    wcol.append(("gdma", b.val("gdma")))
                b.vop("tensor_add", xsf[:, dc * 512:(dc + 1) * 512], ps[:, 0:TC],
                      Aap[:, dc * 2050 + 1 + c * TC:dc * 2050 + 1 + (c + 1) * TC],
                      waits=wcol)
            if DEBUG and c == 0:
                for k in range(8):
                    b.dma(dbg["dbg_xm"][k * 128:(k + 1) * 128, 0:512],
                          Aap[:, 16400 + k * 512:16400 + (k + 1) * 512])
                b.barrier()
            tiles = [xsf[:, o * 512:(o + 1) * 512] for o in range(8)]
            m_rep, r_rep = ln_stats(tiles, TC, w0=True)
            vv = b.val("vx")
            for o in range(8):
                wsub = [("vx", vv), ("dma", x3mk)] if o == 0 else ()
                b.vop("tensor_sub", tiles[o], tiles[o], m_rep, waits=wsub)
                b.vch("tensor_mul", tiles[o], tiles[o], r_rep)
                b.vch("tensor_scalar", tiles[o], tiles[o], l1fg[:, o:o + 1],
                      l1fb[:, o:o + 1], ALU.mult, ALU.add)
                b.vch("tensor_add", tiles[o], tiles[o],
                      A1f[:, 10240 + o * 512:10240 + (o + 1) * 512])
                b.fence_v(tiles[o][:, 511:512])
                b.dma(out_fm[o * 128:(o + 1) * 128, c * TC:(c + 1) * TC], tiles[o],
                      waits=[("vx", b.val("vx"))], q="gpsimd")
        b.barrier()
        if DEBUG:
            b.dma(dbg["dbg_m9"], mask9_dram.ap())
            b.barrier()
    else:
        b.dma(out_fm, xres_A.ap() if PHASE_LIMIT >= 3 else x_ext[:, 3:3 + T], q="gpsimd")

    b.barrier()

    with nc.Block() as block:
        @block.sync
        def _(e):
            for f in b.q["sync"]:
                f(e)

        @block.gpsimd
        def _(e):
            for f in b.q["gpsimd"]:
                f(e)

        @block.tensor
        def _(e):
            for f in b.q["tensor"]:
                f(e)

        @block.scalar
        def _(e):
            for f in b.q["scalar"]:
                f(e)

        @block.vector
        def _(e):
            for f in b.q["vector"]:
                f(e)

    stack.close()
    return nc



def kernel(**inputs):
    maps = prep_inputs(inputs)
    nc = bass.Bass("TRN2", target_bir_lowering=False, debug=False)
    build(nc)
    res = run_bass_kernel_spmd(nc, maps, core_ids=list(range(8)))
    out = np.empty((4, 4096, 1024), np.float32)
    for c in range(8):
        b_, h = c // 2, c % 2
        out[b_, h * T:(h + 1) * T, :] = res.results[c]["out_fm"].T
    return out


# version 31 (final)
# speedup vs baseline: 1.0095x; 1.0056x over previous
import numpy as np
import concourse.bass as bass
import concourse.mybir as mybir
from concourse.bass_utils import run_bass_kernel_spmd
from contextlib import ExitStack

F32 = mybir.dt.float32
BF16 = mybir.dt.bfloat16
I32 = mybir.dt.int32
AF = mybir.ActivationFunctionType
ALU = mybir.AluOpType
AX = mybir.AxisListType

T = 2048
TC = 512
TC4 = 256
NCH = T // TC
NCH4 = T // TC4
NEG_BIG = -1e30

PHASE_LIMIT = 99
DEBUG = False
SCAN_N = 16
P4STAGE = 99
P5STAGE = 99
TTADD = True
NHCORR = 16
CONSTLHS = False
CONSTRHS = False
TTMODE = 0
FWD_MM = True
BACK_MM = True
PAIRS = [[0, 1], [2, 3], [4, 5], [6, 7]]


def _pack(w, cs=128):
    Din, Dout = w.shape
    return np.ascontiguousarray(
        w.reshape(Din // 128, 128, Dout // cs, cs).transpose(1, 2, 0, 3).reshape(128, -1))


def _packv(v, P=128):
    n = v.shape[0]
    return np.ascontiguousarray(v.reshape(n // P, P).T)


def prep_inputs(I):
    f32 = np.float32
    x = np.asarray(I["L0_attn_x"], f32)
    S = {}
    S["W_in_p"] = _pack(np.ascontiguousarray(I["L0_attn_W_in"].T))
    wxt = np.ascontiguousarray(I["L0_attn_W_x"].T)
    S["W_x_p"] = _pack(wxt[:, 0:2048])
    S["W_xbc_p"] = _pack(wxt[:, 2048:2080], cs=32)
    S["W_dt_p"] = _pack(np.ascontiguousarray(I["L0_attn_W_dt"].T))
    S["W_out_p"] = _pack(np.ascontiguousarray(I["L0_attn_W_out"].T))
    cw = I["L0_attn_conv_w"].reshape(2048, 4)
    S["convw_sb"] = np.ascontiguousarray(cw.reshape(16, 128, 4).transpose(1, 0, 2).reshape(128, 64))
    S["convb_sb"] = _packv(I["L0_attn_conv_b"])
    S["bdt_sb"] = _packv(I["L0_attn_b_dt"])
    S["D_sb"] = _packv(I["L0_attn_D"])
    S["l0a_g"] = _packv(I["L0_attn_ln_g"]); S["l0a_b"] = _packv(I["L0_attn_ln_b"])
    S["enc_p"] = _pack(np.asarray(I["L0_ffn_enc_w"], f32))
    S["enc_b_sb"] = _packv(I["L0_ffn_enc_b"])
    S["s2c_p"] = _pack(np.asarray(I["L0_ffn_s2c_w"], f32), cs=64)
    S["s2c_b_sb"] = I["L0_ffn_s2c_b"].reshape(64, 1).astype(f32)
    S["r_w1"] = np.asarray(I["L0_ffn_r_w1"], f32)
    S["r_b1_sb"] = I["L0_ffn_r_b1"].reshape(64, 1).astype(f32)
    S["r_w2"] = np.asarray(I["L0_ffn_r_w2"], f32)
    S["r_b2_sb"] = I["L0_ffn_r_b2"].reshape(8, 1).astype(f32)
    S["e_w1_p"] = np.ascontiguousarray(
        I["L0_ffn_e_w1"].reshape(8, 64, 8, 128).transpose(1, 0, 2, 3).reshape(64, 8192))
    S["e_b1p"] = np.ascontiguousarray(
        I["L0_ffn_e_b1"].reshape(8, 8, 128).transpose(2, 0, 1).reshape(128, 64))
    S["e_w2_p"] = np.ascontiguousarray(
        I["L0_ffn_e_w2"].reshape(8, 8, 128, 64).transpose(2, 0, 1, 3).reshape(128, 4096))
    S["e_b2p"] = np.ascontiguousarray(I["L0_ffn_e_b2"].T)
    S["c2s_p"] = np.ascontiguousarray(
        I["L0_ffn_c2s_w"].reshape(64, 16, 128).transpose(0, 1, 2).reshape(64, 2048))
    S["c2s_b_sb"] = _packv(I["L0_ffn_c2s_b"])
    S["dec_p"] = _pack(np.asarray(I["L0_ffn_dec_w"], f32))
    S["dec_b_sb"] = _packv(I["L0_ffn_dec_b"])
    S["l0f_g"] = _packv(I["L0_ffn_ln_g"]); S["l0f_b"] = _packv(I["L0_ffn_ln_b"])
    S["Wq_p"] = _pack(np.asarray(I["L1_attn_Wq"], f32))
    S["Wk_p"] = _pack(np.asarray(I["L1_attn_Wk"], f32))
    S["Wv_p"] = np.ascontiguousarray(
        I["L1_attn_Wv"].reshape(8, 128, 1024).transpose(1, 0, 2).reshape(128, 8192))
    S["Wo_p"] = _pack(np.asarray(I["L1_attn_Wo"], f32))
    S["bo_sb"] = _packv(I["L1_attn_bo"])
    mk = I["L1_ffn_mix_kernel"]
    S["mixk_sb"] = np.ascontiguousarray(mk.T.reshape(8, 128, 3).transpose(1, 0, 2).reshape(128, 24))
    S["W1_p"] = np.ascontiguousarray(
        I["L1_ffn_W1"].reshape(9, 8, 128, 2, 128).transpose(2, 0, 3, 1, 4).reshape(128, 18432))
    S["b1p"] = np.ascontiguousarray(
        I["L1_ffn_b1"].reshape(9, 2, 128).transpose(2, 0, 1).reshape(128, 18))
    S["W2_p"] = np.ascontiguousarray(
        I["L1_ffn_W2"].reshape(9, 2, 128, 8, 128).transpose(2, 0, 1, 3, 4).reshape(128, 18432))
    S["b2f"] = np.asarray(I["L1_ffn_b2"], f32)
    S["Wo1_p"] = _pack(np.asarray(I["L1_ffn_Wo1"], f32))
    S["bo1_sb"] = _packv(I["L1_ffn_bo1"])
    S["Wo2_p"] = np.ascontiguousarray(
        I["L1_ffn_Wo2"].reshape(2, 128, 9).transpose(1, 0, 2).reshape(128, 18))
    S["bo2_sb"] = I["L1_ffn_bo2"].reshape(9, 1).astype(f32)
    S["l1f_g"] = _packv(I["L1_ffn_gamma"]); S["l1f_b"] = _packv(I["L1_ffn_beta"])
    S["ident"] = np.eye(128, dtype=f32)
    tj = np.arange(128)[:, None]; ti = np.arange(128)[None, :]
    S["cmask"] = (tj <= ti).astype(f32)
    S["ones_row"] = np.ones((1, 128), f32)
    S["inv1024"] = np.full((128, 1), 1.0 / 1024.0, f32)
    S["iota8"] = np.broadcast_to(np.arange(8, dtype=f32), (128, 8)).copy()
    S["iota8m"] = S["iota8"] - 8.0
    S["iota9"] = np.broadcast_to(np.arange(9, dtype=f32), (128, 9)).copy()
    S["rb2_rep"] = np.broadcast_to(I["L0_ffn_r_b2"].astype(f32), (128, 8)).copy()
    S["bo2_rep"] = np.broadcast_to(I["L1_ffn_bo2"].astype(f32), (128, 9)).copy()
    S["iota9m"] = S["iota9"] - 9.0

    maps = []
    p_ = np.arange(128)
    for c in range(8):
        b, h = c // 2, c % 2
        m = dict(S)
        xT = np.ascontiguousarray(x[b].T)
        sl = xT[:, h * T:(h + 1) * T]
        halo = xT[:, h * T - 3:h * T] if h == 1 else np.zeros((1024, 3), f32)
        m["x_ext"] = np.ascontiguousarray(np.concatenate([halo, sl], axis=1))
        m["is_odd"] = np.full((128, 1), float(h), f32)
        m["is_even"] = np.full((128, 1), float(1 - h), f32)
        m["odd64"] = np.full((64, 1), float(h), f32)
        al = I["L0_attn_A_log"]
        m["A_pack"] = np.ascontiguousarray(
            al[1024 * h:1024 * (h + 1)].reshape(8, 128, 16).transpose(1, 0, 2).reshape(128, 128))
        gdu = np.empty((128, 32), np.int32)
        for q in range(2):
            for k in range(8):
                gdu[:, q * 16 + k] = q * 2048 + h * 1024 + k * 128 + p_
                gdu[:, q * 16 + 8 + k] = q * 2080 + h * 1024 + k * 128 + p_
        m["gdu_idx"] = gdu
        gy = np.empty((128, 64), np.int32)
        for k in range(16):
            for cc in range(4):
                gy[:, k * 4 + cc] = (2048 * h + 128 * k + p_) * 4 + cc
        m["gy_idx"] = gy
        maps.append(m)
    return maps


class B:
    ENGINES = ("sync", "gpsimd", "tensor", "scalar", "vector")

    def __init__(self, nc, stack):
        self.nc = nc
        self.q = {e: [] for e in self.ENGINES}
        self.sem = {}
        for s in ("dma", "gdma", "mm", "sc", "vx", "cc", "st", "gp"):
            self.sem[s] = [stack.enter_context(nc.semaphore(s)), 0]

    def val(self, s):
        return self.sem[s][1]

    def emit(self, eng, fn, waits=(), inc=None):
        tgt = None
        if inc is not None:
            self.sem[inc[0]][1] += inc[1]
            tgt = self.sem[inc[0]][1]
        sems = self.sem

        def op(e, fn=fn, waits=tuple(waits), inc=inc):
            for s, v in waits:
                if v > 0:
                    e.wait_ge(sems[s][0], v)
            r = fn(e)
            if inc is not None:
                r.then_inc(sems[inc[0]][0], inc[1])
        self.q[eng].append(op)
        return tgt

    def barrier(self):
        snap = {s: c for s, (hh, c) in self.sem.items()}
        sems = self.sem
        for e in self.ENGINES:
            def bop(eng, snap=dict(snap)):
                for s, v in snap.items():
                    if v > 0:
                        eng.wait_ge(sems[s][0], v)
            self.q[e].append(bop)

    def dma(self, out, in_, waits=(), q="sync"):
        s = "dma" if q == "sync" else "gdma"
        return self.emit(q, lambda e, o=out, i=in_: e.dma_start(out=o, in_=i),
                         waits=waits, inc=(s, 16))

    def gather(self, out, in_, idx, waits=()):
        return self.emit(
            "gpsimd",
            lambda e, o=out, i=in_, x=idx: e.indirect_dma_start(
                out=o, out_offset=None, in_=i,
                in_offset=bass.IndirectOffsetOnAxis(ap=x, axis=0)),
            waits=waits, inc=("gdma", 16))

    def mm(self, out, lhsT, rhs, start, stop, waits=(), inc=True, skip_gc=False):
        return self.emit(
            "tensor",
            lambda e, o=out, l=lhsT, r=rhs, s1=start, s2=stop, sg=skip_gc:
                e.matmul(o, l, r, start=s1, stop=s2, skip_group_check=sg),
            waits=waits, inc=("mm", 1) if inc else None)

    def tr(self, out, in_, ident, waits=()):
        r = self.emit("tensor",
                      lambda e, o=out, i=in_, d=ident: e.transpose(o, i, d),
                      waits=waits, inc=("mm", 1))
        self.emit("tensor",
                  lambda e: e.matmul(self.drain_ps, self.drain_lhs, self.drain_rhs,
                                     start=True, stop=True, skip_group_check=True),
                  inc=("mm", 1))
        return r

    def act(self, out, in_, func, bias=0.0, scale=1.0, waits=()):
        return self.emit(
            "scalar",
            lambda e, o=out, i=in_, f=func, bb=bias, s=scale:
                e.activation(out=o, in_=i, func=f, bias=bb, scale=s),
            waits=waits, inc=("sc", 1))

    def fence_v(self, srcap):
        return self.vop("tensor_copy", self.ftile[0:srcap.shape[0], 0:1], srcap,
                        waits=[("vx", self.val("vx"))])

    def fence_s(self, srcap):
        return self.emit(
            "scalar",
            lambda e, o=self.ftile[0:srcap.shape[0], 1:2], i=srcap:
                e.activation(out=o, in_=i, func=AF.Copy),
            inc=("sc", 1))

    def vch(self, name, *args, waits=(), **kw):
        return self.vop(name, *args, waits=tuple(waits) + (("vx", self.val("vx")),), **kw)

    def vop(self, name, *args, waits=(), **kw):
        return self.emit(
            "vector",
            lambda e, n=name, a=args, k=kw: getattr(e, n)(*a, **k),
            waits=waits, inc=("vx", 1))


def build(nc):
    stack = ExitStack()
    b = B(nc, stack)
    dt = nc.dram_tensor
    EI = lambda n, s, d=F32: dt(n, s, d, kind="ExternalInput").ap()

    x_ext = EI("x_ext", [1024, 2051])
    W_in_p = EI("W_in_p", [128, 32768]); W_x_p = EI("W_x_p", [128, 32768])
    W_xbc_p = EI("W_xbc_p", [128, 512]); W_dt_p = EI("W_dt_p", [128, 32768])
    W_out_p = EI("W_out_p", [128, 16384])
    convw_i = EI("convw_sb", [128, 64]); convb_i = EI("convb_sb", [128, 16])
    bdt_i = EI("bdt_sb", [128, 16]); A_pack = EI("A_pack", [128, 128])
    D_i = EI("D_sb", [128, 16]); l0ag_i = EI("l0a_g", [128, 8]); l0ab_i = EI("l0a_b", [128, 8])
    enc_p = EI("enc_p", [128, 16384]); encb_i = EI("enc_b_sb", [128, 16])
    s2c_p = EI("s2c_p", [128, 1024]); s2cb_i = EI("s2c_b_sb", [64, 1])
    rw1_i = EI("r_w1", [64, 64]); rb1_i = EI("r_b1_sb", [64, 1])
    rw2_i = EI("r_w2", [64, 8]); rb2_i = EI("r_b2_sb", [8, 1])
    ew1_p = EI("e_w1_p", [64, 8192]); eb1_i = EI("e_b1p", [128, 64])
    ew2_p = EI("e_w2_p", [128, 4096]); eb2_i = EI("e_b2p", [64, 8])
    c2s_p = EI("c2s_p", [64, 2048]); c2sb_i = EI("c2s_b_sb", [128, 16])
    dec_p = EI("dec_p", [128, 16384]); decb_i = EI("dec_b_sb", [128, 8])
    l0fg_i = EI("l0f_g", [128, 8]); l0fb_i = EI("l0f_b", [128, 8])
    Wq_p = EI("Wq_p", [128, 8192]); Wk_p = EI("Wk_p", [128, 8192])
    Wv_p = EI("Wv_p", [128, 8192]); Wo_p = EI("Wo_p", [128, 8192])
    bo_i = EI("bo_sb", [128, 8])
    mixk_i = EI("mixk_sb", [128, 24])
    W1_p = EI("W1_p", [128, 18432]); b1p_i = EI("b1p", [128, 18])
    W2_p = EI("W2_p", [128, 18432]); b2f = EI("b2f", [9, 1024])
    Wo1_p = EI("Wo1_p", [128, 2048]); bo1_i = EI("bo1_sb", [128, 2])
    Wo2_p = EI("Wo2_p", [128, 18]); bo2_i = EI("bo2_sb", [9, 1])
    l1fg_i = EI("l1f_g", [128, 8]); l1fb_i = EI("l1f_b", [128, 8])
    ident_i = EI("ident", [128, 128]); cmask_i = EI("cmask", [128, 128])
    onesr_i = EI("ones_row", [1, 128]); inv1024_i = EI("inv1024", [128, 1])
    iota8_i = EI("iota8", [128, 8]); iota8m_i = EI("iota8m", [128, 8])
    iota9_i = EI("iota9", [128, 9]); iota9m_i = EI("iota9m", [128, 9])
    isodd_i = EI("is_odd", [128, 1]); iseven_i = EI("is_even", [128, 1])
    odd64_i = EI("odd64", [64, 1])
    rb2r_i = EI("rb2_rep", [128, 8]); bo2r_i = EI("bo2_rep", [128, 9])
    gdu_i = dt("gdu_idx", [128, 32], I32, kind="ExternalInput").ap()
    gy_i = dt("gy_idx", [128, 64], I32, kind="ExternalInput").ap()

    out_fm = dt("out_fm", [1024, T], F32, kind="ExternalOutput").ap()

    cc_pre_in = dt("cc_pre_in", [4128, T], BF16, kind="Internal")
    cc_pre_outD = dt("cc_pre_outD", [4096, T], BF16, kind="Internal")
    cc_pre_outU = dt("cc_pre_outU", [4160, T], BF16, kind="Internal")
    cc_y_in = dt("cc_y_in", [2048, 2048], BF16, kind="Internal")
    cc_y_out = dt("cc_y_out", [4096, 2048], BF16, kind="Internal")
    cc_att_in = dt("cc_att_in", [64, 1040], F32, kind="Internal")
    cc_att_out = dt("cc_att_out", [128, 1040], F32, kind="Internal")
    cc_mix_in = dt("cc_mix_in", [128, 16], BF16, kind="Internal")
    cc_mix_out = dt("cc_mix_out", [256, 16], BF16, kind="Internal")
    z_dram = dt("z_dram", [2048, T], BF16, kind="Internal")
    ln_dram = dt("ln_dram", [1024, T], BF16, kind="Internal")
    xres_A = dt("xres_A", [1024, T], F32, kind="Internal")
    xres_B = dt("xres_B", [1024, T], F32, kind="Internal")
    comb_dram = dt("comb_dram", [64, TC4], F32, kind="Internal")
    mask9_dram = dt("mask9_dram", [36, TC], F32, kind="Internal")
    q_dram = dt("q_dram", [1024, T], BF16, kind="Internal")
    k_dram = dt("k_dram", [1024, T], BF16, kind="Internal")
    v_dram = dt("v_dram", [T, 1024], BF16, kind="Internal")
    num_dram = dt("num_dram", [1024, T], F32, kind="Internal")
    den_dram = dt("den_dram", [16, T], F32, kind="Internal")
    attn_dram = dt("attn_dram", [1024, T], BF16, kind="Internal")

    dbg = {}
    if DEBUG:
        for nm, sh, ddt in [("dbg_pre", [8256, T], BF16), ("dbg_y", [2048, 4096], BF16),
                            ("dbg_x1", [1024, T], F32), ("dbg_x2", [1024, T], F32),
                            ("dbg_x3", [1024, T], F32), ("dbg_z", [2048, T], BF16),
                            ("dbg_att", [1024, T], BF16),
                            ("dbg_xsh", [2048, 3], BF16), ("dbg_lnh", [1024, 3], BF16),
                            ("dbg_yin", [1024, 4096], BF16), ("dbg_comb", [64, TC4], F32), ("dbg_m9", [36, TC], F32), ("dbg_xm", [1024, TC], BF16),
                            ("dbg_lgt", [128, 8], F32), ("dbg_cmb", [128, 8], F32),
                            ("dbg_lgr", [8, TC4], F32), ("dbg_rh", [64, TC4], F32),
                            ("dbg_cont", [64, TC4], F32)]:
            dbg[nm] = dt(nm, sh, ddt, kind="ExternalOutput").ap()

    sb = lambda n, s, d: stack.enter_context(nc.sbuf_tensor(n, s, d))
    bigA_t = sb("s_bigA", [128, 32768], BF16)
    big1_t = sb("s_big1", [128, 32768], BF16)
    xs_t = sb("s_xs_c", [128, 8240], BF16)
    wp_t = sb("s_wping", [128, 4096], BF16)
    st_t = sb("s_stag", [128, 8192], BF16)
    A_sb = sb("s_A_sb", [128, 128], F32).ap()
    carry = sb("s_carry", [128, 128], F32).ap()
    convw = sb("s_convw", [128, 64], F32).ap(); convb = sb("s_convb", [128, 16], F32).ap()
    bdt = sb("s_bdt", [128, 16], F32).ap(); D_s = sb("s_D_s", [128, 16], F32).ap()
    l0ag = sb("s_l0ag", [128, 8], F32).ap(); l0ab = sb("s_l0ab", [128, 8], F32).ap()
    encb = sb("s_encb", [128, 16], F32).ap(); s2cb = sb("s_s2cb", [64, 1], F32).ap()
    rw1 = sb("s_rw1", [64, 64], BF16).ap(); rb1 = sb("s_rb1", [64, 1], F32).ap()
    rw2 = sb("s_rw2", [64, 8], BF16).ap(); rb2 = sb("s_rb2", [8, 1], F32).ap()
    eb1 = sb("s_eb1", [128, 64], F32).ap(); eb2 = sb("s_eb2", [64, 8], F32).ap()
    c2sb = sb("s_c2sb", [128, 16], F32).ap(); decb = sb("s_decb", [128, 8], F32).ap()
    l0fg = sb("s_l0fg", [128, 8], F32).ap(); l0fb = sb("s_l0fb", [128, 8], F32).ap()
    bo_s = sb("s_bo_s", [128, 8], F32).ap(); mixk = sb("s_mixk", [128, 24], F32).ap()
    b1p = sb("s_b1p_s", [128, 18], F32).ap(); bo1 = sb("s_bo1", [128, 2], F32).ap()
    bo2 = sb("s_bo2", [9, 1], F32).ap()
    l1fg = sb("s_l1fg", [128, 8], F32).ap(); l1fb = sb("s_l1fb", [128, 8], F32).ap()
    b2bf = sb("s_b2bf", [9, 1024], BF16).ap()
    b.ftile = sb("s_ftile", [128, 2], F32).ap()
    rb2rep = sb("s_rb2rep", [128, 8], F32).ap()
    bo2rep = sb("s_bo2rep", [128, 9], F32).ap()
    wo2b = sb("s_wo2b", [128, 18], BF16).ap()
    identf = sb("s_identf", [128, 128], F32).ap(); identb = sb("s_identb", [128, 128], BF16).ap()
    cmask = sb("s_cmask", [128, 128], F32).ap()
    onesr = sb("s_onesr", [1, 128], F32).ap(); inv1024 = sb("s_inv1024", [128, 1], F32).ap()
    iota8 = sb("s_iota8", [128, 8], F32).ap(); iota8m = sb("s_iota8m", [128, 8], F32).ap()
    iota9 = sb("s_iota9", [128, 9], F32).ap(); iota9m = sb("s_iota9m", [128, 9], F32).ap()
    isodd = sb("s_isodd", [128, 1], F32).ap(); iseven = sb("s_iseven", [128, 1], F32).ap()
    odd64 = sb("s_odd64", [64, 1], F32).ap()
    gdu_sb = sb("s_gdu_sb", [128, 32], I32).ap()
    gy_sb = sb("s_gy_sb", [128, 64], I32).ap()
    mean_r = sb("s_mean_r", [1, TC], F32).ap(); msq_r = sb("s_msq_r", [1, TC], F32).ap()
    tmp_r = sb("s_tmp_r", [1, TC], F32).ap(); rstd_r = sb("s_rstd_r", [1, TC], F32).ap()
    vaug = sb("s_vaug", [128, 130], BF16).ap()
    ktok = sb("s_ktok", [128, 64], BF16).ap()
    amb = sb("s_am_bf", [128, 128], BF16).ap()
    S_f = sb("s_S_sb", [64, 65], F32).ap()
    saug = sb("s_saug", [64, 65], BF16).ap()
    sprev_f = sb("s_sprev_f", [64, 65], F32).ap()
    sprev = sb("s_sprev", [64, 65], BF16).ap()
    sprev64 = sb("s_sprev64", [64, 64], BF16).ap()
    mix_sb = sb("s_mix_sb", [128, 16], BF16).ap()
    vaug2 = sb("s_vaug2", [128, 130], BF16).ap()
    ktok2 = sb("s_ktok2", [128, 64], BF16).ap()
    amb2 = sb("s_amb2", [128, 128], BF16).ap()
    S_f2 = sb("s_S_f2", [64, 65], F32).ap()
    saug2 = sb("s_saug2", [64, 65], BF16).ap()
    onesb = sb("s_onesb", [1, 128], BF16).ap()
    mixh = sb("s_mixh", [128, 32], BF16).ap()
    spk1 = sb("s_spk1", [64, 1], BF16).ap()
    den_s = sb("s_den_s", [1, T], F32).ap()
    lgt = sb("s_lgt_t", [128, 16], F32).ap()
    m1t = sb("s_m1_t", [128, 1], F32).ap(); m2t = sb("s_m2_t", [128, 1], F32).ap()
    i1t = sb("s_i1_t", [128, 1], F32).ap()
    candt = sb("s_cand_t", [128, 16], F32).ap()
    msk1 = sb("s_msk1", [128, 16], F32).ap(); msk2 = sb("s_msk2", [128, 16], F32).ap()
    w1t = sb("s_w1_t", [128, 1], F32).ap(); w2t = sb("s_w2_t", [128, 1], F32).ap()
    logit_sb = sb("s_logit_sb", [9, TC], F32).ap()
    comb_sb = sb("s_comb_sb", [8, TC4], F32).ap()
    mask9_sb = sb("s_mask9_sb", [9, TC], F32).ap()
    mask9b = sb("s_mask9_bf", [9, TC], BF16).ap()
    cont_bf = sb("s_cont_bf", [64, TC4], BF16).ap()
    rh_bf = sb("s_rh_bf", [64, TC4], BF16).ap()
    moe_f = sb("s_moe_f", [64, TC4], F32).ap()
    eo_f = sb("s_eo_f", [64, TC4], F32).ap()
    moe_bf = sb("s_moe_bf", [64, TC4], BF16).ap()
    comb_rep = sb("s_comb_rep", [64, TC4], F32).ap()
    shb = sb("s_sh_bf", [128, 1024], BF16).ap()

    Aap = bigA_t.ap(); A1 = big1_t.ap(); XS = xs_t.ap(); WP = wp_t.ap(); ST = st_t.ap()
    Af = Aap.bitcast(F32); A1f = A1.bitcast(F32); big1f = A1f
    xsf = XS.bitcast(F32); wpf = WP.bitcast(F32); stf = ST.bitcast(F32)
    den_s2 = wpf[0:1, 0:T]

    pt = lambda n, s, d: stack.enter_context(nc.psum_tensor(n, s, d))
    P0 = pt("P0", [128, 512], F32).ap()
    P1 = pt("P1", [128, 512], F32).ap()
    P2 = pt("P2", [128, 512], F32).ap()
    P3 = pt("P3", [128, 512], F32).ap()
    P4 = pt("P4", [128, 512], F32).ap()
    PT = pt("PTb", [128, 1024], BF16).ap()

    b.drain_ps = P4[0:1, 384:512]
    b.drain_lhs = inv1024
    b.drain_rhs = identf
    for dst, src in [(A_sb, A_pack), (convw, convw_i), (convb, convb_i),
                     (bdt, bdt_i), (D_s, D_i), (l0ag, l0ag_i), (l0ab, l0ab_i),
                     (encb, encb_i), (s2cb, s2cb_i), (rb1, rb1_i), (rb2, rb2_i),
                     (eb1, eb1_i), (eb2, eb2_i), (c2sb, c2sb_i), (decb, decb_i),
                     (l0fg, l0fg_i), (l0fb, l0fb_i), (bo_s, bo_i), (mixk, mixk_i),
                     (b1p, b1p_i), (bo1, bo1_i), (bo2, bo2_i), (l1fg, l1fg_i),
                     (l1fb, l1fb_i), (identf, ident_i), (cmask, cmask_i),
                     (onesr, onesr_i), (inv1024, inv1024_i), (iota8, iota8_i),
                     (iota8m, iota8m_i), (iota9, iota9_i), (iota9m, iota9m_i),
                     (isodd, isodd_i), (iseven, iseven_i), (odd64, odd64_i),
                     (gdu_sb, gdu_i), (gy_sb, gy_i), (rb2rep, rb2r_i), (bo2rep, bo2r_i)]:
        b.dma(dst, src)
    b.dma(identb, ident_i, q="gpsimd")
    b.dma(b2bf, b2f, q="gpsimd")
    b.dma(rw1, rw1_i, q="gpsimd")
    b.dma(rw2, rw2_i, q="gpsimd")
    b.dma(onesb, onesr_i, q="gpsimd")
    w0 = b.val("dma")
    b.act(A_sb, A_sb, AF.Exp, waits=[("dma", w0)])
    b.emit("scalar", lambda e: e.mul(A_sb, A_sb, -1.0),
           waits=[("sc", b.val("sc"))], inc=("sc", 1))
    b.barrier()

    def ln_stats(tiles, W, w0=False, extra=()):
        vx0 = b.val("vx")
        mm0 = b.val("mm")
        sc0 = b.val("sc")
        for k in range(8):
            wmm = list(extra) + ([("vx", vx0), ("sc", sc0)] if w0 else []) \
                if k == 0 else ()
            b.mm(P4[0:1, 0:W], inv1024, tiles[k], start=(k == 0), stop=(k == 7),
                 waits=wmm)
        for k in range(8):
            sq = wpf[:, (k % 2) * 512:(k % 2) * 512 + W]
            wv = [("mm", b.val("mm"))] if k >= 2 else \
                (list(extra) + [("vx", vx0), ("mm", mm0)] if (w0 and k == 0) else ())
            b.act(sq, tiles[k], AF.Square, waits=wv)
            b.mm(P3[0:1, 0:W], inv1024, sq, start=(k == 0), stop=(k == 7),
                 waits=[("sc", b.val("sc"))], skip_gc=True)
        mmv = b.val("mm")
        b.vop("tensor_copy", mean_r[:, 0:W], P4[0:1, 0:W], waits=[("mm", mmv)])
        b.vop("tensor_copy", msq_r[:, 0:W], P3[0:1, 0:W])
        b.vch("tensor_mul", tmp_r[:, 0:W], mean_r[:, 0:W], mean_r[:, 0:W])
        b.vch("tensor_sub", tmp_r[:, 0:W], msq_r[:, 0:W], tmp_r[:, 0:W])
        b.vch("tensor_scalar", tmp_r[:, 0:W], tmp_r[:, 0:W], 1e-5, None, ALU.add)
        vv = b.val("vx")
        b.act(tmp_r[:, 0:W], tmp_r[:, 0:W], AF.Sqrt, waits=[("vx", vv)])
        b.vop("reciprocal", rstd_r[:, 0:W], tmp_r[:, 0:W], waits=[("sc", b.val("sc"))])
        vv = b.val("vx")
        b.mm(P2[:, 0:W], onesr, mean_r[:, 0:W], start=True, stop=True, waits=[("vx", vv)])
        b.vop("tensor_copy", stf[:, 0:W], P2[:, 0:W], waits=[("mm", b.val("mm"))])
        vv = b.val("vx")
        b.mm(P2[:, 0:W], onesr, rstd_r[:, 0:W], start=True, stop=True, waits=[("vx", vv)])
        b.vop("tensor_copy", stf[:, 512:512 + W], P2[:, 0:W], waits=[("mm", b.val("mm"))])
        return stf[:, 0:W], stf[:, 512:512 + W]

    if PHASE_LIMIT >= 1:
        chunksA = [(0, 3)] + [(3 + c * TC, 3 + (c + 1) * TC) for c in range(NCH)]

        def sa_buf(i, k, W):
            if i % 2 == 0:
                return xsf[:, k * 512:k * 512 + W]
            return Af[:, 8204 + k * 512:8204 + k * 512 + W]

        def sa_loads(i, w):
            a0, a1 = chunksA[i]
            for k in range(8):
                b.dma(sa_buf(i, k, a1 - a0), x_ext[k * 128:(k + 1) * 128, a0:a1],
                      waits=w if k == 0 else ())
            return b.val("dma")

        sa_mk = {0: sa_loads(0, ())}
        for ci, (a0, a1) in enumerate(chunksA):
            W = a1 - a0
            tiles = [sa_buf(ci, k, W) for k in range(8)]
            if ci + 1 < len(chunksA):
                sa_mk[ci + 1] = sa_loads(ci + 1, [("vx", b.val("vx")),
                                                  ("sc", b.val("sc")),
                                                  ("mm", b.val("mm"))])
            m_rep, r_rep = ln_stats(tiles, W, w0=True, extra=[("dma", sa_mk[ci])])
            vv0 = b.val("vx")
            for k in range(8):
                wsub = [("vx", vv0), ("dma", sa_mk[ci])] if k == 0 else ()
                b.vop("tensor_sub", tiles[k], tiles[k], m_rep, waits=wsub)
                b.vch("tensor_tensor", Aap[:, k * 2051 + a0:k * 2051 + a1],
                      tiles[k], r_rep, ALU.mult)
                if a0 >= 3:
                    b.fence_v(Aap[:, k * 2051 + a1 - 1:k * 2051 + a1])
                    b.dma(ln_dram.ap()[k * 128:(k + 1) * 128, a0 - 3:a1 - 3],
                          Aap[:, k * 2051 + a0:k * 2051 + a1], waits=[("vx", b.val("vx"))])
        b.barrier()

        for c in range(NCH):
            e0 = 3 + c * TC
            lw_tab = {0: b.dma(WP[:, 0:1024], W_in_p[:, 0:1024],
                               waits=[("mm", b.val("mm"))], q="gpsimd")}
            for o in range(32):
                half = (o % 2) * 2048
                if o + 1 < 32:
                    lw_tab[o + 1] = b.dma(
                        WP[:, ((o + 1) % 2) * 2048:((o + 1) % 2) * 2048 + 1024],
                        W_in_p[:, (o + 1) * 1024:(o + 2) * 1024],
                        waits=[("mm", b.val("mm"))], q="gpsimd")
                lw = lw_tab[o]
                ps = P0 if o % 2 == 0 else P1
                if o < 16 and c == 0:
                    for k in range(8):
                        wv0 = [("gdma", lw), ("vx", b.val("vx"))] if k == 0 else ()
                        b.mm(P2[:, 0:3], WP[:, half + k * 128:half + (k + 1) * 128],
                             Aap[:, k * 2051:k * 2051 + 3], start=(k == 0), stop=(k == 7),
                             waits=wv0)
                    b.vop("tensor_copy", XS[:, o * 515:o * 515 + 3], P2[:, 0:3],
                          waits=[("mm", b.val("mm"))])
                for k in range(8):
                    wv = []
                    if k == 0:
                        wv = [("gdma", lw), ("vx", b.val("vx")), ("sc", b.val("sc"))]
                    b.mm(ps[:, 0:TC], WP[:, half + k * 128:half + (k + 1) * 128],
                         Aap[:, k * 2051 + e0:k * 2051 + e0 + TC],
                         start=(k == 0), stop=(k == 7), waits=wv)
                mmv = b.val("mm")
                if o < 16:
                    b.vop("tensor_copy", XS[:, o * 515 + 3:o * 515 + 515], ps[:, 0:TC],
                          waits=[("mm", mmv)])
                else:
                    kz = o - 16
                    zst = ST[:, (kz % 2) * 512:(kz % 2) * 512 + 512]
                    b.act(zst, ps[:, 0:TC], AF.Silu, waits=[("mm", mmv), ("dma", b.val("dma"))])
                    b.fence_s(zst[:, 511:512])
                    b.dma(z_dram.ap()[kz * 128:(kz + 1) * 128, c * TC:(c + 1) * TC], zst,
                          waits=[("sc", b.val("sc"))])
            b.barrier()
            if DEBUG and c == 0:
                for k in range(16):
                    b.dma(dbg["dbg_xsh"][k * 128:(k + 1) * 128, :], XS[:, k * 515:k * 515 + 3])
                for k in range(8):
                    b.dma(dbg["dbg_lnh"][k * 128:(k + 1) * 128, :], Aap[:, k * 2051:k * 2051 + 3])
                b.barrier()
            for k in range(16):
                acc = stf[:, 2048 + (k % 2) * 512:2048 + (k % 2) * 512 + 512]
                base = k * 515
                wv = [("sc", b.val("sc"))] if k >= 2 else ()
                b.vop("tensor_scalar", acc, XS[:, base:base + 512],
                      convw[:, k * 4:k * 4 + 1], None, ALU.mult, waits=wv)
                for j in range(1, 4):
                    b.vop("scalar_tensor_tensor", acc, XS[:, base + j:base + j + 512],
                          convw[:, k * 4 + j:k * 4 + j + 1], acc, ALU.mult, ALU.add)
                b.act(A1[:, k * 2048 + c * TC:k * 2048 + (c + 1) * TC], acc, AF.Silu,
                      bias=convb[:, k:k + 1], waits=[("vx", b.val("vx"))])
                b.fence_s(A1[:, k * 2048 + (c + 1) * TC - 1:k * 2048 + (c + 1) * TC])
                b.dma(cc_pre_in.ap()[2048 + k * 128:2048 + (k + 1) * 128, c * TC:(c + 1) * TC],
                      A1[:, k * 2048 + c * TC:k * 2048 + (c + 1) * TC],
                      waits=[("sc", b.val("sc"))])
                if c < NCH - 1:
                    b.vop("tensor_copy", XS[:, base:base + 3], XS[:, base + 512:base + 515],
                          waits=[("sc", b.val("sc"))])
            b.barrier()

        def wx_src(o):
            return W_x_p[:, o * 2048:(o + 1) * 2048] if o < 16 else W_xbc_p[:, 0:512]
        lw_tab = {0: b.dma(WP[:, 0:2048], wx_src(0),
                           waits=[("mm", b.val("mm"))], q="gpsimd")}
        for o in range(17):
            cols = 128 if o < 16 else 32
            half = (o % 2) * 2048
            if o + 1 < 17:
                ncols = 128 if o + 1 < 16 else 32
                lw_tab[o + 1] = b.dma(
                    WP[:, ((o + 1) % 2) * 2048:((o + 1) % 2) * 2048 + 16 * ncols],
                    wx_src(o + 1), waits=[("mm", b.val("mm"))], q="gpsimd")
            lw = lw_tab[o]
            for c in range(NCH):
                ps = P0 if (o * NCH + c) % 2 == 0 else P1
                for k in range(16):
                    wv = [("gdma", lw), ("vx", b.val("vx"))] if k == 0 else ()
                    b.mm(ps[0:cols, 0:TC], WP[:, half + k * cols:half + (k + 1) * cols],
                         A1[:, k * 2048 + c * TC:k * 2048 + (c + 1) * TC],
                         start=(k == 0), stop=(k == 15), waits=wv)
                mmv = b.val("mm")
                if o < 16:
                    b.vop("tensor_copy", Aap[:, o * 2048 + c * TC:o * 2048 + (c + 1) * TC],
                          ps[:, 0:TC], waits=[("mm", mmv)])
                else:
                    bc = ST[0:32, 0:512]
                    b.vop("tensor_copy", bc, ps[0:32, 0:TC],
                          waits=[("mm", mmv), ("dma", b.val("dma"))])
                    b.fence_v(bc[:, 511:512])
                    b.dma(cc_pre_in.ap()[4096:4128, c * TC:(c + 1) * TC], bc,
                          waits=[("vx", b.val("vx"))])
        b.barrier()
        b.emit("gpsimd",
               lambda e: e.collective_compute(
                   "AllGather", ALU.bypass, replica_groups=PAIRS,
                   ins=[cc_pre_in.ap()[2048:4128, :].opt()],
                   outs=[cc_pre_outU.ap().opt()]),
               waits=[("dma", b.val("dma")), ("gdma", b.val("gdma"))], inc=("cc", 1))
        ccU = b.val("cc")
        dpostD = b.val("dma")

        lw_tab = {0: b.dma(WP[:, 0:2048], W_dt_p[:, 0:2048],
                           waits=[("mm", b.val("mm"))], q="gpsimd")}
        for o in range(16):
            half = (o % 2) * 2048
            if o + 1 < 16:
                lw_tab[o + 1] = b.dma(
                    WP[:, ((o + 1) % 2) * 2048:((o + 1) % 2) * 2048 + 2048],
                    W_dt_p[:, (o + 1) * 2048:(o + 2) * 2048],
                    waits=[("mm", b.val("mm"))], q="gpsimd")
            lw = lw_tab[o]
            for c in range(NCH):
                ps = P0 if (o * NCH + c) % 2 == 0 else P1
                for k in range(16):
                    wv = []
                    if k == 0:
                        wv = [("gdma", lw), ("dma", b.val("dma")),
                              ("sc", b.val("sc")), ("vx", b.val("vx"))]
                    b.mm(ps[:, 0:TC], WP[:, half + k * 128:half + (k + 1) * 128],
                         Aap[:, k * 2048 + c * TC:k * 2048 + (c + 1) * TC],
                         start=(k == 0), stop=(k == 15), waits=wv)
                mmv = b.val("mm")
                sf1 = stf[:, 2048:2560]
                sf2 = stf[:, 2560:3072]
                bcol = bdt[:, o:o + 1]
                b.act(sf1, ps[:, 0:TC], AF.Abs, bias=bcol, waits=[("mm", mmv)])
                b.act(sf1, sf1, AF.Exp, scale=-1.0)
                b.act(sf1, sf1, AF.Ln, bias=1.0)
                b.act(sf2, ps[:, 0:TC], AF.Relu, bias=bcol)
                dl = ST[:, 6144:6656]
                b.vop("tensor_add", dl, sf1, sf2,
                      waits=[("sc", b.val("sc")), ("dma", b.val("dma"))])
                b.fence_v(dl[:, 511:512])
                b.dma(cc_pre_in.ap()[o * 128:(o + 1) * 128, c * TC:(c + 1) * TC], dl,
                      waits=[("vx", b.val("vx"))])
        pend0 = (b.dma(ST[:, 0:2048],
                       cc_pre_outU.ap()[2048:2049, :].partition_broadcast(128),
                       waits=[("cc", ccU), ("dma", dpostD)]),
                 b.dma(ST[:, 2048:4096],
                       cc_pre_outU.ap()[2064:2065, :].partition_broadcast(128)))
        b.emit("gpsimd",
               lambda e: e.collective_compute(
                   "AllGather", ALU.bypass, replica_groups=PAIRS,
                   ins=[cc_pre_in.ap()[0:2048, :].opt()],
                   outs=[cc_pre_outD.ap().opt()]),
               waits=[("dma", b.val("dma")), ("gdma", b.val("gdma"))], inc=("cc", 1))
        ccD = b.val("cc")
        if DEBUG:
            b.dma(dbg["dbg_pre"], cc_pre_out.ap())
            b.dma(dbg["dbg_z"], z_dram.ap())
            b.barrier()

    if PHASE_LIMIT >= 2:
        preD = cc_pre_outD.ap()[0:128, :]
        preU = cc_pre_outU.ap()[0:128, :]
        dA_read = {}

        def bslot(i):
            return ST[:, (i % 2) * 4096:(i % 2) * 4096 + 2048]

        def cslot(i):
            return ST[:, (i % 2) * 4096 + 2048:(i % 2) * 4096 + 4096]

        for q in range(2):
            def ld_bc(i, w, _q=q):
                return (("dma", b.dma(bslot(i),
                        cc_pre_outU.ap()[_q * 2080 + 2048 + i:_q * 2080 + 2049 + i, :]
                        .partition_broadcast(128), waits=w)),
                        ("dma", b.dma(cslot(i),
                         cc_pre_outU.ap()[_q * 2080 + 2064 + i:_q * 2080 + 2065 + i, :]
                         .partition_broadcast(128))))
            for k in range(8):
                wu = ()
                if k == 0:
                    wu = [("cc", ccD)] if q == 0 else [("vx", b.val("vx"))]
                b.gather(Aap[:, (8 + k) * 2048:(9 + k) * 2048], preU,
                         gdu_sb[:, q * 16 + 8 + k:q * 16 + 9 + k], waits=wu)
            gw = () if q == 0 else [("vx", b.val("vx")), ("sc", b.val("sc"))]
            for k in range(8):
                b.gather(Aap[:, k * 2048:(k + 1) * 2048], preD,
                         gdu_sb[:, q * 16 + k:q * 16 + k + 1], waits=gw if k == 0 else ())
            fv = b.dma(gdu_sb, gdu_i, q="gpsimd")
            for k in range(8):
                b.vop("tensor_mul", Aap[:, (8 + k) * 2048:(9 + k) * 2048],
                      Aap[:, k * 2048:(k + 1) * 2048],
                      Aap[:, (8 + k) * 2048:(9 + k) * 2048],
                      waits=[("gdma", fv)] if k == 0 else ())
            b.barrier()
            if q == 0:
                pend = {0: (("dma", pend0[0]), ("dma", pend0[1]))}
            else:
                pend = {0: ld_bc(0, ())}
            carry_gp = None
            for n in range(SCAN_N):
                (bsem, lb), (csem, lc) = pend.pop(n)
                if n + 1 < SCAN_N:
                    pend[n + 1] = ld_bc(n + 1, [("vx", b.val("vx"))])
                for k in range(8):
                    it = n * 8 + k
                    ping = it % 2
                    dAv = xsf[:, ping * 2048:(ping + 1) * 2048]
                    wv = []
                    if ping in dA_read:
                        wv = [("vx", dA_read[ping])]
                    b.act(dAv, Aap[:, k * 2048:(k + 1) * 2048], AF.Exp,
                          scale=A_sb[:, k * 16 + n:k * 16 + n + 1], waits=wv)
                    scv = b.val("sc")
                    dBu = WP[:, 2048:4096]
                    wv2 = [(bsem, lb)] if k == 0 else []
                    b.vop("tensor_mul", dBu, Aap[:, (8 + k) * 2048:(9 + k) * 2048],
                          bslot(n), waits=wv2)
                    h = WP[:, 0:2048]
                    init = 0.0 if q == 0 else carry[:, k * 16 + n:k * 16 + n + 1]
                    wsc = [("sc", scv)]
                    if q == 0 and carry_gp is not None:
                        wsc.append(("gp", carry_gp))
                    b.vop("tensor_tensor_scan", h, dAv, dBu, init, ALU.mult, ALU.add,
                          waits=wsc)
                    dA_read[ping] = b.val("vx")
                    if q == 0:
                        cdst = carry[:, k * 16 + n:k * 16 + n + 1]
                        csrc = h[:, 2047:2048]
                        vhv = b.val("vx")
                        b.emit("gpsimd",
                               lambda e, _d=cdst, _s=csrc: e.tensor_copy(_d, _s),
                               waits=[("vx", vhv)], inc=("gp", 1))
                        carry_gp = b.val("gp")
                    tmp = WP[:, 2048:4096]
                    wv3 = [(csem, lc)] if k == 0 else []
                    b.vop("tensor_mul", tmp, h, cslot(n), waits=wv3)
                    ya = big1f[:, k * 2048:(k + 1) * 2048]
                    if n == 0:
                        b.vop("tensor_copy", ya, tmp)
                    else:
                        b.vop("tensor_add", ya, ya, tmp)
            vx_scan_end = b.val("vx")
            for k in range(8):
                slot = XS[:, (k % 2) * 4096:(k % 2) * 4096 + 4096].bitcast(BF16) \
                    if False else XS[:, (k % 2) * 2048 * 2:(k % 2) * 2048 * 2 + 2048 * 2]
                stg = XS[:, (k % 2) * 4096:(k % 2) * 4096 + 2048]
                wv = [("dma", b.val("dma"))] if k >= 2 else ()
                wv = tuple(wv) + (("gp", b.val("gp")),)
                b.vop("tensor_copy", stg, big1f[:, k * 2048:(k + 1) * 2048], waits=wv)
                b.fence_v(stg[:, 2047:2048])
                b.dma(cc_y_in.ap()[q * 1024 + k * 128:q * 1024 + (k + 1) * 128, :],
                      stg, waits=[("vx", b.val("vx"))])
            b.barrier()
            b.emit("gpsimd",
                   lambda e, _q=q: e.collective_compute(
                       "AllGather", ALU.bypass, replica_groups=PAIRS,
                       ins=[cc_y_in.ap()[_q * 1024:(_q + 1) * 1024, :].opt()],
                       outs=[cc_y_out.ap()[_q * 2048:(_q + 1) * 2048, :].opt()]),
                   waits=[("dma", b.val("dma")), ("gdma", b.val("gdma"))],
                   inc=("cc", 1))
        cc_y_done = b.val("cc")
        if DEBUG:
            b.dma(dbg["dbg_y"], cc_y_out.ap())
            b.dma(dbg["dbg_yin"], cc_y_in.ap())
            b.barrier()

    if PHASE_LIMIT >= 3:
        y_flat = cc_y_out.ap().rearrange("d (c w) -> (d c) w", w=512)[0:128, :]
        p3_cc = [cc_y_done]
        for o in range(8):
            b.dma(A1[:, o * 2048:(o + 1) * 2048], W_out_p[:, o * 2048:(o + 1) * 2048],
                  q="gpsimd")

        def p3_yb(c):
            return 0 if c % 2 == 0 else 24576

        def p3_y(c, w):
            base = p3_yb(c)
            for k in range(16):
                b.gather(Aap[:, base + k * 512:base + (k + 1) * 512], y_flat,
                         gy_sb[:, k * 4 + c:k * 4 + c + 1], waits=w if k == 0 else ())
            return b.dma(gy_sb, gy_i, q="gpsimd")

        def p3_usz(c, w):
            for k in range(16):
                b.dma(Aap[:, 8192 + k * 512:8192 + (k + 1) * 512],
                      cc_pre_in.ap()[2048 + k * 128:2048 + (k + 1) * 128,
                                     c * TC:(c + 1) * TC], waits=w if k == 0 else ())
                b.dma(A1[:, 20480 + k * 512:20480 + (k + 1) * 512],
                      z_dram.ap()[k * 128:(k + 1) * 128, c * TC:(c + 1) * TC])
            return b.val("dma")

        def p3_ln(c, w):
            for k in range(8):
                b.dma(A1[:, 16384 + k * 512:16384 + (k + 1) * 512],
                      ln_dram.ap()[k * 128:(k + 1) * 128, c * TC:(c + 1) * TC],
                      waits=w if k == 0 else ())
            return b.val("dma")

        def p3_x0(c, w):
            for k in range(8):
                b.dma(Af[:, 8192 + k * 512:8192 + (k + 1) * 512],
                      x_ext[k * 128:(k + 1) * 128, 3 + c * TC:3 + (c + 1) * TC],
                      waits=w if k == 0 else ())
            return b.val("dma")

        fy = {0: p3_y(0, [("cc", cc_y_done)])}
        du = {0: p3_usz(0, ())}
        dln = {0: p3_ln(0, ())}
        dx0 = {0: p3_x0(0, ())}
        store_mark = None
        for c in range(NCH):
            base = p3_yb(c)
            for k in range(16):
                yk = Aap[:, base + k * 512:base + (k + 1) * 512]
                w1 = [("gdma", fy[c]), ("dma", du[c])] if k == 0 else ()
                b.vop("scalar_tensor_tensor", yk,
                      Aap[:, 8192 + k * 512:8192 + (k + 1) * 512],
                      D_s[:, k:k + 1], yk, ALU.mult, ALU.add, waits=w1)
                b.vop("tensor_mul", yk, yk, A1[:, 20480 + k * 512:20480 + (k + 1) * 512])
            vx_y2 = b.val("vx")
            if c + 1 < NCH:
                fy[c + 1] = p3_y(c + 1, [("mm", b.val("mm"))])
                du[c + 1] = p3_usz(c + 1, [("vx", vx_y2)])
            for o in range(8):
                ps = P0 if o % 2 == 0 else P1
                for k in range(16):
                    wv = ()
                    if k == 0:
                        wv = [("vx", vx_y2 if o < 2 else b.val("vx"))]
                    b.mm(ps[:, 0:TC], A1[:, (o * 16 + k) * 128:(o * 16 + k + 1) * 128],
                         Aap[:, base + k * 512:base + (k + 1) * 512],
                         start=(k == 0), stop=(k == 15), waits=wv)
                wadd = [("mm", b.val("mm")), ("dma", dln[c])]
                if o == 0 and store_mark is not None:
                    wadd.append(("dma", store_mark))
                b.vop("tensor_add", xsf[:, o * 512:(o + 1) * 512], ps[:, 0:TC],
                      A1[:, 16384 + o * 512:16384 + (o + 1) * 512], waits=wadd)
            if c + 1 < NCH:
                dln[c + 1] = p3_ln(c + 1, [("vx", b.val("vx"))])
            tiles = [xsf[:, o * 512:(o + 1) * 512] for o in range(8)]
            m_rep, r_rep = ln_stats(tiles, TC, w0=True)
            vv = b.val("vx")
            for o in range(8):
                wsub = [("vx", vv)] if o == 0 else ()
                if o == 0:
                    wsub = wsub + [("dma", dx0[c])]
                b.vop("tensor_sub", tiles[o], tiles[o], m_rep, waits=wsub)
                b.vch("tensor_mul", tiles[o], tiles[o], r_rep)
                b.vch("tensor_scalar", tiles[o], tiles[o], l0ag[:, o:o + 1],
                      l0ab[:, o:o + 1], ALU.mult, ALU.add)
                b.vch("tensor_add", tiles[o], tiles[o],
                      Af[:, 8192 + o * 512:8192 + (o + 1) * 512])
                b.fence_v(tiles[o][:, 511:512])
                b.dma(xres_A.ap()[o * 128:(o + 1) * 128, c * TC:(c + 1) * TC], tiles[o],
                      waits=[("vx", b.val("vx"))])
            store_mark = b.val("dma")
            if c + 1 < NCH:
                dx0[c + 1] = p3_x0(c + 1, [("vx", b.val("vx"))])
        b.barrier()
        if DEBUG:
            b.dma(dbg["dbg_x1"], xres_A.ap())
            b.barrier()

    if PHASE_LIMIT >= 4:
        for j in range(4):
            b.dma(A1[:, j * 4096:(j + 1) * 4096], enc_p[:, j * 4096:(j + 1) * 4096], q="gpsimd")
            b.dma(A1[:, 16384 + j * 4096:16384 + (j + 1) * 4096],
                  dec_p[:, j * 4096:(j + 1) * 4096], q="gpsimd")
        b.dma(Aap[0:64, 20480:28672], ew1_p, q="gpsimd")
        b.dma(ST[0:64, 3072:5120], c2s_p, q="gpsimd")
        b.dma(Aap[:, 28672:32768], ew2_p, q="gpsimd")
        b.dma(ST[:, 2048:3072], s2c_p, q="gpsimd")
        b.emit("tensor", lambda e: None,
               waits=[("gdma", b.val("gdma")), ("dma", b.val("dma"))])
        W4 = TC4

        def p4_x1buf(c):
            return xsf if c % 2 == 0 else Af[:, 4096:6144]

        def p4_x1(c, w):
            xb = p4_x1buf(c)
            for k in range(8):
                b.dma(xb[:, k * W4:(k + 1) * W4],
                      xres_A.ap()[k * 128:(k + 1) * 128, c * W4:(c + 1) * W4],
                      waits=w if k == 0 else ())
            return b.val("dma")

        dx1 = {0: p4_x1(0, [("vx", b.val("vx")), ("sc", b.val("sc"))])}
        p4_store = None
        for c in range(NCH4):
            xb = p4_x1buf(c)
            tiles = [xb[:, k * W4:(k + 1) * W4] for k in range(8)]
            m_rep, r_rep = ln_stats(tiles, W4, w0=True, extra=[("dma", dx1[c])])
            vv = b.val("vx")
            ln2 = []
            for k in range(8):
                b.vop("tensor_sub", wpf[:, 1024:1024 + W4], tiles[k], m_rep,
                      waits=[("vx", vv), ("dma", dx1[c]), ("mm", b.val("mm"))]
                      if k == 0 else ())
                dst = Aap[:, 16384 + k * W4:16384 + (k + 1) * W4]
                b.vch("tensor_tensor", dst, wpf[:, 1024:1024 + W4], r_rep, ALU.mult)
                ln2.append(dst)
            vx_ln2 = b.val("vx")
            if c + 1 < NCH4:
                dx1[c + 1] = p4_x1(c + 1, [("vx", b.val("vx")),
                                           ("sc", b.val("sc")),
                                           ("mm", b.val("mm"))])
            for o in range(16):
                ps = P0 if o % 2 == 0 else P1
                for k in range(8):
                    wv = ()
                    if k == 0:
                        wv = [("sc", b.val("sc"))]
                        if o < 2:
                            wv.append(("vx", vx_ln2))
                    b.mm(ps[:, 0:W4], A1[:, (o * 8 + k) * 128:(o * 8 + k + 1) * 128],
                         ln2[k], start=(k == 0), stop=(k == 7), waits=wv)
                b.act(Aap[:, o * W4:(o + 1) * W4], ps[:, 0:W4], AF.Sigmoid,
                      bias=encb[:, o:o + 1], waits=[("mm", b.val("mm"))])
            spk = [Aap[:, o * W4:(o + 1) * W4] for o in range(16)]
            for k in range(16):
                wv = [("sc", b.val("sc")), ("vx", b.val("vx"))] if k == 0 else ()
                b.mm(P2[0:64, 0:W4], ST[:, 2048 + k * 64:2048 + (k + 1) * 64], spk[k],
                     start=(k == 0), stop=(k == 15), waits=wv)
            b.vop("tensor_scalar", moe_f, P2[0:64, 0:W4], s2cb[:, 0:1], None, ALU.add,
                  waits=[("mm", b.val("mm"))])
            b.vch("tensor_copy", cont_bf, moe_f)
            vx_cont = b.val("vx")
            b.mm(P2[0:64, 0:W4], rw1, cont_bf, start=True, stop=True,
                 waits=[("vx", vx_cont)])
            b.act(rh_bf, P2[0:64, 0:W4], AF.Relu, bias=rb1[:, 0:1],
                  waits=[("mm", b.val("mm"))])
            if DEBUG and c == 0:
                b.dma(dbg["dbg_rh"], rh_bf, q="gpsimd")
                b.dma(dbg["dbg_cont"], cont_bf, q="gpsimd")
                b.barrier()
            if DEBUG and c == 0 and P4STAGE >= 21:
                b.fence_v(logit_sb[0:8, 0:1])
                b.dma(dbg["dbg_lgr"], logit_sb[0:8, 0:W4], waits=[("vx", b.val("vx"))])
                b.barrier()
            for tb in range(W4 // 128 if P4STAGE >= 3 else 0):
                po = (c % 2) * 32 + tb * 16
                wtb = [("vx", b.val("vx"))]
                if tb == 0:
                    wtb.append(("sc", b.val("sc")))
                b.mm(P3[:, po:po + 8], rh_bf[0:64, tb * 128:(tb + 1) * 128], rw2,
                     start=True, stop=True, waits=wtb)
                wcc = [("mm", b.val("mm"))]
                if tb == 0:
                    wcc.append(("dma", b.val("dma")))
                b.vch("tensor_tensor", lgt[:, 0:8], P3[:, po:po + 8], rb2rep, ALU.add,
                      waits=wcc)
                b.vch("reduce_max", m1t, lgt[:, 0:8], AX.X)
                vv = b.val("vx")
                b.vch("tensor_scalar", msk1[:, 0:8], lgt[:, 0:8], m1t[:, 0:1], None,
                      ALU.is_equal, waits=[("vx", vv)])
                b.vch("tensor_mul", candt[:, 0:8], msk1[:, 0:8], iota8m)
                b.vch("tensor_scalar", candt[:, 0:8], candt[:, 0:8], 8.0, None, ALU.add)
                b.vch("tensor_reduce", i1t, candt[:, 0:8], AX.X, ALU.min)
                vv = b.val("vx")
                b.vch("tensor_scalar", msk1[:, 0:8], iota8, i1t[:, 0:1], None,
                      ALU.is_equal, waits=[("vx", vv)])
                b.vch("tensor_scalar", candt[:, 0:8], msk1[:, 0:8], NEG_BIG, None, ALU.mult)
                b.vch("tensor_add", candt[:, 0:8], candt[:, 0:8], lgt[:, 0:8])
                b.vch("reduce_max", m2t, candt[:, 0:8], AX.X)
                vv = b.val("vx")
                b.vch("tensor_scalar", msk2[:, 0:8], candt[:, 0:8], m2t[:, 0:1], None,
                      ALU.is_equal, waits=[("vx", vv)])
                b.vch("tensor_mul", msk2[:, 0:8], msk2[:, 0:8], iota8m)
                b.vch("tensor_scalar", msk2[:, 0:8], msk2[:, 0:8], 8.0, None, ALU.add)
                b.vch("tensor_reduce", i1t, msk2[:, 0:8], AX.X, ALU.min)
                vv = b.val("vx")
                b.vch("tensor_scalar", msk2[:, 0:8], iota8, i1t[:, 0:1], None,
                      ALU.is_equal, waits=[("vx", vv)])
                b.vch("tensor_sub", w1t, m1t, m2t)
                vv = b.val("vx")
                b.act(w1t, w1t, AF.Sigmoid, waits=[("vx", vv)])
                scv = b.val("sc")
                b.vch("tensor_scalar", w2t, w1t, -1.0, 1.0, ALU.mult, ALU.add,
                      waits=[("sc", scv)])
                vv = b.val("vx")
                b.vch("tensor_scalar", candt[:, 0:8], msk1[:, 0:8], w1t[:, 0:1], None,
                      ALU.mult, waits=[("vx", vv)])
                b.vch("scalar_tensor_tensor", candt[:, 0:8], msk2[:, 0:8], w2t[:, 0:1],
                      candt[:, 0:8], ALU.mult, ALU.add)
                vv = b.val("vx")
                b.mm(P3[0:8, 256:384], candt[:, 0:8], identf, start=True, stop=True,
                     waits=[("vx", vv)])
                b.vch("tensor_copy", comb_sb[0:8, tb * 128:(tb + 1) * 128],
                      P3[0:8, 256:384], waits=[("mm", b.val("mm"))])
            b.fence_v(comb_sb[0:8, 0:1])
            b.fence_v(comb_sb[0:8, W4 - 1:W4])
            b.dma(comb_dram.ap()[c * 8:(c + 1) * 8, :], comb_sb,
                  waits=[("vx", b.val("vx"))])
            for e in range(8 if P4STAGE >= 4 else 0):
                lb = b.dma(comb_rep, comb_dram.ap()[c * 8 + e:c * 8 + e + 1, :]
                           .partition_broadcast(64), waits=[("vx", b.val("vx"))])
                for oc in range(8):
                    wenc = [("sc", b.val("sc"))]
                    if e == 0 and oc == 0:
                        wenc.append(("vx", vx_cont))
                    b.mm(P0[:, 0:W4], Aap[0:64, 20480 + (e * 8 + oc) * 128:
                                          20480 + (e * 8 + oc + 1) * 128],
                         cont_bf, start=True, stop=True, waits=wenc)
                    b.act(shb[:, 0:W4], P0[:, 0:W4], AF.Relu,
                          bias=eb1[:, e * 8 + oc:e * 8 + oc + 1],
                          waits=[("mm", b.val("mm"))])
                    b.mm(P1[0:64, 0:W4], Aap[:, 28672 + (e * 8 + oc) * 64:
                                             28672 + (e * 8 + oc + 1) * 64],
                         shb[:, 0:W4], start=(oc == 0), stop=(oc == 7),
                         waits=[("sc", b.val("sc"))], skip_gc=True)
                mmv = b.val("mm")
                b.vop("tensor_scalar", eo_f, P1[0:64, 0:W4], eb2[:, e:e + 1], None,
                      ALU.add, waits=[("mm", mmv)])
                b.vch("tensor_mul", eo_f, eo_f, comb_rep, waits=[("dma", lb)])
                if e == 0:
                    b.vch("tensor_copy", moe_f, eo_f)
                else:
                    b.vch("tensor_add", moe_f, moe_f, eo_f)
            b.vch("tensor_copy", moe_bf, moe_f)
            vx_moe = b.val("vx")
            for o in range(16):
                ps = P0 if o % 2 == 0 else P1
                wc2 = [("sc", b.val("sc"))]
                if o < 2:
                    wc2.append(("vx", vx_moe))
                b.mm(ps[:, 0:W4], ST[0:64, 3072 + o * 128:3072 + (o + 1) * 128],
                     moe_bf, start=True, stop=True, waits=wc2)
                b.act(Aap[:, o * W4:(o + 1) * W4], ps[:, 0:W4], AF.Sigmoid,
                      bias=c2sb[:, o:o + 1], waits=[("mm", b.val("mm"))])
            rates = [Aap[:, k * W4:(k + 1) * W4] for k in range(16)]
            for o in range(8):
                ps = P0 if o % 2 == 0 else P1
                for k in range(16):
                    wv = [("sc", b.val("sc"))] if k == 0 else ()
                    b.mm(ps[:, 0:W4], A1[:, 16384 + (o * 16 + k) * 128:
                                         16384 + (o * 16 + k + 1) * 128],
                         rates[k], start=(k == 0), stop=(k == 15), waits=wv)
                wact = [("mm", b.val("mm"))]
                if o == 0:
                    wact.append(("vx", b.val("vx")))
                    if p4_store is not None:
                        wact.append(("dma", p4_store))
                b.act(xsf[:, 2048 + o * W4:2048 + (o + 1) * W4], ps[:, 0:W4], AF.Sigmoid,
                      bias=decb[:, o:o + 1], waits=wact)
            tiles = [xsf[:, 2048 + o * W4:2048 + (o + 1) * W4] for o in range(8)]
            m_rep, r_rep = ln_stats(tiles, W4, w0=True)
            vv = b.val("vx")
            scv9 = b.val("sc")
            for o in range(8):
                b.vop("tensor_sub", tiles[o], tiles[o], m_rep,
                      waits=[("vx", vv), ("sc", scv9)] if o == 0 else ())
                b.vch("tensor_mul", tiles[o], tiles[o], r_rep)
                b.vch("tensor_scalar", tiles[o], tiles[o], l0fg[:, o:o + 1],
                      l0fb[:, o:o + 1], ALU.mult, ALU.add)
                b.vch("tensor_add", tiles[o], tiles[o], xb[:, o * W4:(o + 1) * W4])
                b.fence_v(tiles[o][:, W4 - 1:W4])
                b.dma(xres_B.ap()[o * 128:(o + 1) * 128, c * W4:(c + 1) * W4], tiles[o],
                      waits=[("vx", b.val("vx"))])
            p4_store = b.val("dma")
        b.barrier()
        if DEBUG:
            b.dma(dbg["dbg_x2"], xres_B.ap())
            b.dma(dbg["dbg_comb"], comb_dram.ap())
            b.barrier()

    if PHASE_LIMIT >= 5:
        for j in range(2):
            b.dma(A1[:, j * 4096:(j + 1) * 4096], Wq_p[:, j * 4096:(j + 1) * 4096], q="gpsimd")
            b.dma(A1[:, 8192 + j * 4096:8192 + (j + 1) * 4096],
                  Wk_p[:, j * 4096:(j + 1) * 4096], q="gpsimd")
            b.dma(A1[:, 16384 + j * 4096:16384 + (j + 1) * 4096],
                  Wv_p[:, j * 4096:(j + 1) * 4096], q="gpsimd")
            b.dma(A1[:, 24576 + j * 4096:24576 + (j + 1) * 4096],
                  Wo_p[:, j * 4096:(j + 1) * 4096], q="gpsimd")
        def a5_buf(c, k):
            base = 8192 if c % 2 == 0 else 12288
            return Af[:, base + k * 512:base + (k + 1) * 512]

        def a5_ld(c, w):
            for k in range(8):
                b.dma(a5_buf(c, k),
                      xres_B.ap()[k * 128:(k + 1) * 128, c * TC:(c + 1) * TC],
                      waits=w if k == 0 else ())
            return b.val("dma")

        a5_mk = {0: a5_ld(0, [("vx", b.val("vx"))])}
        for c in range(NCH):
            tiles = [a5_buf(c, k) for k in range(8)]
            if c + 1 < NCH:
                a5_mk[c + 1] = a5_ld(c + 1, [("vx", b.val("vx")),
                                             ("sc", b.val("sc")),
                                             ("mm", b.val("mm"))])
            m_rep, r_rep = ln_stats(tiles, TC, w0=True, extra=[("dma", a5_mk[c])])
            vv = b.val("vx")
            for k in range(8):
                wsub = [("vx", vv), ("dma", a5_mk[c])] if k == 0 else ()
                b.vop("tensor_sub", tiles[k], tiles[k], m_rep, waits=wsub)
                b.vch("tensor_tensor", Aap[:, k * 2048 + c * TC:k * 2048 + (c + 1) * TC],
                      tiles[k], r_rep, ALU.mult)
        b.barrier()
        b.emit("tensor", lambda e: None, waits=[("gdma", b.val("gdma"))])
        a5_store = {}
        a5_i = 0
        for (wofs, dst) in [(0, q_dram), (8192, k_dram)]:
            for o in range(8):
                for c in range(NCH):
                    ps = P0 if (o * NCH + c) % 2 == 0 else P1
                    for k in range(8):
                        wv = [("vx", b.val("vx"))] if (o + c > 0 and k == 0) else ()
                        b.mm(ps[:, 0:TC],
                             A1[:, wofs + (o * 8 + k) * 128:wofs + (o * 8 + k + 1) * 128],
                             Aap[:, k * 2048 + c * TC:k * 2048 + (c + 1) * TC],
                             start=(k == 0), stop=(k == 7), waits=wv)
                    sofs = (a5_i % 2) * 512
                    wvp = [("mm", b.val("mm"))]
                    if a5_i % 2 in a5_store:
                        wvp.append(("dma", a5_store[a5_i % 2]))
                    elif a5_i < 2:
                        wvp.append(("dma", b.val("dma")))
                    b.vop("tensor_scalar", ST[:, sofs:sofs + 512], ps[:, 0:TC],
                          0.0, 0.001, ALU.max, ALU.add, waits=wvp)
                    b.fence_v(ST[:, sofs + 511:sofs + 512])
                    a5_store[a5_i % 2] = b.dma(
                        dst.ap()[o * 128:(o + 1) * 128, c * TC:(c + 1) * TC],
                        ST[:, sofs:sofs + 512], waits=[("vx", b.val("vx"))])
                    a5_i += 1
        b.barrier()
        a5_store = {}
        a5_i = 0
        for tb in range(16):
            for vc in range(2):
                ps = P0 if (tb * 2 + vc) % 2 == 0 else P1
                for k in range(8):
                    wv = [("vx", b.val("vx"))] if (tb + vc > 0 and k == 0) else ()
                    b.mm(ps[:, 0:512],
                         Aap[:, k * 2048 + tb * 128:k * 2048 + (tb + 1) * 128],
                         A1[:, 16384 + k * 1024 + vc * 512:16384 + k * 1024 + (vc + 1) * 512],
                         start=(k == 0), stop=(k == 7), waits=wv)
                sofs = (a5_i % 2) * 512
                wvp = [("mm", b.val("mm"))]
                if a5_i % 2 in a5_store:
                    wvp.append(("dma", a5_store[a5_i % 2]))
                elif a5_i < 2:
                    wvp.append(("dma", b.val("dma")))
                b.vop("tensor_copy", ST[:, sofs:sofs + 512], ps[:, 0:512], waits=wvp)
                b.fence_v(ST[:, sofs + 511:sofs + 512])
                a5_store[a5_i % 2] = b.dma(
                    v_dram.ap()[tb * 128:(tb + 1) * 128, vc * 512:(vc + 1) * 512],
                    ST[:, sofs:sofs + 512], waits=[("vx", b.val("vx"))])
                a5_i += 1
        b.barrier()
        b.vop("memset", vaug[:, 64:65], 1.0)
        b.vop("memset", vaug[:, 129:130], 1.0)
        b.vop("memset", vaug2[:, 64:65], 1.0)
        b.vop("memset", vaug2[:, 129:130], 1.0)
        SFs = (S_f, S_f2)
        SAs = (saug, saug2)
        VAs = (vaug, vaug2)
        KTs = (ktok, ktok2)
        AMs = (amb, amb2)
        DENs = (den_s, den_s2)
        SUPD = (P4[0:64, 0:65], P1[0:64, 0:65])
        for j in range(8 if P5STAGE >= 2 else 0):
            lq = {}; lk = {}
            for s in (0, 1):
                hh = 2 * j + s
                QH = ST[0:64, s * 4096:s * 4096 + 2048]
                KH = ST[0:64, s * 4096 + 2048:s * 4096 + 4096]
                lq[s] = b.dma(QH, q_dram.ap()[hh * 64:(hh + 1) * 64, :],
                              waits=[("mm", b.val("mm")), ("vx", b.val("vx"))] if s == 0 else ())
                lk[s] = b.dma(KH, k_dram.ap()[hh * 64:(hh + 1) * 64, :])
                b.vop("memset", SFs[s], 0.0, waits=[("dma", b.val("dma"))] if (j > 0 and s == 0) else ())
            for cch in range(16):
                for s in (0, 1):
                    hh = 2 * j + s
                    QH = ST[0:64, s * 4096:s * 4096 + 2048]
                    KH = ST[0:64, s * 4096 + 2048:s * 4096 + 4096]
                    numv = xsf[0:64, s * 2048:(s + 1) * 2048]
                    vofs = (cch % 2) * 65
                    lv = b.dma(VAs[s][:, vofs:vofs + 64],
                               v_dram.ap()[cch * 128:(cch + 1) * 128, hh * 64:(hh + 1) * 64],
                               waits=[("mm", b.val("mm"))])
                    wk0 = [("vx", b.val("vx"))]
                    if cch == 0:
                        wk0.append(("dma", lk[s]))
                    b.mm(P4[:, 128 + 64 * s:192 + 64 * s], KH[:, cch * 128:(cch + 1) * 128],
                         identb[0:64, 0:64], start=True, stop=True, skip_gc=True, waits=wk0)
                    b.vop("tensor_copy", KTs[s], P4[:, 128 + 64 * s:192 + 64 * s],
                          waits=[("mm", b.val("mm"))])
                    wq0 = [("vx", b.val("vx"))]
                    if cch == 0:
                        wq0.append(("dma", lq[s]))
                    b.mm(P2[:, s * 128:(s + 1) * 128], KH[:, cch * 128:(cch + 1) * 128],
                         QH[:, cch * 128:(cch + 1) * 128], start=True, stop=True,
                         skip_gc=True, waits=wq0)
                    b.vop("tensor_mul", AMs[s], P2[:, s * 128:(s + 1) * 128], cmask,
                          waits=[("mm", b.val("mm"))])
                    b.vch("tensor_copy", SAs[s], SFs[s])
                    vv = b.val("vx")
                    b.mm(P3[0:64, s * 128:(s + 1) * 128], VAs[s][:, vofs:vofs + 64], AMs[s],
                         start=True, stop=True, skip_gc=True,
                         waits=[("vx", vv), ("dma", lv)], inc=False)
                    b.mm(P3[0:64, s * 128:(s + 1) * 128], SAs[s][:, 0:64],
                         QH[:, cch * 128:(cch + 1) * 128], start=False, stop=True, skip_gc=True)
                    b.mm(P4[0:1, 256 + 128 * s:384 + 128 * s], VAs[s][:, vofs + 64:vofs + 65],
                         AMs[s], start=True, stop=True, skip_gc=True, inc=False)
                    b.mm(P4[0:1, 256 + 128 * s:384 + 128 * s], SAs[s][:, 64:65],
                         QH[:, cch * 128:(cch + 1) * 128], start=False, stop=True, skip_gc=True)
                    mmv = b.val("mm")
                    wnum = [("mm", mmv)]
                    if cch == 0:
                        wnum.append(("dma", b.val("dma")))
                    b.vop("tensor_copy", numv[:, cch * 128:(cch + 1) * 128],
                          P3[0:64, s * 128:(s + 1) * 128], waits=wnum)
                    b.vop("tensor_copy", DENs[s][:, cch * 128:(cch + 1) * 128],
                          P4[0:1, 256 + 128 * s:384 + 128 * s])
                    b.mm(SUPD[s], KTs[s], VAs[s][:, vofs:vofs + 65], start=True, stop=True,
                         skip_gc=True, waits=[("vx", b.val("vx"))])
                    b.vch("tensor_add", SFs[s], SFs[s], SUPD[s], waits=[("mm", b.val("mm"))])
            for s in (0, 1):
                hh = 2 * j + s
                numv = xsf[0:64, s * 2048:(s + 1) * 2048]
                b.fence_v(SFs[s][:, 64:65])
                b.fence_v(numv[:, 2047:2048])
                b.fence_v(DENs[s][:, 2047:2048])
                vv = b.val("vx")
                b.dma(cc_att_in.ap()[:, hh * 65:(hh + 1) * 65], SFs[s], waits=[("vx", vv)])
                b.dma(num_dram.ap()[hh * 64:(hh + 1) * 64, :], numv, waits=[("vx", vv)])
                b.dma(den_dram.ap()[hh:hh + 1, :], DENs[s], waits=[("vx", vv)])
        b.barrier()
        b.emit("gpsimd",
               lambda e: e.collective_compute(
                   "AllGather", ALU.bypass, replica_groups=PAIRS,
                   ins=[cc_att_in.ap().opt()], outs=[cc_att_out.ap().opt()]),
               waits=[("dma", b.val("dma")), ("gdma", b.val("gdma"))], inc=("cc", 1))
        b.barrier()
        def c5_QH(p):
            return ST[0:64, p * 2048:(p + 1) * 2048]

        def c5_NUM(p):
            return xsf[0:64, p * 2048:(p + 1) * 2048]

        def c5_DEN(p):
            return big1f[0:1, p * 2048:(p + 1) * 2048]

        c5_SPV = (sprev_f, stf[0:64, 2048:2113])
        c5_SP64 = (sprev64, ST[0:64, 4240:4304])
        c5_SPK = (spk1, ST[0:64, 4380:4381])
        c5_DBF = (A1[0:1, 8192:10240], A1[0:1, 10240:12288])
        c5_CORR = (P3, P2)
        c5_BC = (P0, P1)
        c5_L = {}
        c5_vxend = {}
        c5_mmq = {}
        c5_mmend = {}

        def c5_loads(hh):
            p = hh % 2
            if hh >= 2:
                w = [("vx", c5_vxend[hh - 2]), ("mm", c5_mmq[hh - 2])]
            else:
                w = [("vx", b.val("vx")), ("mm", b.val("mm"))]
            lq = b.dma(c5_QH(p), q_dram.ap()[hh * 64:(hh + 1) * 64, :], waits=w)
            b.dma(c5_SPV[p], cc_att_out.ap()[0:64, hh * 65:(hh + 1) * 65])
            b.dma(c5_NUM(p), num_dram.ap()[hh * 64:(hh + 1) * 64, :])
            ld = b.dma(c5_DEN(p), den_dram.ap()[hh:hh + 1, :])
            c5_L[hh] = (lq, ld)

        NH5C = 16 if P5STAGE >= 3 else 0
        for hh in range(min(2, NH5C)):
            c5_loads(hh)
        for hh in range(NH5C):
            p = hh % 2
            lq, ld = c5_L[hh]
            Qh = c5_QH(p)
            numv = c5_NUM(p)
            denv = c5_DEN(p)
            CR = c5_CORR[p]
            wsp = [("dma", ld)]
            if hh >= 2:
                wsp.append(("mm", c5_mmq[hh - 2]))
            b.vop("tensor_scalar", c5_SP64[p], c5_SPV[p][:, 0:64], odd64[:, 0:1], None,
                  ALU.mult, waits=wsp)
            b.vop("tensor_scalar", c5_SPK[p], c5_SPV[p][:, 64:65], odd64[:, 0:1], None,
                  ALU.mult)
            for qc in range(4):
                for j4 in range(4):
                    b.mm(CR[0:64, j4 * 128:(j4 + 1) * 128], c5_SP64[p],
                         Qh[:, qc * 512 + j4 * 128:qc * 512 + (j4 + 1) * 128],
                         start=True, stop=True, skip_gc=True,
                         waits=[("vx", b.val("vx")), ("dma", lq)] if j4 == 0 else ())
                b.mm(P4[0:1, 0:512], c5_SPK[p], Qh[:, qc * 512:(qc + 1) * 512],
                     start=True, stop=True, skip_gc=True)
                mmv = b.val("mm")
                for h2 in range(2):
                    b.vop("tensor_add",
                          numv[:, qc * 512 + h2 * 256:qc * 512 + (h2 + 1) * 256],
                          numv[:, qc * 512 + h2 * 256:qc * 512 + (h2 + 1) * 256],
                          CR[0:64, h2 * 256:(h2 + 1) * 256],
                          waits=[("mm", mmv)] if h2 == 0 else ())
                b.vch("tensor_add", denv[:, qc * 512:(qc + 1) * 512],
                      denv[:, qc * 512:(qc + 1) * 512], P4[0:1, 0:512])
            c5_mmq[hh] = b.val("mm")
            b.vch("reciprocal", denv, denv)
            wdb = [("mm", c5_mmend[hh - 2])] if hh >= 2 else ()
            b.vch("tensor_copy", c5_DBF[p], denv, waits=wdb)
            vv = b.val("vx")
            for qc in range(4):
                b.mm(c5_BC[p][0:64, 0:512], onesb[0:1, 0:64],
                     c5_DBF[p][:, qc * 512:(qc + 1) * 512],
                     start=True, stop=True, skip_gc=True,
                     waits=[("vx", vv)] if qc == 0 else [("vx", b.val("vx"))])
                wmul = [("mm", b.val("mm"))]
                if qc == 0:
                    wmul.append(("gdma", b.val("gdma")))
                b.vop("tensor_mul", wpf[0:64, qc * 512:(qc + 1) * 512],
                      numv[:, qc * 512:(qc + 1) * 512], c5_BC[p][0:64, 0:512],
                      waits=wmul)
            c5_mmend[hh] = b.val("mm")
            c5_vxend[hh] = b.val("vx")
            if hh + 2 < NH5C:
                c5_loads(hh + 2)
            b.fence_v(wpf[0:64, 2047:2048])
            b.dma(attn_dram.ap()[hh * 64:(hh + 1) * 64, :],
                  wpf[0:64, 0:2048], waits=[("vx", b.val("vx"))], q="gpsimd")
        b.barrier()
        if DEBUG:
            b.dma(dbg["dbg_att"], attn_dram.ap())
        for c in range(NCH if P5STAGE >= 4 else 0):
            for k in range(8):
                b.dma(Aap[:, k * 512:(k + 1) * 512],
                      attn_dram.ap()[k * 128:(k + 1) * 128, c * TC:(c + 1) * TC],
                      waits=[("vx", b.val("vx"))] if k == 0 else ())
                b.dma(Af[:, 8192 + k * 512:8192 + (k + 1) * 512],
                      xres_B.ap()[k * 128:(k + 1) * 128, c * TC:(c + 1) * TC])
            b.barrier()
            for o in range(8):
                ps = P0 if o % 2 == 0 else P1
                for k in range(8):
                    wv = [("vx", b.val("vx"))] if o >= 2 and k == 0 else ()
                    b.mm(ps[:, 0:TC],
                         A1[:, 24576 + (o * 8 + k) * 128:24576 + (o * 8 + k + 1) * 128],
                         Aap[:, k * 512:(k + 1) * 512], start=(k == 0), stop=(k == 7),
                         waits=wv)
                b.vop("tensor_scalar", wpf[:, 0:512], ps[:, 0:TC], bo_s[:, o:o + 1],
                      None, ALU.add, waits=[("mm", b.val("mm")), ("dma", b.val("dma"))])
                b.vch("tensor_add", wpf[:, 0:512], wpf[:, 0:512],
                      Af[:, 8192 + o * 512:8192 + (o + 1) * 512])
                b.fence_v(wpf[:, 511:512])
                b.dma(xres_A.ap()[o * 128:(o + 1) * 128, c * TC:(c + 1) * TC],
                      wpf[:, 0:512], waits=[("vx", b.val("vx"))])
            b.barrier()
        if DEBUG:
            b.dma(dbg["dbg_x3"], xres_A.ap())
            b.barrier()

    if PHASE_LIMIT >= 6:
        for j in range(3):
            b.dma(A1[:, j * 6144:(j + 1) * 6144], W2_p[:, j * 6144:(j + 1) * 6144], q="gpsimd")
        b.dma(A1[:, 18432:20480], Wo1_p, q="gpsimd")
        b.dma(wo2b, Wo2_p, q="gpsimd")
        b.emit("tensor", lambda e: None, waits=[("gdma", b.val("gdma"))])
        def l5_buf(c, k):
            if c % 2 == 0:
                return xsf[:, k * 512:(k + 1) * 512]
            return Af[:, 8704 + k * 512:8704 + (k + 1) * 512]

        def l5_ld(c, w):
            for k in range(8):
                b.dma(l5_buf(c, k),
                      xres_A.ap()[k * 128:(k + 1) * 128, c * TC:(c + 1) * TC],
                      waits=w if k == 0 else ())
            return b.val("dma")

        l5_mk = {0: l5_ld(0, [("vx", b.val("vx"))])}
        for c in range(NCH):
            tiles = [l5_buf(c, k) for k in range(8)]
            if c + 1 < NCH:
                l5_mk[c + 1] = l5_ld(c + 1, [("vx", b.val("vx")),
                                             ("sc", b.val("sc")),
                                             ("mm", b.val("mm"))])
            m_rep, r_rep = ln_stats(tiles, TC, w0=True, extra=[("dma", l5_mk[c])])
            vv = b.val("vx")
            for k in range(8):
                wsub = [("vx", vv), ("dma", l5_mk[c])] if k == 0 else ()
                b.vop("tensor_sub", tiles[k], tiles[k], m_rep, waits=wsub)
                b.vch("tensor_tensor",
                      Aap[:, k * 2050 + 1 + c * TC:k * 2050 + 1 + (c + 1) * TC],
                      tiles[k], r_rep, ALU.mult)
        b.barrier()
        for k in range(8):
            b.vch("tensor_copy", mix_sb[:, k:k + 1], Aap[:, k * 2050 + 1:k * 2050 + 2])
            b.vch("tensor_copy", mix_sb[:, 8 + k:9 + k],
                  Aap[:, k * 2050 + 2048:k * 2050 + 2049])
        b.fence_v(mix_sb[:, 15:16])
        b.dma(cc_mix_in.ap(), mix_sb, waits=[("vx", b.val("vx"))])
        b.emit("gpsimd",
               lambda e: e.collective_compute(
                   "AllGather", ALU.bypass, replica_groups=PAIRS,
                   ins=[cc_mix_in.ap().opt()], outs=[cc_mix_out.ap().opt()]),
               waits=[("dma", b.val("dma")), ("gdma", b.val("gdma"))], inc=("cc", 1))
        b.barrier()
        b.dma(mixh[:, 0:16], cc_mix_out.ap()[0:128, :])
        b.dma(mixh[:, 16:32], cc_mix_out.ap()[128:256, :])
        b.vop("tensor_scalar", mixh[:, 8:16], mixh[:, 8:16], isodd[:, 0:1], None,
              ALU.mult, waits=[("dma", b.val("dma"))])
        b.vch("tensor_scalar", mixh[:, 16:24], mixh[:, 16:24], iseven[:, 0:1], None,
              ALU.mult)
        for k in range(8):
            b.vch("tensor_copy", Aap[:, k * 2050:k * 2050 + 1], mixh[:, 8 + k:9 + k])
            b.vch("tensor_copy", Aap[:, k * 2050 + 2049:k * 2050 + 2050],
                  mixh[:, 16 + k:17 + k])
        b.barrier()
        for c in range(NCH):
            x3c = A1f[:, 10240 + 0:10240 + 4096]
            for k in range(8):
                b.dma(A1f[:, 10240 + k * 512:10240 + (k + 1) * 512],
                      xres_A.ap()[k * 128:(k + 1) * 128, c * TC:(c + 1) * TC],
                      waits=[("vx", b.val("vx"))] if k == 0 else ())
            x3mk = b.val("dma")
            for k in range(8):
                base = k * 2050 + c * TC
                acc = stf[:, 2048:2560]
                wv = [("vx", b.val("vx")), ("mm", b.val("mm"))] if k == 0 else ()
                b.vop("tensor_scalar", acc, Aap[:, base:base + 512],
                      mixk[:, k * 3:k * 3 + 1], None, ALU.mult, waits=wv)
                b.vop("scalar_tensor_tensor", acc, Aap[:, base + 1:base + 513],
                      mixk[:, k * 3 + 1:k * 3 + 2], acc, ALU.mult, ALU.add)
                b.vop("scalar_tensor_tensor", acc, Aap[:, base + 2:base + 514],
                      mixk[:, k * 3 + 2:k * 3 + 3], acc, ALU.mult, ALU.add)
                b.vop("tensor_copy", Aap[:, 16400 + k * 512:16400 + (k + 1) * 512], acc)
            vx_xm = b.val("vx")
            xm = [Aap[:, 16400 + k * 512:16400 + (k + 1) * 512] for k in range(8)]
            for o in range(2):
                ps = P0 if o == 0 else P1
                for k in range(8):
                    wv = [("vx", vx_xm), ("sc", b.val("sc"))] if k == 0 else ()
                    b.mm(ps[:, 0:TC], A1[:, 18432 + (o * 8 + k) * 128:
                                         18432 + (o * 8 + k + 1) * 128],
                         xm[k], start=(k == 0), stop=(k == 7), waits=wv)
                b.act(shb[:, o * 512:(o + 1) * 512], ps[:, 0:TC], AF.Tanh,
                      bias=bo1[:, o:o + 1], waits=[("mm", b.val("mm"))])
            scv = b.val("sc")
            for tb in range(4):
                wtb = [("vx", b.val("vx"))]
                if tb == 0:
                    wtb.append(("sc", scv))
                for mc in range(2):
                    b.mm(P3[:, 0:9], shb[:, mc * 512 + tb * 128:mc * 512 + (tb + 1) * 128],
                         wo2b[:, mc * 9:(mc + 1) * 9], start=(mc == 0), stop=(mc == 1),
                         waits=wtb if mc == 0 else ())
                wlg = [("mm", b.val("mm"))]
                if tb == 0:
                    wlg.append(("dma", b.val("dma")))
                b.vch("tensor_tensor", lgt[:, 0:9], P3[:, 0:9], bo2rep, ALU.add,
                      waits=wlg)
                b.vch("reduce_max", m1t, lgt[:, 0:9], AX.X)
                vv = b.val("vx")
                b.vch("tensor_scalar", msk1[:, 0:9], lgt[:, 0:9], m1t[:, 0:1], None,
                      ALU.is_equal, waits=[("vx", vv)])
                b.vch("tensor_mul", candt[:, 0:9], msk1[:, 0:9], iota9m)
                b.vch("tensor_scalar", candt[:, 0:9], candt[:, 0:9], 9.0, None, ALU.add)
                b.vch("tensor_reduce", i1t, candt[:, 0:9], AX.X, ALU.min)
                vv = b.val("vx")
                b.vch("tensor_scalar", msk1[:, 0:9], iota9, i1t[:, 0:1], None,
                      ALU.is_equal, waits=[("vx", vv)])
                vv = b.val("vx")
                b.mm(P3[0:9, 128:256], msk1[:, 0:9], identf, start=True, stop=True,
                     waits=[("vx", vv)])
                b.vch("tensor_copy", mask9_sb[0:9, tb * 128:(tb + 1) * 128],
                      P3[0:9, 128:256], waits=[("mm", b.val("mm"))])
            b.vch("tensor_copy", mask9b[0:9, 0:TC], mask9_sb[0:9, 0:TC],
                  waits=[("mm", b.val("mm"))])
            b.fence_v(mask9_sb[0:9, 0:1])
            b.fence_v(mask9_sb[0:9, TC - 1:TC])
            b.dma(mask9_dram.ap()[c * 9:(c + 1) * 9, :], mask9_sb[0:9, 0:TC],
                  waits=[("vx", b.val("vx"))])
            b.barrier()
            lw_tab = {0: b.dma(WP[:, 0:2048], W1_p[:, 0:2048],
                               waits=[("mm", b.val("mm"))], q="gpsimd")}
            for p in range(9):
                if p + 1 < 9:
                    lw_tab[p + 1] = b.dma(
                        WP[:, ((p + 1) % 2) * 2048:((p + 1) % 2) * 2048 + 2048],
                        W1_p[:, (p + 1) * 2048:(p + 2) * 2048],
                        waits=[("mm", b.val("mm"))], q="gpsimd")
                lw = lw_tab[p]
                lmk = b.dma(stf[:, 1536:2048],
                            mask9_dram.ap()[c * 9 + p:c * 9 + p + 1, :]
                            .partition_broadcast(128), waits=[("vx", b.val("vx"))])
                halfp = (p % 2) * 2048
                for oc in range(2):
                    ps = P0 if oc == 0 else P1
                    for k in range(8):
                        wv = ()
                        if k == 0:
                            if p == 0:
                                wv = [("gdma", lw), ("vx", vx_xm),
                                      ("sc", b.val("sc"))]
                            else:
                                wv = [("gdma", lw), ("vx", b.val("vx"))]
                        b.mm(ps[:, 0:TC],
                             WP[:, halfp + (oc * 8 + k) * 128:halfp + (oc * 8 + k + 1) * 128],
                             xm[k], start=(k == 0), stop=(k == 7), waits=wv)
                    b.vop("tensor_scalar", stf[:, 1024:1536],
                          ps[:, 0:TC], b1p[:, p * 2 + oc:p * 2 + oc + 1], None, ALU.add,
                          waits=[("mm", b.val("mm")), ("dma", lmk)])
                    b.vop("tensor_tensor",
                          Aap[:, 20992 + p * 1024 + oc * 512:20992 + p * 1024 + (oc + 1) * 512],
                          stf[:, 1024:1536],
                          stf[:, 1536:2048], ALU.mult)
            vx_paths = b.val("vx")
            for dc in range(8):
                ps = P0 if dc % 2 == 0 else P1
                first = True
                for p in range(9):
                    for mc in range(2):
                        wv = [("vx", b.val("vx") if dc >= 2 else vx_paths)] \
                            if first else ()
                        b.mm(ps[:, 0:TC],
                             A1[:, ((p * 2 + mc) * 8 + dc) * 128:
                                ((p * 2 + mc) * 8 + dc + 1) * 128],
                             Aap[:, 20992 + p * 1024 + mc * 512:
                                 20992 + p * 1024 + (mc + 1) * 512],
                             start=first, stop=False, waits=wv)
                        first = False
                b.mm(ps[:, 0:TC], b2bf[:, dc * 128:(dc + 1) * 128], mask9b[0:9, 0:TC],
                     start=False, stop=True)
                wcol = [("mm", b.val("mm"))]
                if dc == 0:
                    wcol.append(("gdma", b.val("gdma")))
                b.vop("tensor_add", xsf[:, dc * 512:(dc + 1) * 512], ps[:, 0:TC],
                      Aap[:, dc * 2050 + 1 + c * TC:dc * 2050 + 1 + (c + 1) * TC],
                      waits=wcol)
            if DEBUG and c == 0:
                for k in range(8):
                    b.dma(dbg["dbg_xm"][k * 128:(k + 1) * 128, 0:512],
                          Aap[:, 16400 + k * 512:16400 + (k + 1) * 512])
                b.barrier()
            tiles = [xsf[:, o * 512:(o + 1) * 512] for o in range(8)]
            m_rep, r_rep = ln_stats(tiles, TC, w0=True)
            vv = b.val("vx")
            for o in range(8):
                wsub = [("vx", vv), ("dma", x3mk)] if o == 0 else ()
                b.vop("tensor_sub", tiles[o], tiles[o], m_rep, waits=wsub)
                b.vch("tensor_mul", tiles[o], tiles[o], r_rep)
                b.vch("tensor_scalar", tiles[o], tiles[o], l1fg[:, o:o + 1],
                      l1fb[:, o:o + 1], ALU.mult, ALU.add)
                b.vch("tensor_add", tiles[o], tiles[o],
                      A1f[:, 10240 + o * 512:10240 + (o + 1) * 512])
                b.fence_v(tiles[o][:, 511:512])
                b.dma(out_fm[o * 128:(o + 1) * 128, c * TC:(c + 1) * TC], tiles[o],
                      waits=[("vx", b.val("vx"))], q="gpsimd")
        b.barrier()
        if DEBUG:
            b.dma(dbg["dbg_m9"], mask9_dram.ap())
            b.barrier()
    else:
        b.dma(out_fm, xres_A.ap() if PHASE_LIMIT >= 3 else x_ext[:, 3:3 + T], q="gpsimd")

    b.barrier()

    with nc.Block() as block:
        @block.sync
        def _(e):
            for f in b.q["sync"]:
                f(e)

        @block.gpsimd
        def _(e):
            for f in b.q["gpsimd"]:
                f(e)

        @block.tensor
        def _(e):
            for f in b.q["tensor"]:
                f(e)

        @block.scalar
        def _(e):
            for f in b.q["scalar"]:
                f(e)

        @block.vector
        def _(e):
            for f in b.q["vector"]:
                f(e)

    stack.close()
    return nc



def kernel(**inputs):
    maps = prep_inputs(inputs)
    nc = bass.Bass("TRN2", target_bir_lowering=False, debug=False)
    build(nc)
    res = run_bass_kernel_spmd(nc, maps, core_ids=list(range(8)))
    out = np.empty((4, 4096, 1024), np.float32)
    for c in range(8):
        b_, h = c // 2, c % 2
        out[b_, h * T:(h + 1) * T, :] = res.results[c]["out_fm"].T
    return out
